# version 03
# baseline (speedup 1.0000x reference)
import numpy as np
import ml_dtypes
from contextlib import ExitStack

import concourse.bass as bass
import concourse.mybir as mybir
from concourse.bass_utils import run_bass_kernel_spmd

F32 = mybir.dt.float32
BF16 = mybir.dt.bfloat16
AF = mybir.ActivationFunctionType
ALU = mybir.AluOpType
AX = mybir.AxisListType.X

NCORES = 8
D = 1024
SEQ = 2048
BPC = 2
T = BPC * SEQ
DI = 2048
DS = 16
NH = 16
DH = 64
NL = 64
LW = 32
NE = 8
FF = 4096


class Sem:
    def __init__(self, ctx, nc, name):
        self.h = ctx.enter_context(nc.semaphore(name))
        self.n = 0


class Prog:
    ENGS = ("sync", "tensor", "vector", "scalar", "gpsimd")

    def __init__(self, nc, ctx):
        self.nc = nc
        self.ctx = ctx
        self.ops = {e: [] for e in self.ENGS}
        self.init_ops = {e: [] for e in self.ENGS}

    def ini(self, eng, fn):
        self.init_ops[eng].append(lambda e, fn=fn: fn(e))

    def doi(self, eng, fn, inc=None, wait=()):
        waits = [(s.h, v if v is not None else s.n) for (s, v) in wait]
        if inc is not None:
            sem, k = inc
            sem.n += k
            tgt = (sem.h, k)
        else:
            tgt = None

        def op(e, fn=fn, waits=waits, tgt=tgt):
            for h, v in waits:
                if v > 0:
                    e.wait_ge(h, v)
            instr = fn(e)
            if tgt is not None:
                instr.then_inc(tgt[0], tgt[1])

        self.init_ops[eng].append(op)

    def play_init(self, block):
        for eng in self.ENGS:
            ops = self.init_ops[eng]
            if not ops:
                continue

            def body(e, ops=ops):
                for fn in ops:
                    fn(e)

            getattr(block, eng)(body)

    def do(self, eng, fn, inc=None, wait=()):
        waits = [(s.h, v if v is not None else s.n) for (s, v) in wait]
        if inc is not None:
            sem, k = inc
            sem.n += k
            tgt = (sem.h, k)
        else:
            tgt = None

        def op(e, fn=fn, waits=waits, tgt=tgt):
            for h, v in waits:
                if v > 0:
                    e.wait_ge(h, v)
            instr = fn(e)
            if tgt is not None:
                assert instr is not None, "fn must return instruction for inc"
                instr.then_inc(tgt[0], tgt[1])

        self.ops[eng].append(op)
        return None

    def play(self, block):
        for eng in self.ENGS:
            ops = self.ops[eng]
            if not ops:
                continue

            def body(e, ops=ops):
                for fn in ops:
                    fn(e)

            getattr(block, eng)(body)


def W_(s, v=None):
    return (s, v)


class Chain:

    SEM_LIMIT = 24000

    def __init__(self, p, ctx, name, init=False):
        self.p = p
        self.ctx = ctx
        self.name = name
        self.init = init
        self.n_sems = 0
        self.sem = self._new_sem()
        self.carry = None

    def _new_sem(self):
        self.n_sems += 1
        return Sem(self.ctx, self.p.nc, f"{self.name}_{self.n_sems}")

    def step(self, eng, fn, dma=False, wait=()):
        k = 16 if dma else 1
        if self.sem.n + k > self.SEM_LIMIT:
            self.carry = (self.sem, self.sem.n)
            self.sem = self._new_sem()
        w = list(wait) + [W_(self.sem, self.sem.n)]
        if self.carry is not None:
            w.append(W_(self.carry[0], self.carry[1]))
            self.carry = None
        if self.init:
            self.p.doi(eng, fn, inc=(self.sem, k), wait=w)
        else:
            self.p.do(eng, fn, inc=(self.sem, k), wait=w)


class ConstPool:
    def __init__(self, p, ctx, values, ic):
        self.nc = p.nc
        vals = sorted(set(float(v) for v in values))
        self.tile = ctx.enter_context(
            p.nc.sbuf_tensor("constpool", [128, max(1, len(vals))], F32,
                             side="right"))
        self.idx = {}
        for i, v in enumerate(vals):
            self.idx[v] = i
            ic.step("gpsimd", lambda e, i=i, v=v: e.memset(
                self.tile[:, i:i + 1], v))

    def col(self, v, parts=128):
        i = self.idx[float(v)]
        return self.tile[0:parts, i:i + 1]


def fm(dram_ap, chunk, col0, ncols, nchunks=1):
    r = dram_ap.rearrange("(c p) t -> p c t", p=128)
    return r[:, chunk : chunk + nchunks, col0 : col0 + ncols]


def bd(sb, width, chunk, col0, ncols, nchunks=1):
    r = sb[:].rearrange("p (c t) -> p c t", t=width)
    return r[:, chunk : chunk + nchunks, col0 : col0 + ncols]


def ch(sb, width, chunk, col0=0, ncols=None):
    if ncols is None:
        ncols = width
    base = chunk * width + col0
    return sb[:, base : base + ncols]



def emit_ln(p, ctx, tctx, ic, x_d, xn_d, rows, tag, cp, xnf_d=None):
    nc = p.nc
    NC = rows // 128
    W = 512
    NT = T // W

    xt = [tctx.enter_context(nc.sbuf_tensor(f"{tag}x{i}", [128, NC * W], F32)) for i in range(2)]
    xb = [tctx.enter_context(nc.sbuf_tensor(f"{tag}xb{i}", [128, NC * W], BF16)) for i in range(2)]
    sq = [tctx.enter_context(nc.sbuf_tensor(f"{tag}sq{i}", [128, W], BF16)) for i in range(2)]
    xo = [tctx.enter_context(nc.sbuf_tensor(f"{tag}xo{i}", [128, NC * W], BF16)) for i in range(2)]
    onesb = ctx.enter_context(nc.sbuf_tensor(f"{tag}ones", [128, 1], BF16,
                                             side="right"))
    stat = tctx.enter_context(nc.sbuf_tensor(f"{tag}st", [1, 7 * W], F32))
    rsb = [tctx.enter_context(nc.sbuf_tensor(f"{tag}rsb{i}", [128, W], F32)) for i in range(2)]
    mbb = [tctx.enter_context(nc.sbuf_tensor(f"{tag}mbb{i}", [128, W], F32)) for i in range(2)]
    ps = tctx.enter_context(nc.psum_tensor(f"{tag}ps", [1, W], F32))
    pq = tctx.enter_context(nc.psum_tensor(f"{tag}pq", [1, W], F32))
    rows_d = nc.dram_tensor(f"{tag}rows", [2, T], F32, kind="Internal").ap()

    s_ld = Sem(ctx, nc, f"{tag}ld")
    s_cast = Sem(ctx, nc, f"{tag}cast")
    s_mm = Sem(ctx, nc, f"{tag}mm")
    s_st = Sem(ctx, nc, f"{tag}stt")
    s_rw = Sem(ctx, nc, f"{tag}rw")
    s_bc = Sem(ctx, nc, f"{tag}bc")
    s_out = Sem(ctx, nc, f"{tag}out")
    s_fin = Sem(ctx, nc, f"{tag}fin")

    ic.step("gpsimd", lambda e: e.memset(onesb[:], 1.0))

    out_hist = []
    for it in range(NT):
        b = it % 2
        c0 = it * W
        tile_cast0 = s_cast.n

        w = []
        if it >= 2:
            w = [W_(s_fin, it - 1), W_(s_out, out_hist[it - 1])]
        p.do("sync",
             lambda e, b=b, c0=c0: e.dma_start(
                 xt[b][:].rearrange("p (c t) -> p c t", t=W),
                 fm(x_d, 0, c0, W, NC)),
             inc=(s_ld, 16), wait=w)

        p.do("vector",
             lambda e, b=b: e.tensor_copy(xb[b][:], xt[b][:]),
             inc=(s_cast, 1), wait=[W_(s_ld, None)])

        tw = [W_(s_cast, None)]
        if it >= 1:
            tw.append(W_(s_st, (it - 1) * 3 + 1))
        for c in range(NC):
            sb2 = c % 2
            sw = [W_(s_ld, None)]
            if c >= 2:
                sw.append(W_(s_mm, s_mm.n - 1))
            p.do("scalar",
                 lambda e, b=b, c=c, sb2=sb2: e.activation(
                     sq[sb2][:], ch(xt[b], W, c), AF.Square),
                 inc=(s_cast, 1), wait=sw)

            def mk_mm(e, b=b, c=c, sb2=sb2):
                e.matmul(ps[:], onesb[:], ch(xb[b], W, c),
                         start=(c == 0), stop=(c == NC - 1))
                return e.matmul(pq[:], onesb[:], sq[sb2][:],
                                start=(c == 0), stop=(c == NC - 1))
            p.do("tensor", mk_mm, inc=(s_mm, 1),
                 wait=(tw + [W_(s_cast, tile_cast0 + c + 2)]) if c == 0
                 else [W_(s_cast, tile_cast0 + c + 2)])

        mean = stat[0:1, 0:W]
        ex2 = stat[0:1, W:2 * W]
        var = stat[0:1, 2 * W:3 * W]
        std = stat[0:1, 3 * W:4 * W]
        rs = stat[0:1, 4 * W:5 * W]
        mb = stat[0:1, 5 * W:6 * W]
        tmp = stat[0:1, 6 * W:7 * W]

        def mk_stats(e):
            e.tensor_scalar(mean, ps[:], 1.0 / rows, None, ALU.mult)
            e.tensor_scalar(ex2, pq[:], 1.0 / rows, None, ALU.mult)
            e.tensor_tensor(tmp, mean, mean, ALU.mult)
            return e.tensor_tensor(var, ex2, tmp, ALU.subtract)
        vw = [W_(s_mm, None)]
        if it >= 1:
            vw.append(W_(s_rw, it * 32))
        p.do("vector", mk_stats, inc=(s_st, 1), wait=vw)

        p.do("scalar", lambda e: e.activation(std, var, AF.Sqrt,
                                              bias=cp.col(1e-5, 1)),
             inc=(s_st, 1), wait=[W_(s_st, None)])

        def mk_rs(e):
            e.reciprocal(rs, std)
            return e.scalar_tensor_tensor(mb, mean, -1.0, rs, ALU.mult, ALU.mult)
        p.do("vector", mk_rs, inc=(s_st, 1), wait=[W_(s_st, None)])

        p.do("sync", lambda e, c0=c0: e.dma_start(rows_d[0:1, c0:c0 + W], rs),
             inc=(s_rw, 16), wait=[W_(s_st, None)])
        p.do("sync", lambda e, c0=c0: e.dma_start(rows_d[1:2, c0:c0 + W], mb),
             inc=(s_rw, 16))

        p.do("sync",
             lambda e, b=b, c0=c0: e.dma_start(
                 rsb[b][:],
                 rows_d[0:1, c0:c0 + W].partition_broadcast(128)[:, 0, :]),
             inc=(s_bc, 16), wait=[W_(s_rw, None)])
        p.do("sync",
             lambda e, b=b, c0=c0: e.dma_start(
                 mbb[b][:],
                 rows_d[1:2, c0:c0 + W].partition_broadcast(128)[:, 0, :]),
             inc=(s_bc, 16))

        for c in range(NC):
            p.do("vector",
                 lambda e, b=b, c=c: e.tensor_tensor(
                     ch(xt[b], W, c), ch(xt[b], W, c), rsb[b][:], ALU.mult),
                 wait=[W_(s_bc, None)] if c == 0 else ())
        for c in range(NC):
            p.do("vector",
                 lambda e, b=b, c=c: e.tensor_tensor(
                     ch(xt[b], W, c), ch(xt[b], W, c), mbb[b][:], ALU.add))
        p.do("vector", lambda e, b=b: e.tensor_copy(xo[b][:], xt[b][:]),
             inc=(s_fin, 1))

        if xnf_d is not None:
            p.do("sync",
                 lambda e, b=b, c0=c0: e.dma_start(
                     fm(xnf_d, 0, c0, W, NC),
                     xt[b][:].rearrange("p (c t) -> p c t", t=W)),
                 inc=(s_out, 16), wait=[W_(s_fin, None)])
        p.do("sync",
             lambda e, b=b, c0=c0: e.dma_start(
                 fm(xn_d, 0, c0, W, NC),
                 xo[b][:].rearrange("p (c t) -> p c t", t=W)),
             inc=(s_out, 16), wait=[W_(s_fin, None)])
        out_hist.append(s_out.n)

    return s_out



def ini_identities(p, ctx, ic, tag=""):
    nc = p.nc
    idf = ctx.enter_context(nc.sbuf_tensor(f"idf{tag}", [128, 128], F32,
                                            side="right"))
    idb = ctx.enter_context(nc.sbuf_tensor(f"idb{tag}", [128, 128], BF16,
                                           side="right"))

    def mk(e):
        e.memset(idf[:], 1.0)
        e.affine_select(idf[:], idf[:], pattern=[[1, 128]],
                        compare_op=ALU.is_equal, fill=0.0, base=0,
                        channel_multiplier=-1)
        e.memset(idb[:], 1.0)
        return e.affine_select(idb[:], idb[:], pattern=[[1, 128]],
                               compare_op=ALU.is_equal, fill=0.0, base=0,
                               channel_multiplier=-1)
    ic.step("gpsimd", mk)
    return idf, idb


def ini_weight_bf16(p, ctx, ic, w_d, R, C, tag, sctx=None, side="left",
                    stage_side="right"):
    nc = p.nc
    nk = R // 128
    wb = ctx.enter_context(nc.sbuf_tensor(f"wb{tag}", [128, nk * C], BF16,
                                          side=side))
    with nc.sbuf_tensor(f"wstg{tag}", [128, C], F32, side=stage_side) as stg:
        for c in range(nk):
            ic.step("sync",
                    lambda e, c=c: e.dma_start(
                        stg[:], w_d[c * 128:(c + 1) * 128, :]), dma=True)
            ic.step("vector",
                    lambda e, c=c: e.tensor_copy(ch(wb, C, c), stg[:]))
    return wb


def ini_biasT(p, ctx, ic, b_d, m, ptmp, idf, tag, sctx=None, side="right"):
    nc = p.nc
    dst = ctx.enter_context(nc.sbuf_tensor(f"bT{tag}", [128, m], F32,
                                           side=side))
    with nc.sbuf_tensor(f"bstg{tag}", [m, 128], F32, side="right") as stg:
        ic.step("sync", lambda e: e.dma_start(stg[:], b_d), dma=True)
        ic.step("tensor",
                lambda e: e.transpose(ptmp[:, 0:m], stg[:], idf[0:m, 0:m]))
        ic.step("vector", lambda e: e.tensor_copy(dst[:], ptmp[:, 0:m]))
    return dst


def ini_row_bf16(p, ctx, ic, b_d, n, tag, sctx=None, side="right",
                 stage_side="right"):
    nc = p.nc
    dst = ctx.enter_context(nc.sbuf_tensor(f"row{tag}", [1, n], BF16,
                                           side=side))
    with nc.sbuf_tensor(f"rstg{tag}", [1, n], F32, side=stage_side) as stg:
        flat = b_d.rearrange("m p -> (m p)").unsqueeze(0)
        ic.step("sync", lambda e: e.dma_start(stg[:], flat), dma=True)
        ic.step("vector", lambda e: e.tensor_copy(dst[:], stg[:]))
    return dst



def emit_l0attn(p, ctx, tctx, ic, c, cp, idf, idb, ptmp,
                xn_d, wqkv_d, bqkvm_d, wout_d, boutm_d, s0_d, tag="b",
                dbg=None):
    nc = p.nc

    bqkvT = ini_biasT(p, ctx, ic, bqkvm_d, 24, ptmp, idf, f"{tag}qkv")
    boutT = ini_biasT(p, ctx, ic, boutm_d, 8, ptmp, idf, f"{tag}out")

    bqkvT_qs = ctx.enter_context(nc.sbuf_tensor(f"{tag}bqs", [128, 8], F32,
                                                side="right"))
    ic.step("vector", lambda e: e.tensor_scalar(
        bqkvT_qs[:], bqkvT[:, 0:8], 0.125, None, ALU.mult))

    i7 = ctx.enter_context(nc.sbuf_tensor(f"{tag}i7", [64, 64], F32,
                                           side="right"))
    i15 = ctx.enter_context(nc.sbuf_tensor(f"{tag}i15", [64, 64], F32,
                                           side="right"))
    i13 = ctx.enter_context(nc.sbuf_tensor(f"{tag}i13", [64, 64], F32,
                                           side="right"))
    ic.step("vector", lambda e: e.tensor_scalar(
        i7[:], idf[0:64, 0:64], 7.0, None, ALU.mult))
    ic.step("vector", lambda e: e.tensor_scalar(
        i15[:], idf[0:64, 0:64], 15.0, None, ALU.mult))
    ic.step("vector", lambda e: e.tensor_scalar(
        i13[:], idf[0:64, 0:64], 13.0, None, ALU.mult))
    onesr = ctx.enter_context(nc.sbuf_tensor(f"{tag}onesr", [1, 128], BF16,
                                              side="right"))
    ic.step("gpsimd", lambda e: e.memset(onesr[:], 1.0))

    sb = lambda n, s, dt: tctx.enter_context(nc.sbuf_tensor(n, s, dt))
    wqkvb = ini_weight_bf16(p, tctx, c, wqkv_d, D, 3 * D, f"{tag}qkv",
                            stage_side="left")
    woutb = ini_weight_bf16(p, tctx, c, wout_d, D, D, f"{tag}out",
                            stage_side="left")
    bqkv_row = ini_row_bf16(p, tctx, c, bqkvm_d, 3 * D, f"{tag}qkv",
                            side="left", stage_side="left")
    xnb = [sb(f"{tag}xnb0", [128, 8 * 512], BF16)]
    qg = sb(f"{tag}qg", [128, 4 * SEQ], BF16)
    kg = sb(f"{tag}kg", [128, 4 * SEQ], BF16)
    vg = sb(f"{tag}vg", [128, 16 * 512], BF16)
    ohd = sb(f"{tag}ohd", [128, 8 * SEQ], BF16)
    t1 = sb(f"{tag}t1", [128, 1024], F32)
    t2 = sb(f"{tag}t2", [128, 512], F32)
    t3 = sb(f"{tag}t3", [128, 256], F32)
    t4 = sb(f"{tag}t4", [128, 128], F32)
    t5 = sb(f"{tag}t5", [128, 64], F32)
    qlb = sb(f"{tag}qlb", [128, 4 * 64], BF16)
    klb = sb(f"{tag}klb", [128, 4 * 64], BF16)
    a1m = sb(f"{tag}a1m", [64, SEQ], BF16)
    a1t = sb(f"{tag}a1t", [128, 64], BF16)
    e1f = sb(f"{tag}e1f", [128, 64], F32)
    s3f = sb(f"{tag}s3f", [64, SEQ], F32)
    a3f = sb(f"{tag}a3f", [64, SEQ], F32)
    a3b = sb(f"{tag}a3b", [64, SEQ], BF16)
    a3T = sb(f"{tag}a3T", [128, 16 * 64], BF16)
    e2f = sb(f"{tag}e2f", [64, 64], F32)
    cols = sb(f"{tag}cols", [128, 4], F32)
    a2b = sb(f"{tag}a2b", [64, 64], F32)
    a2T = sb(f"{tag}a2T", [64, 64], F32)
    Vb = sb(f"{tag}Vb", [64, 64], F32)
    VTb = sb(f"{tag}VTb", [64, 64], F32)
    KVb = sb(f"{tag}KVb", [64, 64], F32)
    KVT = sb(f"{tag}KVT", [64, 64], F32)
    C1b = sb(f"{tag}C1b", [64, 64], F32)
    C2b = sb(f"{tag}C2b", [64, 64], F32)
    C3b = sb(f"{tag}C3b", [64, 64], F32)
    avb = sb(f"{tag}avb", [64, 64], F32)
    zb = sb(f"{tag}zb", [64, 64], BF16)
    s0t = sb(f"{tag}s0t", [128, 512], F32)

    pA = tctx.enter_context(nc.psum_tensor(f"{tag}pA", [128, 512], F32))
    pE = tctx.enter_context(nc.psum_tensor(f"{tag}pE", [128, 64], F32))
    pT = tctx.enter_context(nc.psum_tensor(f"{tag}pT", [128, 128], BF16))
    pE64 = pE[0:64, 0:64]
    pT64 = pT[0:64, 0:64]
    idb64 = idb[0:64, 0:64]

    neg = cols[:, 0:1]
    ssum = cols[:, 1:2]
    rec = cols[:, 2:3]

    def softmax_free(parts, src, tmp, dst_b):
        c.step("vector", lambda e: e.reduce_max(
            neg[0:parts, :], src, axis=AX, negate=True))
        c.step("scalar", lambda e: e.activation(
            tmp, src, AF.Exp, bias=neg[0:parts, :],
            accum_out=ssum[0:parts, :]))
        c.step("vector", lambda e: e.reciprocal(rec[0:parts, :],
                                                ssum[0:parts, :]))
        c.step("vector", lambda e: e.tensor_scalar(
            dst_b, tmp, rec[0:parts, :], None, ALU.mult))

    for b in range(BPC):
        tb = b * SEQ
        for hg in range(2):
            F0 = hg * 512
            for tt in range(4):
                xv = xnb[0]
                c.step("sync", lambda e, xv=xv, tb=tb, tt=tt: e.dma_start(
                    xv[:].rearrange("p (c t) -> p c t", t=512),
                    fm(xn_d, 0, tb + tt * 512, 512, 8)), dma=True)
                for qc in range(4):
                    gfc = F0 // 128 + qc

                    def mmq(e, xv=xv, gfc=gfc):
                        for k in range(8):
                            i = e.matmul(
                                pA[:],
                                wqkvb[:, k * 3 * D + gfc * 128:
                                      k * 3 * D + gfc * 128 + 128],
                                ch(xv, 512, k),
                                start=(k == 0), stop=(k == 7))
                        return i
                    c.step("tensor", mmq)
                    c.step("scalar", lambda e, qc=qc, tt=tt, gfc=gfc: e.activation(
                        qg[:, qc * SEQ + tt * 512: qc * SEQ + tt * 512 + 512],
                        pA[:], AF.Identity, bias=bqkvT_qs[:, gfc:gfc + 1],
                        scale=cp.col(0.125)))
                for kc in range(4):
                    gfc = 8 + F0 // 128 + kc
                    col0 = D + F0 + kc * 128

                    def mmk(e, xv=xv, col0=col0):
                        for k in range(8):
                            i = e.matmul(
                                pA[:],
                                wqkvb[:, k * 3 * D + col0:
                                      k * 3 * D + col0 + 128],
                                ch(xv, 512, k),
                                start=(k == 0), stop=(k == 7))
                        return i
                    c.step("tensor", mmk)
                    c.step("scalar", lambda e, kc=kc, tt=tt, gfc=gfc: e.activation(
                        kg[:, kc * SEQ + tt * 512: kc * SEQ + tt * 512 + 512],
                        pA[:], AF.Identity, bias=bqkvT[:, gfc:gfc + 1]))
                for ts4 in range(4):
                    i128 = tt * 4 + ts4
                    vcol = 2 * D + F0

                    def mmv(e, xv=xv, ts4=ts4, vcol=vcol):
                        for k in range(8):
                            e.matmul(
                                pA[:],
                                xv[:, k * 512 + ts4 * 128:
                                   k * 512 + ts4 * 128 + 128],
                                wqkvb[:, k * 3 * D + vcol:
                                      k * 3 * D + vcol + 512],
                                start=(k == 0), stop=False)
                        return e.matmul(
                            pA[:], onesr[0:1, :],
                            bqkv_row[0:1, vcol:vcol + 512],
                            start=False, stop=True)
                    c.step("tensor", mmv)
                    c.step("vector", lambda e, i128=i128: e.tensor_copy(
                        ch(vg, 512, i128), pA[:]))

            if dbg is not None and b == 0 and hg == 0:
                c.step("sync", lambda e: e.dma_start(dbg["qg"], qg[:]), dma=True)
                c.step("sync", lambda e: e.dma_start(dbg["kg"], kg[:]), dma=True)
                c.step("sync", lambda e: e.dma_start(dbg["vg"], vg[:]), dma=True)

            for qc in range(4):
                for (srcb, dstb) in ((qg, qlb), (kg, klb)):
                    s0c = qc * SEQ

                    def tree(e, srcb=srcb, dstb=dstb, s0c=s0c, qc=qc):
                        e.tensor_tensor(t1[:], srcb[:, s0c:s0c + SEQ:2],
                                        srcb[:, s0c + 1:s0c + SEQ:2], ALU.add)
                        e.tensor_tensor(t2[:], t1[:, 0:1024:2],
                                        t1[:, 1:1024:2], ALU.add)
                        e.tensor_tensor(t3[:], t2[:, 0:512:2],
                                        t2[:, 1:512:2], ALU.add)
                        e.tensor_tensor(t4[:], t3[:, 0:256:2],
                                        t3[:, 1:256:2], ALU.add)
                        e.tensor_tensor(t5[:], t4[:, 0:128:2],
                                        t4[:, 1:128:2], ALU.add)
                        return e.tensor_scalar(
                            dstb[:, qc * 64:(qc + 1) * 64],
                            t5[:], 1.0 / 32.0, None, ALU.mult)
                    c.step("vector", tree)

            for hh in range(8):
                qc = hh // 2
                half = (hh % 2) * 64
                ql_h = qlb[half:half + 64, qc * 64:(qc + 1) * 64]
                kl_h = klb[half:half + 64, qc * 64:(qc + 1) * 64]

                if dbg is not None and b == 0 and hg == 0 and hh == 0:
                    c.step("sync", lambda e: e.dma_start(dbg["qlb"], qlb[:]), dma=True)
                    c.step("sync", lambda e: e.dma_start(dbg["klb"], klb[:]), dma=True)

                c.step("tensor", lambda e, ql_h=ql_h, kl_h=kl_h: e.matmul(
                    pE64, ql_h, kl_h, start=True, stop=True))
                c.step("vector", lambda e: e.tensor_copy(e2f[:], pE64))
                softmax_free(64, e2f[:], a3f[:, 0:64], a2b[:])
                c.step("tensor", lambda e: e.transpose(
                    ptmp[0:64, 0:64], a2b[:], idf[0:64, 0:64]))
                c.step("vector", lambda e: e.tensor_copy(a2T[:], ptmp[0:64, 0:64]))
                c.step("vector", lambda e: e.tensor_copy(Vb[:], a2b[:]))

                for _ in range(6):
                    c.step("tensor", lambda e: e.matmul(
                        pE64, a2T[:], Vb[:], start=True, stop=True))
                    c.step("vector", lambda e: e.tensor_copy(KVb[:], pE64))
                    c.step("vector", lambda e: e.scalar_tensor_tensor(
                        C1b[:], pE64, -1.0, i7[:], ALU.mult, ALU.add))
                    c.step("tensor", lambda e: e.transpose(
                        ptmp[0:64, 0:64], KVb[:], idf[0:64, 0:64]))
                    c.step("vector", lambda e: e.tensor_copy(KVT[:], ptmp[0:64, 0:64]))
                    c.step("tensor", lambda e: e.matmul(
                        pE64, KVT[:], C1b[:], start=True, stop=True))
                    c.step("vector", lambda e: e.scalar_tensor_tensor(
                        C2b[:], pE64, -1.0, i15[:], ALU.mult, ALU.add))
                    c.step("tensor", lambda e: e.matmul(
                        pE64, KVT[:], C2b[:], start=True, stop=True))
                    c.step("vector", lambda e: e.scalar_tensor_tensor(
                        C3b[:], pE64, -1.0, i13[:], ALU.mult, ALU.add))
                    c.step("tensor", lambda e: e.transpose(
                        ptmp[0:64, 0:64], Vb[:], idf[0:64, 0:64]))
                    c.step("vector", lambda e: e.tensor_copy(VTb[:], ptmp[0:64, 0:64]))
                    c.step("tensor", lambda e: e.matmul(
                        pE64, VTb[:], C3b[:], start=True, stop=True))
                    c.step("vector", lambda e: e.tensor_scalar(
                        Vb[:], pE64, 0.25, None, ALU.mult))
                c.step("tensor", lambda e: e.transpose(
                    ptmp[0:64, 0:64], Vb[:], idf[0:64, 0:64]))
                c.step("vector", lambda e: e.tensor_copy(VTb[:], ptmp[0:64, 0:64]))
                if dbg is not None and b == 0 and hg == 0 and hh == 0:
                    c.step("sync", lambda e: e.dma_start(dbg["a2b"], a2b[:]), dma=True)
                    c.step("sync", lambda e: e.dma_start(dbg["Vb"], Vb[:]), dma=True)

                for i in range(16):
                    qsl = qg[half:half + 64,
                             qc * SEQ + i * 128: qc * SEQ + i * 128 + 128]
                    c.step("tensor", lambda e, qsl=qsl, kl_h=kl_h: e.matmul(
                        pA[:, 0:64], qsl, kl_h, start=True, stop=True))
                    c.step("vector", lambda e: e.reduce_max(
                        neg, pA[:, 0:64], axis=AX, negate=True))
                    c.step("scalar", lambda e: e.activation(
                        e1f[:], pA[:, 0:64], AF.Exp, bias=neg,
                        accum_out=ssum))
                    c.step("vector", lambda e: e.reciprocal(rec, ssum))
                    c.step("vector", lambda e: e.tensor_scalar(
                        a1t[:], e1f[:], rec, None, ALU.mult))
                    c.step("tensor", lambda e: e.transpose(
                        pT[0:64, :], a1t[:], idb[:]))
                    c.step("vector", lambda e, i=i: e.tensor_copy(
                        a1m[:, i * 128:(i + 1) * 128], pT[0:64, :]))

                for j in range(4):
                    ksl = kg[half:half + 64,
                             qc * SEQ + j * 512: qc * SEQ + j * 512 + 512]
                    c.step("tensor", lambda e, ksl=ksl, ql_h=ql_h: e.matmul(
                        pA[0:64, :], ql_h, ksl, start=True, stop=True))
                    c.step("vector", lambda e, j=j: e.tensor_copy(
                        s3f[:, j * 512:(j + 1) * 512], pA[0:64, :]))
                c.step("vector", lambda e: e.reduce_max(
                    neg[0:64, :], s3f[:], axis=AX, negate=True))
                c.step("scalar", lambda e: e.activation(
                    a3f[:], s3f[:], AF.Exp, bias=neg[0:64, :],
                    accum_out=ssum[0:64, :]))
                c.step("vector", lambda e: e.reciprocal(rec[0:64, :],
                                                        ssum[0:64, :]))
                c.step("vector", lambda e: e.tensor_scalar(
                    a3b[:], a3f[:], rec[0:64, :], None, ALU.mult))
                for i in range(16):
                    c.step("tensor", lambda e, i=i: e.transpose(
                        pT[:, 0:64], a3b[:, i * 128:(i + 1) * 128], idb64))
                    c.step("vector", lambda e, i=i: e.tensor_copy(
                        a3T[:, i * 64:(i + 1) * 64], pT[:, 0:64]))

                def mmav(e, hh=hh):
                    for i in range(16):
                        ins = e.matmul(
                            pE64, a3T[:, i * 64:(i + 1) * 64],
                            vg[:, i * 512 + hh * 64: i * 512 + hh * 64 + 64],
                            start=(i == 0), stop=(i == 15))
                    return ins
                c.step("tensor", mmav)
                c.step("vector", lambda e: e.tensor_copy(avb[:], pE64))

                c.step("tensor", lambda e: e.matmul(
                    pE64, VTb[:], avb[:], start=True, stop=True))
                c.step("vector", lambda e: e.tensor_copy(zb[:], pE64))
                if dbg is not None and b == 0 and hg == 0 and hh == 0:
                    c.step("sync", lambda e: e.dma_start(dbg["avb"], avb[:]), dma=True)
                    c.step("sync", lambda e: e.dma_start(dbg["zb"], zb[:]), dma=True)
                    c.step("sync", lambda e: e.dma_start(dbg["a1m"], a1m[:]), dma=True)
                    c.step("sync", lambda e: e.dma_start(dbg["a3b"], a3b[:]), dma=True)
                gc = hg * 4 + qc
                for j in range(4):
                    c.step("tensor", lambda e, j=j: e.matmul(
                        pA[0:64, :], zb[:],
                        a1m[:, j * 512:(j + 1) * 512], start=True, stop=True))
                    c.step("vector", lambda e, j=j, gc=gc, half=half:
                           e.tensor_copy(
                               ohd[half:half + 64,
                                   gc * SEQ + j * 512: gc * SEQ + j * 512 + 512],
                               pA[0:64, :]))

        for fc in range(8):
            for ttj in range(4):
                def mmo(e, fc=fc, ttj=ttj):
                    for k in range(8):
                        i = e.matmul(
                            pA[:],
                            woutb[:, k * D + fc * 128: k * D + fc * 128 + 128],
                            ohd[:, k * SEQ + ttj * 512: k * SEQ + ttj * 512 + 512],
                            start=(k == 0), stop=(k == 7))
                    return i
                c.step("tensor", mmo)
                c.step("scalar", lambda e, fc=fc: e.activation(
                    s0t[:], pA[:], AF.Identity, bias=boutT[:, fc:fc + 1]))
                c.step("sync", lambda e, fc=fc, ttj=ttj, tb=tb: e.dma_start(
                    s0_d[fc * 128:(fc + 1) * 128,
                         tb + ttj * 512: tb + ttj * 512 + 512], s0t[:]),
                    dma=True)
    return c


def build_l0attn(debug=False):
    nc = bass.Bass("TRN2", target_bir_lowering=False)
    xn_d = nc.dram_tensor("xn", [D, T], BF16, kind="ExternalInput").ap()
    wqkv_d = nc.dram_tensor("wqkv", [D, 3 * D], F32, kind="ExternalInput").ap()
    bqkvm_d = nc.dram_tensor("bqkvm", [24, 128], F32, kind="ExternalInput").ap()
    wout_d = nc.dram_tensor("wout0", [D, D], F32, kind="ExternalInput").ap()
    boutm_d = nc.dram_tensor("boutm", [8, 128], F32, kind="ExternalInput").ap()
    s0_d = nc.dram_tensor("s0", [D, T], F32, kind="ExternalOutput").ap()
    dbg = None
    if debug:
        dbg = {}
        for nm, shp, dt in [("qg", [128, 4 * SEQ], BF16), ("kg", [128, 4 * SEQ], BF16),
                            ("vg", [128, 16 * 512], BF16), ("qlb", [128, 256], BF16),
                            ("klb", [128, 256], BF16), ("a2b", [64, 64], F32),
                            ("Vb", [64, 64], F32), ("avb", [64, 64], F32),
                            ("zb", [64, 64], BF16), ("a1m", [64, SEQ], BF16),
                            ("a3b", [64, SEQ], BF16)]:
            dbg[nm] = nc.dram_tensor("dbg_" + nm, shp, dt, kind="ExternalOutput").ap()
    ctx = ExitStack()
    with ctx:
        p = Prog(nc, ctx)
        ic = Chain(p, ctx, "sic", init=True)
        c = Chain(p, ctx, "sc")
        cp = ConstPool(p, ctx, [1e-5, 0.125], ic)
        idf, idb = ini_identities(p, ctx, ic)
        ptmp = ctx.enter_context(nc.psum_tensor("ptmp", [128, 128], F32))
        emit_l0attn(p, ctx, ctx, ic, c, cp, idf, idb, ptmp,
                    xn_d, wqkv_d, bqkvm_d, wout_d, boutm_d, s0_d, dbg=dbg)
        p.do("sync", lambda e: e.wait_ge(c.sem.h, c.sem.n))
        with nc.Block() as blk0:
            p.play_init(blk0)
        with nc.Block() as blk:
            p.play(blk)
    return nc





def emit_l0moe(p, ctx, tctx, ic, c, cp, idf, idb, ptmp,
               x_d, xn_d, xnf_d, s0_d, rwT_d, rb_d, ewT_d, ebm_d,
               x1_d, xn1_d, xn1f_d, tag="m"):
    nc = p.nc

    rwf = ctx.enter_context(nc.sbuf_tensor(f"{tag}rwf", [128, 8 * NE], F32,
                                           side="right"))
    for k in range(8):
        ic.step("sync", lambda e, k=k: e.dma_start(
            rwf[:, k * NE:(k + 1) * NE], rwT_d[k * 128:(k + 1) * 128, :]),
            dma=True)

    sb = lambda n, s, dt: tctx.enter_context(nc.sbuf_tensor(n, s, dt))
    sbr = lambda n, s, dt: ctx.enter_context(
        nc.sbuf_tensor(n, s, dt, side="right"))
    rbb = sbr(f"{tag}rbb", [128, NE], F32)
    ic.step("sync", lambda e: e.dma_start(
        rbb[:], rb_d[0:1, :].partition_broadcast(128)[:, 0, :]), dma=True)
    ones512 = sbr(f"{tag}ones512", [1, 512], BF16)
    onesc = sbr(f"{tag}onesc", [128, 1], BF16)
    ic.step("gpsimd", lambda e: e.memset(ones512[:], 1.0))
    ic.step("gpsimd", lambda e: e.memset(onesc[:], 1.0))

    xnb = sb(f"{tag}xnb", [128, 8 * 512], BF16)
    xnfb = sb(f"{tag}xnfb", [128, 8 * 512], F32)
    ebstg = sb(f"{tag}ebstg", [1, D], F32)
    ebe = sb(f"{tag}ebe", [1, D], BF16)
    yacc = sb(f"{tag}yacc", [128, 8 * SEQ], F32)
    ewb = sb(f"{tag}ewb", [128, 8 * D], BF16)
    wstg = sb(f"{tag}wstg", [128, D], F32)
    lgf = sb(f"{tag}lgf", [128, NE], F32)
    exf = sb(f"{tag}exf", [128, NE], F32)
    iseq = sb(f"{tag}iseq", [128, NE], F32)
    maskt = sb(f"{tag}maskt", [128, NE], F32)
    cols = sb(f"{tag}cols", [128, 4], F32)
    maskb = sb(f"{tag}maskb", [128, 512], F32)
    tmpf = sb(f"{tag}tmpf", [128, 512], F32)
    xband = sb(f"{tag}xband", [128, 8 * 512], F32)
    sband = sb(f"{tag}sband", [128, 8 * 512], F32)
    gtile = sb(f"{tag}gtile", [128, 512], F32)
    x1bb = sb(f"{tag}x1bb", [128, 8 * 512], BF16)
    xo = sb(f"{tag}xo", [128, 8 * 512], BF16)
    sqb = sb(f"{tag}sqb", [128, 512], BF16)
    stat = sb(f"{tag}stat", [1, 7 * 512], F32)
    rsb = sb(f"{tag}rsb", [128, 512], F32)
    mbb = sb(f"{tag}mbb", [128, 512], F32)

    pA = tctx.enter_context(nc.psum_tensor(f"{tag}pA", [128, 512], F32))
    pS = tctx.enter_context(nc.psum_tensor(f"{tag}pS", [1, 512], F32))
    pQ = tctx.enter_context(nc.psum_tensor(f"{tag}pQ", [1, 512], F32))
    cem_d = nc.dram_tensor(f"{tag}cem", [NE, T], F32, kind="Internal").ap()
    rows_d = nc.dram_tensor(f"{tag}rows", [2, T], F32, kind="Internal").ap()

    neg = cols[:, 0:1]
    mx = cols[:, 1:2]
    ssum = cols[:, 2:3]
    rec = cols[:, 3:4]

    for tt in range(8):
        tcol = tt * 512
        c.step("sync", lambda e, tcol=tcol: e.dma_start(
            xnfb[:].rearrange("p (c t) -> p c t", t=512),
            fm(xnf_d, 0, tcol, 512, 8)), dma=True)
        for ts in range(4):
            def mmr(e, ts=ts):
                for k in range(8):
                    i = e.matmul(pA[:, 0:NE],
                                 xnfb[:, k * 512 + ts * 128: k * 512 + ts * 128 + 128],
                                 rwf[:, k * NE:(k + 1) * NE],
                                 start=(k == 0), stop=(k == 7))
                return i
            c.step("tensor", mmr)
            c.step("vector", lambda e: e.tensor_tensor(
                lgf[:], pA[:, 0:NE], rbb[:], ALU.add))
            c.step("vector", lambda e: e.reduce_max(mx, lgf[:], axis=AX))
            c.step("vector", lambda e: e.reduce_max(neg, lgf[:], axis=AX,
                                                    negate=True))
            c.step("scalar", lambda e: e.activation(
                exf[:], lgf[:], AF.Exp, bias=neg, accum_out=ssum))
            c.step("vector", lambda e: e.reciprocal(rec, ssum))
            c.step("vector", lambda e: e.tensor_scalar(
                iseq[:], lgf[:], mx, 0.0, ALU.subtract, ALU.is_ge))
            c.step("vector", lambda e: e.tensor_scalar(
                maskt[:], iseq[:], rec, None, ALU.mult))
            c.step("tensor", lambda e: e.transpose(
                ptmp[0:NE, :], maskt[:], idf[:]))
            c.step("vector", lambda e: e.tensor_copy(tmpf[0:NE, 0:128],
                                                     ptmp[0:NE, :]))
            c.step("sync", lambda e, tcol=tcol, ts=ts: e.dma_start(
                cem_d[:, tcol + ts * 128: tcol + ts * 128 + 128],
                tmpf[0:NE, 0:128]), dma=True)

    for half in range(2):
        Tb = half * SEQ
        for e_i in range(NE):
            c.step("sync", lambda e, e_i=e_i: e.dma_start(
                ebstg[:], ebm_d[e_i * 8:(e_i + 1) * 8, :]
                .rearrange("m p -> (m p)").unsqueeze(0)), dma=True)
            c.step("vector", lambda e: e.tensor_copy(ebe[:], ebstg[:]))
            for k in range(8):
                c.step("sync", lambda e, e_i=e_i, k=k: e.dma_start(
                    wstg[:], ewT_d[e_i, k * 128:(k + 1) * 128, :]), dma=True)
                c.step("vector", lambda e, k=k: e.tensor_copy(
                    ch(ewb, D, k), wstg[:]))
            for T_t in range(4):
                tcol = Tb + T_t * 512
                c.step("sync", lambda e, tcol=tcol: e.dma_start(
                    xnb[:].rearrange("p (c t) -> p c t", t=512),
                    fm(xn_d, 0, tcol, 512, 8)), dma=True)
                c.step("sync", lambda e, e_i=e_i, tcol=tcol: e.dma_start(
                    maskb[:],
                    cem_d[e_i:e_i + 1, tcol:tcol + 512]
                    .partition_broadcast(128)[:, 0, :]), dma=True)
                for fc in range(8):
                    def mme(e, e_i=e_i, fc=fc):
                        for k in range(8):
                            e.matmul(pA[:],
                                     ewb[:, k * D + fc * 128: k * D + fc * 128 + 128],
                                     ch(xnb, 512, k), start=(k == 0), stop=False)
                        return e.matmul(
                            pA[:], ebe[0:1, fc * 128: fc * 128 + 128],
                            ones512[0:1, :], start=False, stop=True)
                    c.step("tensor", mme)
                    ydst = lambda fc=fc, T_t=T_t: yacc[:, fc * SEQ + T_t * 512:
                                                       fc * SEQ + T_t * 512 + 512]
                    if e_i == 0:
                        c.step("vector", lambda e, ydst=ydst: e.tensor_tensor(
                            ydst(), pA[:], maskb[:], ALU.mult))
                    else:
                        c.step("vector", lambda e: e.tensor_tensor(
                            tmpf[:], pA[:], maskb[:], ALU.mult))
                        c.step("vector", lambda e, ydst=ydst: e.tensor_tensor(
                            ydst(), ydst(), tmpf[:], ALU.add))

        for T_t in range(4):
            tcol = Tb + T_t * 512
            c.step("sync", lambda e, tcol=tcol: e.dma_start(
                xband[:].rearrange("p (c t) -> p c t", t=512),
                fm(x_d, 0, tcol, 512, 8)), dma=True)
            c.step("sync", lambda e, tcol=tcol: e.dma_start(
                sband[:].rearrange("p (c t) -> p c t", t=512),
                fm(s0_d, 0, tcol, 512, 8)), dma=True)
            for fc in range(8):
                ysl = yacc[:, fc * SEQ + T_t * 512: fc * SEQ + T_t * 512 + 512]
                c.step("scalar", lambda e, ysl=ysl: e.activation(
                    gtile[:], ysl, AF.Silu))
                c.step("vector", lambda e, fc=fc: e.tensor_tensor(
                    gtile[:], gtile[:], ch(sband, 512, fc), ALU.mult))
                c.step("vector", lambda e, fc=fc: e.tensor_tensor(
                    ch(xband, 512, fc), ch(xband, 512, fc), gtile[:], ALU.add))
            c.step("sync", lambda e, tcol=tcol: e.dma_start(
                fm(x1_d, 0, tcol, 512, 8),
                xband[:].rearrange("p (c t) -> p c t", t=512)), dma=True)
            c.step("vector", lambda e: e.tensor_copy(x1bb[:], xband[:]))
            for fc in range(8):
                c.step("scalar", lambda e, fc=fc: e.activation(
                    sqb[:], ch(xband, 512, fc), AF.Square))
                def mmln(e, fc=fc):
                    e.matmul(pS[:], onesc[:], ch(x1bb, 512, fc),
                             start=(fc == 0), stop=(fc == 7))
                    return e.matmul(pQ[:], onesc[:], sqb[:],
                                    start=(fc == 0), stop=(fc == 7))
                c.step("tensor", mmln)
            mean = stat[0:1, 0:512]
            ex2 = stat[0:1, 512:1024]
            var = stat[0:1, 1024:1536]
            stdv = stat[0:1, 1536:2048]
            rs = stat[0:1, 2048:2560]
            mb = stat[0:1, 2560:3072]
            tmp1 = stat[0:1, 3072:3584]

            def mkst(e):
                e.tensor_scalar(mean, pS[:], 1.0 / D, None, ALU.mult)
                e.tensor_scalar(ex2, pQ[:], 1.0 / D, None, ALU.mult)
                e.tensor_tensor(tmp1, mean, mean, ALU.mult)
                return e.tensor_tensor(var, ex2, tmp1, ALU.subtract)
            c.step("vector", mkst)
            c.step("scalar", lambda e: e.activation(
                stdv, var, AF.Sqrt, bias=cp.col(1e-5, 1)))

            def mkrs(e):
                e.reciprocal(rs, stdv)
                return e.scalar_tensor_tensor(mb, mean, -1.0, rs,
                                              ALU.mult, ALU.mult)
            c.step("vector", mkrs)
            c.step("sync", lambda e, tcol=tcol: e.dma_start(
                rows_d[0:1, tcol:tcol + 512], rs), dma=True)
            c.step("sync", lambda e, tcol=tcol: e.dma_start(
                rows_d[1:2, tcol:tcol + 512], mb), dma=True)
            c.step("sync", lambda e, tcol=tcol: e.dma_start(
                rsb[:], rows_d[0:1, tcol:tcol + 512]
                .partition_broadcast(128)[:, 0, :]), dma=True)
            c.step("sync", lambda e, tcol=tcol: e.dma_start(
                mbb[:], rows_d[1:2, tcol:tcol + 512]
                .partition_broadcast(128)[:, 0, :]), dma=True)
            for fc in range(8):
                c.step("vector", lambda e, fc=fc: e.tensor_tensor(
                    ch(xband, 512, fc), ch(xband, 512, fc), rsb[:], ALU.mult))
            for fc in range(8):
                c.step("vector", lambda e, fc=fc: e.tensor_tensor(
                    ch(xband, 512, fc), ch(xband, 512, fc), mbb[:], ALU.add))
            c.step("vector", lambda e: e.tensor_copy(xo[:], xband[:]))
            c.step("sync", lambda e, tcol=tcol: e.dma_start(
                fm(xn1f_d, 0, tcol, 512, 8),
                xband[:].rearrange("p (c t) -> p c t", t=512)), dma=True)
            c.step("sync", lambda e, tcol=tcol: e.dma_start(
                fm(xn1_d, 0, tcol, 512, 8),
                xo[:].rearrange("p (c t) -> p c t", t=512)), dma=True)
    return c


def build_l0moe():
    nc = bass.Bass("TRN2", target_bir_lowering=False)
    x_d = nc.dram_tensor("x", [D, T], F32, kind="ExternalInput").ap()
    xn_d = nc.dram_tensor("xn", [D, T], BF16, kind="ExternalInput").ap()
    xnf_d = nc.dram_tensor("xnf", [D, T], F32, kind="ExternalInput").ap()
    s0_d = nc.dram_tensor("s0", [D, T], F32, kind="ExternalInput").ap()
    rwT_d = nc.dram_tensor("rwT", [D, NE], F32, kind="ExternalInput").ap()
    rb_d = nc.dram_tensor("rb", [1, NE], F32, kind="ExternalInput").ap()
    ewT_d = nc.dram_tensor("ewT", [NE, D, D], F32, kind="ExternalInput").ap()
    ebm_d = nc.dram_tensor("ebm", [64, 128], F32, kind="ExternalInput").ap()
    x1_d = nc.dram_tensor("x1", [D, T], F32, kind="ExternalOutput").ap()
    xn1_d = nc.dram_tensor("xn1", [D, T], BF16, kind="ExternalOutput").ap()
    xn1f_d = nc.dram_tensor("xn1f", [D, T], F32, kind="ExternalOutput").ap()
    ctx = ExitStack()
    with ctx:
        p = Prog(nc, ctx)
        ic = Chain(p, ctx, "sic", init=True)
        c = Chain(p, ctx, "sc")
        cp = ConstPool(p, ctx, [1e-5, 0.125], ic)
        idf, idb = ini_identities(p, ctx, ic)
        ptmp = ctx.enter_context(nc.psum_tensor("ptmp", [128, 128], F32))
        emit_l0moe(p, ctx, ctx, ic, c, cp, idf, idb, ptmp,
                   x_d, xn_d, xnf_d, s0_d, rwT_d, rb_d, ewT_d, ebm_d,
                   x1_d, xn1_d, xn1f_d)
        p.do("sync", lambda e: e.wait_ge(c.sem.h, c.sem.n))
        with nc.Block() as blk0:
            p.play_init(blk0)
        with nc.Block() as blk:
            p.play(blk)
    return nc





def emit_l1pre(p, ctx, tctx, ic, c, cp, idf, idb, ptmp,
               xn1_d, winT_d, wconv_d, bconvm_d, wxT_d, wdtT_d, bdtm_d,
               res_d, xsp_d, xspb_d, bfull_d, dt_d, tag="d"):
    nc = p.nc
    bconvT = ini_biasT(p, ctx, ic, bconvm_d, 16, ptmp, idf, f"{tag}bc")
    bdtT = ini_biasT(p, ctx, ic, bdtm_d, 16, ptmp, idf, f"{tag}bd")
    wxb = ini_weight_bf16(p, ctx, ic, wxT_d, DI, 32, f"{tag}wx",
                          side="right")

    sb = lambda n, s, dt: tctx.enter_context(nc.sbuf_tensor(n, s, dt))
    wcv = ctx.enter_context(nc.sbuf_tensor(f"{tag}wcv", [128, 16 * 4], F32,
                                           side="right"))
    for ci in range(16):
        ic.step("sync", lambda e, ci=ci: e.dma_start(
            wcv[:, ci * 4:(ci + 1) * 4],
            wconv_d[ci * 128:(ci + 1) * 128, :]), dma=True)

    wband = sb(f"{tag}wband", [128, 32768], BF16)
    wstg = sb(f"{tag}wstg", [128, 4096], F32)
    xn1b = sb(f"{tag}xn1b", [128, 8 * 512], BF16)
    ot = sb(f"{tag}ot", [128, 512], F32)
    xpad = sb(f"{tag}xpad", [128, SEQ + 3], F32)
    cvt = sb(f"{tag}cvt", [128, SEQ], F32)
    xst = sb(f"{tag}xst", [128, SEQ], F32)
    xsb = sb(f"{tag}xsb", [128, SEQ], BF16)
    spband = sb(f"{tag}spband", [128, 16 * 512], BF16)
    sp1 = sb(f"{tag}sp1", [128, 512], F32)
    sp2 = sb(f"{tag}sp2", [128, 512], F32)
    bx = sb(f"{tag}bx", [32, 512], F32)

    pA = tctx.enter_context(nc.psum_tensor(f"{tag}pA", [128, 512], F32))
    pX = tctx.enter_context(nc.psum_tensor(f"{tag}pX", [32, 512], F32))
    xsr_d = nc.dram_tensor(f"{tag}xsr", [DI, T], F32, kind="Internal").ap()

    for k in range(8):
        c.step("sync", lambda e, k=k: e.dma_start(
            wstg[:], winT_d[k * 128:(k + 1) * 128, :]), dma=True)
        c.step("vector", lambda e, k=k: e.tensor_copy(
            ch(wband, 4096, k), wstg[:]))
    for tt in range(8):
        tcol = tt * 512
        c.step("sync", lambda e, tcol=tcol: e.dma_start(
            xn1b[:].rearrange("p (c t) -> p c t", t=512),
            fm(xn1_d, 0, tcol, 512, 8)), dma=True)
        for fc in range(32):
            def mmw(e, fc=fc):
                for k in range(8):
                    i = e.matmul(pA[:],
                                 wband[:, k * 4096 + fc * 128:
                                       k * 4096 + fc * 128 + 128],
                                 ch(xn1b, 512, k), start=(k == 0), stop=(k == 7))
                return i
            c.step("tensor", mmw)
            c.step("vector", lambda e: e.tensor_copy(ot[:], pA[:]))
            if fc < 16:
                c.step("sync", lambda e, fc=fc, tcol=tcol: e.dma_start(
                    xsr_d[fc * 128:(fc + 1) * 128, tcol:tcol + 512], ot[:]),
                    dma=True)
            else:
                c.step("sync", lambda e, fc=fc, tcol=tcol: e.dma_start(
                    res_d[(fc - 16) * 128:(fc - 15) * 128, tcol:tcol + 512],
                    ot[:]), dma=True)

    for ci in range(16):
        for b in range(BPC):
            tb = b * SEQ
            c.step("gpsimd", lambda e: e.memset(xpad[:, 0:3], 0.0))
            c.step("sync", lambda e, ci=ci, tb=tb: e.dma_start(
                xpad[:, 3:SEQ + 3],
                xsr_d[ci * 128:(ci + 1) * 128, tb:tb + SEQ]), dma=True)

            def cv(e, ci=ci):
                e.tensor_scalar(cvt[:], xpad[:, 0:SEQ],
                                wcv[:, ci * 4:ci * 4 + 1], None, ALU.mult)
                e.scalar_tensor_tensor(cvt[:], xpad[:, 1:SEQ + 1],
                                       wcv[:, ci * 4 + 1:ci * 4 + 2], cvt[:],
                                       ALU.mult, ALU.add)
                e.scalar_tensor_tensor(cvt[:], xpad[:, 2:SEQ + 2],
                                       wcv[:, ci * 4 + 2:ci * 4 + 3], cvt[:],
                                       ALU.mult, ALU.add)
                return e.scalar_tensor_tensor(
                    cvt[:], xpad[:, 3:SEQ + 3],
                    wcv[:, ci * 4 + 3:ci * 4 + 4], cvt[:], ALU.mult, ALU.add)
            c.step("vector", cv)
            c.step("scalar", lambda e, ci=ci: e.activation(
                xst[:], cvt[:], AF.Silu, bias=bconvT[:, ci:ci + 1]))
            c.step("vector", lambda e: e.tensor_copy(xsb[:], xst[:]))
            c.step("sync", lambda e, ci=ci, tb=tb: e.dma_start(
                xsp_d[ci * 128:(ci + 1) * 128, tb:tb + SEQ], xst[:]), dma=True)
            c.step("sync", lambda e, ci=ci, tb=tb: e.dma_start(
                xspb_d[ci * 128:(ci + 1) * 128, tb:tb + SEQ], xsb[:]), dma=True)

    for k in range(16):
        c.step("sync", lambda e, k=k: e.dma_start(
            wstg[:, 0:2048], wdtT_d[k * 128:(k + 1) * 128, :]), dma=True)
        c.step("vector", lambda e, k=k: e.tensor_copy(
            ch(wband, 2048, k), wstg[:, 0:2048]))

    for tt in range(8):
        tcol = tt * 512
        c.step("sync", lambda e, tcol=tcol: e.dma_start(
            spband[:].rearrange("p (c t) -> p c t", t=512),
            fm(xspb_d, 0, tcol, 512, 16)), dma=True)

        def mmx(e):
            for k in range(16):
                i = e.matmul(pX[:], ch(wxb, 32, k), ch(spband, 512, k),
                             start=(k == 0), stop=(k == 15))
            return i
        c.step("tensor", mmx)
        c.step("vector", lambda e: e.tensor_copy(bx[:], pX[:]))
        c.step("sync", lambda e, tcol=tcol: e.dma_start(
            bfull_d[:, tcol:tcol + 512], bx[:]), dma=True)

        for fc in range(16):
            def mmd(e, fc=fc):
                for k in range(16):
                    i = e.matmul(pA[:],
                                 wband[:, k * 2048 + fc * 128:
                                       k * 2048 + fc * 128 + 128],
                                 ch(spband, 512, k), start=(k == 0),
                                 stop=(k == 15))
                return i
            c.step("tensor", mmd)
            c.step("scalar", lambda e, fc=fc: e.activation(
                sp1[:], pA[:], AF.Abs, bias=bdtT[:, fc:fc + 1]))
            c.step("scalar", lambda e: e.activation(
                sp1[:], sp1[:], AF.Exp, scale=cp.col(-1.0)))
            c.step("scalar", lambda e: e.activation(
                sp1[:], sp1[:], AF.Ln, bias=cp.col(1.0)))
            c.step("scalar", lambda e, fc=fc: e.activation(
                sp2[:], pA[:], AF.Relu, bias=bdtT[:, fc:fc + 1]))
            c.step("vector", lambda e: e.tensor_tensor(
                sp2[:], sp2[:], sp1[:], ALU.add))
            c.step("sync", lambda e, fc=fc, tcol=tcol: e.dma_start(
                dt_d[fc * 128:(fc + 1) * 128, tcol:tcol + 512], sp2[:]),
                dma=True)
    return c


def build_l1pre():
    nc = bass.Bass("TRN2", target_bir_lowering=False)
    xn1_d = nc.dram_tensor("xn1", [D, T], BF16, kind="ExternalInput").ap()
    winT_d = nc.dram_tensor("winT", [D, 2 * DI], F32, kind="ExternalInput").ap()
    wconv_d = nc.dram_tensor("wconv", [DI, 4], F32, kind="ExternalInput").ap()
    bconvm_d = nc.dram_tensor("bconvm", [16, 128], F32, kind="ExternalInput").ap()
    wxT_d = nc.dram_tensor("wxT", [DI, 32], F32, kind="ExternalInput").ap()
    wdtT_d = nc.dram_tensor("wdtT", [DI, DI], F32, kind="ExternalInput").ap()
    bdtm_d = nc.dram_tensor("bdtm", [16, 128], F32, kind="ExternalInput").ap()
    res_d = nc.dram_tensor("res", [DI, T], F32, kind="ExternalOutput").ap()
    xsp_d = nc.dram_tensor("xsp", [DI, T], F32, kind="ExternalOutput").ap()
    xspb_d = nc.dram_tensor("xspb", [DI, T], BF16, kind="ExternalOutput").ap()
    bfull_d = nc.dram_tensor("bfull", [32, T], F32, kind="ExternalOutput").ap()
    dt_d = nc.dram_tensor("dt", [DI, T], F32, kind="ExternalOutput").ap()
    ctx = ExitStack()
    with ctx:
        p = Prog(nc, ctx)
        ic = Chain(p, ctx, "sic", init=True)
        c = Chain(p, ctx, "sc")
        cp = ConstPool(p, ctx, [1e-5, 0.125, -1.0, 1.0], ic)
        idf, idb = ini_identities(p, ctx, ic)
        ptmp = ctx.enter_context(nc.psum_tensor("ptmp", [128, 128], F32))
        emit_l1pre(p, ctx, ctx, ic, c, cp, idf, idb, ptmp,
                   xn1_d, winT_d, wconv_d, bconvm_d, wxT_d, wdtT_d, bdtm_d,
                   res_d, xsp_d, xspb_d, bfull_d, dt_d)
        p.do("sync", lambda e: e.wait_ge(c.sem.h, c.sem.n))
        with nc.Block() as blk0:
            p.play_init(blk0)
        with nc.Block() as blk:
            p.play(blk)
    return nc





def emit_scan(p, ctx, tctx, ic, c, cp, idf, idb, ptmp,
              xsp_d, res_d, bfull_d, dt_d, alog_d, dm_d, zb_d,
              y_dbg_d=None, tag="s"):
    nc = p.nc
    Dt = ini_biasT(p, ctx, ic, dm_d, 16, ptmp, idf, f"{tag}D")

    sb = lambda n, s, dt: tctx.enter_context(nc.sbuf_tensor(n, s, dt))
    Af = ctx.enter_context(nc.sbuf_tensor(f"{tag}Af", [128, 16 * 16], F32,
                                          side="right"))
    with nc.sbuf_tensor(f"{tag}astg", [128, 16], F32, side="right") as astg:
        for ci in range(16):
            ic.step("sync", lambda e, ci=ci: e.dma_start(
                astg[:], alog_d[ci * 128:(ci + 1) * 128, :]), dma=True)
            ic.step("scalar", lambda e, ci=ci: e.activation(
                Af[:, ci * 16:(ci + 1) * 16], astg[:], AF.Exp))
            ic.step("vector", lambda e, ci=ci: e.tensor_scalar(
                Af[:, ci * 16:(ci + 1) * 16], Af[:, ci * 16:(ci + 1) * 16],
                -1.0, None, ALU.mult))

    dtb = sb(f"{tag}dtb", [128, 4 * SEQ], F32)
    xpb = sb(f"{tag}xpb", [128, 4 * SEQ], F32)
    ub = sb(f"{tag}ub", [128, 4 * SEQ], F32)
    yb = sb(f"{tag}yb", [128, 4 * SEQ], F32)
    at = sb(f"{tag}at", [128, SEQ], F32)
    bt = sb(f"{tag}bt", [128, SEQ], F32)
    ht = sb(f"{tag}ht", [128, SEQ], F32)
    Bb = sb(f"{tag}Bb", [128, SEQ], F32)
    rt = sb(f"{tag}rt", [128, SEQ], F32)
    zt = sb(f"{tag}zt", [128, SEQ], BF16)

    for b in range(BPC):
        tb = b * SEQ
        for cg in range(4):
            c.step("sync", lambda e, cg=cg, tb=tb: e.dma_start(
                dtb[:].rearrange("p (c t) -> p c t", t=SEQ),
                fm(dt_d, cg * 4, tb, SEQ, 4)), dma=True)
            c.step("sync", lambda e, cg=cg, tb=tb: e.dma_start(
                xpb[:].rearrange("p (c t) -> p c t", t=SEQ),
                fm(xsp_d, cg * 4, tb, SEQ, 4)), dma=True)
            c.step("vector", lambda e: e.tensor_tensor(
                ub[:], dtb[:], xpb[:], ALU.mult))
            for si in range(DS):
                c.step("sync", lambda e, si=si, tb=tb: e.dma_start(
                    Bb[:], bfull_d[16 + si:17 + si, tb:tb + SEQ]
                    .partition_broadcast(128)[:, 0, :]), dma=True)
                for cc in range(4):
                    ci = cg * 4 + cc
                    c.step("scalar", lambda e, cc=cc, ci=ci, si=si:
                           e.activation(at[:], ch(dtb, SEQ, cc), AF.Exp,
                                        scale=Af[:, ci * 16 + si:
                                                 ci * 16 + si + 1]))
                    c.step("vector", lambda e, cc=cc: e.tensor_tensor(
                        bt[:], ch(ub, SEQ, cc), Bb[:], ALU.mult))
                    c.step("vector", lambda e: e.tensor_tensor_scan(
                        ht[:], at[:], bt[:], 0.0, ALU.mult, ALU.add))
                    if si == 0:
                        c.step("vector", lambda e, cc=cc: e.tensor_copy(
                            ch(yb, SEQ, cc), ht[:]))
                    else:
                        c.step("vector", lambda e, cc=cc: e.tensor_tensor(
                            ch(yb, SEQ, cc), ch(yb, SEQ, cc), ht[:], ALU.add))
            for cc in range(4):
                ci = cg * 4 + cc
                c.step("vector", lambda e, cc=cc, ci=ci:
                       e.scalar_tensor_tensor(
                           at[:], ch(xpb, SEQ, cc), Dt[:, ci:ci + 1],
                           ch(yb, SEQ, cc), ALU.mult, ALU.add))
                if y_dbg_d is not None:
                    c.step("sync", lambda e, ci=ci, tb=tb: e.dma_start(
                        y_dbg_d[ci * 128:(ci + 1) * 128, tb:tb + SEQ], at[:]),
                        dma=True)
                c.step("sync", lambda e, ci=ci, tb=tb: e.dma_start(
                    rt[:], res_d[ci * 128:(ci + 1) * 128, tb:tb + SEQ]),
                    dma=True)
                c.step("scalar", lambda e: e.activation(rt[:], rt[:], AF.Silu))
                c.step("vector", lambda e: e.tensor_tensor(
                    zt[:], at[:], rt[:], ALU.mult))
                c.step("sync", lambda e, ci=ci, tb=tb: e.dma_start(
                    zb_d[ci * 128:(ci + 1) * 128, tb:tb + SEQ], zt[:]),
                    dma=True)
    return c


def build_scan(debug=False):
    nc = bass.Bass("TRN2", target_bir_lowering=False)
    xsp_d = nc.dram_tensor("xsp", [DI, T], F32, kind="ExternalInput").ap()
    res_d = nc.dram_tensor("res", [DI, T], F32, kind="ExternalInput").ap()
    bfull_d = nc.dram_tensor("bfull", [32, T], F32, kind="ExternalInput").ap()
    dt_d = nc.dram_tensor("dt", [DI, T], F32, kind="ExternalInput").ap()
    alog_d = nc.dram_tensor("alog", [DI, 16], F32, kind="ExternalInput").ap()
    dm_d = nc.dram_tensor("dm", [16, 128], F32, kind="ExternalInput").ap()
    zb_d = nc.dram_tensor("zb", [DI, T], BF16, kind="ExternalOutput").ap()
    y_dbg_d = None
    if debug:
        y_dbg_d = nc.dram_tensor("ydbg", [DI, T], F32, kind="ExternalOutput").ap()
    ctx = ExitStack()
    with ctx:
        p = Prog(nc, ctx)
        ic = Chain(p, ctx, "sic", init=True)
        c = Chain(p, ctx, "sc")
        cp = ConstPool(p, ctx, [1e-5], ic)
        idf, idb = ini_identities(p, ctx, ic)
        ptmp = ctx.enter_context(nc.psum_tensor("ptmp", [128, 128], F32))
        emit_scan(p, ctx, ctx, ic, c, cp, idf, idb, ptmp,
                  xsp_d, res_d, bfull_d, dt_d, alog_d, dm_d, zb_d,
                  y_dbg_d=y_dbg_d)
        p.do("sync", lambda e: e.wait_ge(c.sem.h, c.sem.n))
        with nc.Block() as blk0:
            p.play_init(blk0)
        with nc.Block() as blk:
            p.play(blk)
    return nc





def emit_l1moe(p, ctx, tctx, ic, c, cp, idf, idb, ptmp,
               xn1_d, xn1f_d, gw_d, w1_d, b1m_d, w2_d, b2m_d, y1_d, tag="f",
               cem_out=None):
    nc = p.nc
    gwf = ctx.enter_context(nc.sbuf_tensor(f"{tag}gwf", [128, 8 * NE], F32,
                                           side="right"))
    for k in range(8):
        ic.step("sync", lambda e, k=k: e.dma_start(
            gwf[:, k * NE:(k + 1) * NE], gw_d[k * 128:(k + 1) * 128, :]),
            dma=True)
    b1T = []
    for e_i in range(NE):
        b1T.append(ini_biasT(p, ctx, ic, b1m_d[e_i * 32:(e_i + 1) * 32, :],
                             32, ptmp, idf, f"{tag}b1_{e_i}"))

    sb = lambda n, s, dt: tctx.enter_context(nc.sbuf_tensor(n, s, dt))
    ones512 = ctx.enter_context(nc.sbuf_tensor(f"{tag}ones512", [1, 512],
                                               BF16, side="right"))
    ic.step("gpsimd", lambda e: e.memset(ones512[:], 1.0))

    xn1b = sb(f"{tag}xn1b", [128, 8 * 512], BF16)
    w1b = sb(f"{tag}w1b", [128, 8 * FF], BF16)
    w2b = sb(f"{tag}w2b", [128, 32 * D], BF16)
    wstg = sb(f"{tag}wstg", [128, 512], F32)
    b2stg = sb(f"{tag}b2stg", [1, D], F32)
    b2e = sb(f"{tag}b2e", [1, D], BF16)
    hbig = sb(f"{tag}hbig", [128, 8192], F32)
    hband = hbig[:].bitcast(BF16)
    yband = sb(f"{tag}yband", [128, 8 * 512], F32)
    maskb = sb(f"{tag}maskb", [128, 512], F32)
    tmpf = sb(f"{tag}tmpf", [128, 512], F32)
    lgf = sb(f"{tag}lgf", [128, NE], F32)
    lg2 = sb(f"{tag}lg2", [128, NE], F32)
    iseq1 = sb(f"{tag}iseq1", [128, NE], F32)
    iseq2 = sb(f"{tag}iseq2", [128, NE], F32)
    comb = sb(f"{tag}comb", [128, NE], F32)
    cols = sb(f"{tag}cols", [128, 8], F32)

    pA = tctx.enter_context(nc.psum_tensor(f"{tag}pA", [128, 512], F32))
    pY = tctx.enter_context(nc.psum_tensor(f"{tag}pY", [128, 512], F32))
    cem_d = nc.dram_tensor(f"{tag}cem", [NE, T], F32, kind="Internal").ap()
    if cem_out is not None:
        dbgw1_d = nc.dram_tensor(f"{tag}dbgw1", [128, 8 * FF], BF16,
                                 kind="ExternalOutput").ap()
        dbgw2_d = nc.dram_tensor(f"{tag}dbgw2", [128, 32 * D], BF16,
                                 kind="ExternalOutput").ap()
        dbgh_d = nc.dram_tensor(f"{tag}dbgh", [128, 32 * 512], BF16,
                                kind="ExternalOutput").ap()
        dbgy_d = nc.dram_tensor(f"{tag}dbgy", [D, T], F32,
                                kind="ExternalOutput").ap()

    mx1 = cols[:, 0:1]
    neg1 = cols[:, 1:2]
    mx2 = cols[:, 2:3]
    dd = cols[:, 3:4]
    den = cols[:, 4:5]
    c1 = cols[:, 5:6]
    c2 = cols[:, 6:7]

    xn1fb = hbig[:]
    for tt in range(8):
        tcol = tt * 512
        c.step("sync", lambda e, tcol=tcol: e.dma_start(
            xn1fb[:, 0:4096].rearrange("p (c t) -> p c t", t=512),
            fm(xn1f_d, 0, tcol, 512, 8)), dma=True)
        for ts in range(4):
            def mmg(e, ts=ts):
                for k in range(8):
                    i = e.matmul(pA[:, 0:NE],
                                 xn1fb[:, k * 512 + ts * 128:
                                       k * 512 + ts * 128 + 128],
                                 gwf[:, k * NE:(k + 1) * NE],
                                 start=(k == 0), stop=(k == 7))
                return i
            c.step("tensor", mmg)
            c.step("vector", lambda e: e.tensor_copy(lgf[:], pA[:, 0:NE]))
            c.step("vector", lambda e: e.reduce_max(mx1, lgf[:], axis=AX))
            c.step("vector", lambda e: e.reduce_max(neg1, lgf[:], axis=AX,
                                                    negate=True))
            c.step("vector", lambda e: e.tensor_scalar(
                iseq1[:], lgf[:], mx1, 0.0, ALU.subtract, ALU.is_ge))
            c.step("vector", lambda e: e.scalar_tensor_tensor(
                lg2[:], iseq1[:], -1e30, lgf[:], ALU.mult, ALU.add))
            c.step("vector", lambda e: e.reduce_max(mx2, lg2[:], axis=AX))
            c.step("scalar", lambda e: e.activation(
                dd, mx2, AF.Exp, bias=neg1))
            c.step("vector", lambda e: e.tensor_scalar(
                den, dd, 1.0, None, ALU.add))
            c.step("vector", lambda e: e.reciprocal(c1, den))
            c.step("vector", lambda e: e.tensor_tensor(c2, dd, c1, ALU.mult))
            c.step("vector", lambda e: e.tensor_scalar(
                iseq2[:], lg2[:], mx2, 0.0, ALU.subtract, ALU.is_ge))
            c.step("vector", lambda e: e.tensor_scalar(
                iseq2[:], iseq2[:], c2, None, ALU.mult))

            def mkcomb(e):
                return e.scalar_tensor_tensor(
                    comb[:], iseq1[:], 1.0, iseq2[:], ALU.mult, ALU.add)
            c.step("vector", lambda e: e.tensor_scalar(
                comb[:], iseq1[:], c1, None, ALU.mult))
            c.step("vector", lambda e: e.tensor_tensor(
                comb[:], comb[:], iseq2[:], ALU.add))
            c.step("tensor", lambda e: e.transpose(
                ptmp[0:NE, :], comb[:], idf[:]))
            c.step("vector", lambda e: e.tensor_copy(tmpf[0:NE, 0:128],
                                                     ptmp[0:NE, :]))
            c.step("sync", lambda e, tcol=tcol, ts=ts: e.dma_start(
                cem_d[:, tcol + ts * 128: tcol + ts * 128 + 128],
                tmpf[0:NE, 0:128]), dma=True)

    if cem_out is not None:
        for rr in range(NE):
            c.step("sync", lambda e, rr=rr: e.dma_start(
                cem_out[rr:rr + 1, :], cem_d[rr:rr + 1, :]), dma=True)

    for e_i in range(NE):
        c.step("sync", lambda e, e_i=e_i: e.dma_start(
            b2stg[:], b2m_d[e_i * 8:(e_i + 1) * 8, :]
            .rearrange("m p -> (m p)").unsqueeze(0)), dma=True)
        c.step("vector", lambda e: e.tensor_copy(b2e[:], b2stg[:]))
        for k in range(8):
            for qr in range(8):
                c.step("sync", lambda e, e_i=e_i, k=k, qr=qr: e.dma_start(
                    wstg[:], w1_d[e_i, k * 128:(k + 1) * 128,
                                  qr * 512:(qr + 1) * 512]), dma=True)
                c.step("vector", lambda e, k=k, qr=qr: e.tensor_copy(
                    w1b[:, k * FF + qr * 512: k * FF + (qr + 1) * 512],
                    wstg[:]))
        for k in range(32):
            for qr in range(2):
                c.step("sync", lambda e, e_i=e_i, k=k, qr=qr: e.dma_start(
                    wstg[:], w2_d[e_i, k * 128:(k + 1) * 128,
                                  qr * 512:(qr + 1) * 512]), dma=True)
                c.step("vector", lambda e, k=k, qr=qr: e.tensor_copy(
                    w2b[:, k * D + qr * 512: k * D + (qr + 1) * 512],
                    wstg[:]))
        if cem_out is not None and e_i == 0:
            c.step("sync", lambda e: e.dma_start(dbgw1_d, w1b[:]), dma=True)
            c.step("sync", lambda e: e.dma_start(dbgw2_d, w2b[:]), dma=True)
        dump_h = (cem_out is not None and e_i == 0)
        for T_t in range(8):
            tcol = T_t * 512
            c.step("sync", lambda e, tcol=tcol: e.dma_start(
                xn1b[:].rearrange("p (c t) -> p c t", t=512),
                fm(xn1_d, 0, tcol, 512, 8)), dma=True)
            c.step("sync", lambda e, e_i=e_i, tcol=tcol: e.dma_start(
                maskb[:],
                cem_d[e_i:e_i + 1, tcol:tcol + 512]
                .partition_broadcast(128)[:, 0, :]), dma=True)
            if e_i > 0:
                c.step("sync", lambda e, tcol=tcol: e.dma_start(
                    yband[:].rearrange("p (c t) -> p c t", t=512),
                    fm(y1_d, 0, tcol, 512, 8)), dma=True)
            for hc in range(32):
                def mmh(e, hc=hc):
                    for k in range(8):
                        i = e.matmul(pA[:],
                                     w1b[:, k * FF + hc * 128:
                                         k * FF + hc * 128 + 128],
                                     ch(xn1b, 512, k), start=(k == 0),
                                     stop=(k == 7))
                    return i
                c.step("tensor", mmh)
                c.step("scalar", lambda e, hc=hc, e_i=e_i: e.activation(
                    hband[:, hc * 512:(hc + 1) * 512], pA[:], AF.Gelu,
                    bias=b1T[e_i][:, hc:hc + 1]))
            for oc in range(8):
                def mmy(e, oc=oc, e_i=e_i):
                    for k in range(32):
                        e.matmul(pY[:],
                                 w2b[:, k * D + oc * 128: k * D + oc * 128 + 128],
                                 hband[:, k * 512:(k + 1) * 512],
                                 start=(k == 0), stop=False)
                    return e.matmul(
                        pY[:], b2e[0:1, oc * 128: oc * 128 + 128],
                        ones512[0:1, :], start=False, stop=True)
                c.step("tensor", mmy)
                if e_i == 0:
                    c.step("vector", lambda e, oc=oc: e.tensor_tensor(
                        ch(yband, 512, oc), pY[:], maskb[:], ALU.mult))
                else:
                    c.step("vector", lambda e: e.tensor_tensor(
                        tmpf[:], pY[:], maskb[:], ALU.mult))
                    c.step("vector", lambda e, oc=oc: e.tensor_tensor(
                        ch(yband, 512, oc), ch(yband, 512, oc), tmpf[:],
                        ALU.add))
            if dump_h and T_t == 0:
                c.step("sync", lambda e: e.dma_start(
                    dbgh_d, hband[:, 0:16384]), dma=True)
            c.step("sync", lambda e, tcol=tcol: e.dma_start(
                fm(y1_d, 0, tcol, 512, 8),
                yband[:].rearrange("p (c t) -> p c t", t=512)), dma=True)
        if cem_out is not None and e_i == 0:
            c.step("sync", lambda e: e.dma_start(dbgy_d, y1_d), dma=True)
    return c


def build_l1moe():
    nc = bass.Bass("TRN2", target_bir_lowering=False)
    xn1_d = nc.dram_tensor("xn1", [D, T], BF16, kind="ExternalInput").ap()
    xn1f_d = nc.dram_tensor("xn1f", [D, T], F32, kind="ExternalInput").ap()
    gw_d = nc.dram_tensor("gw", [D, NE], F32, kind="ExternalInput").ap()
    w1_d = nc.dram_tensor("w1", [NE, D, FF], F32, kind="ExternalInput").ap()
    b1m_d = nc.dram_tensor("b1m", [256, 128], F32, kind="ExternalInput").ap()
    w2_d = nc.dram_tensor("w2", [NE, FF, D], F32, kind="ExternalInput").ap()
    b2m_d = nc.dram_tensor("b2m", [64, 128], F32, kind="ExternalInput").ap()
    y1_d = nc.dram_tensor("y1", [D, T], F32, kind="ExternalOutput").ap()
    cemo_d = nc.dram_tensor("cemo", [NE, T], F32, kind="ExternalOutput").ap()
    ctx = ExitStack()
    with ctx:
        p = Prog(nc, ctx)
        ic = Chain(p, ctx, "sic", init=True)
        c = Chain(p, ctx, "sc")
        cp = ConstPool(p, ctx, [1e-5], ic)
        idf, idb = ini_identities(p, ctx, ic)
        ptmp = ctx.enter_context(nc.psum_tensor("ptmp", [128, 128], F32))
        emit_l1moe(p, ctx, ctx, ic, c, cp, idf, idb, ptmp,
                   xn1_d, xn1f_d, gw_d, w1_d, b1m_d, w2_d, b2m_d, y1_d,
                   cem_out=cemo_d)
        p.do("sync", lambda e: e.wait_ge(c.sem.h, c.sem.n))
        with nc.Block() as blk0:
            p.play_init(blk0)
        with nc.Block() as blk:
            p.play(blk)
    return nc





def emit_final(p, ctx, tctx, ic, c, cp, idf, idb, ptmp,
               x1_d, zb_d, y1_d, wo1T_d, out_d, tag="g"):
    nc = p.nc
    sb = lambda n, s, dt: tctx.enter_context(nc.sbuf_tensor(n, s, dt))
    wo1b = ini_weight_bf16(p, tctx, c, wo1T_d, DI, D, f"{tag}wo",
                           stage_side="left")
    zband = sb(f"{tag}zband", [128, 16 * 512], BF16)
    x1band = sb(f"{tag}x1band", [128, 8 * 512], F32)
    y1band = sb(f"{tag}y1band", [128, 8 * 512], F32)
    gt = sb(f"{tag}gt", [128, 512], F32)
    tt_ = sb(f"{tag}tt", [128, 512], F32)
    pA = tctx.enter_context(nc.psum_tensor(f"{tag}pA", [128, 512], F32))

    for tt in range(8):
        tcol = tt * 512
        c.step("sync", lambda e, tcol=tcol: e.dma_start(
            zband[:].rearrange("p (c t) -> p c t", t=512),
            fm(zb_d, 0, tcol, 512, 16)), dma=True)
        c.step("sync", lambda e, tcol=tcol: e.dma_start(
            x1band[:].rearrange("p (c t) -> p c t", t=512),
            fm(x1_d, 0, tcol, 512, 8)), dma=True)
        c.step("sync", lambda e, tcol=tcol: e.dma_start(
            y1band[:].rearrange("p (c t) -> p c t", t=512),
            fm(y1_d, 0, tcol, 512, 8)), dma=True)
        for fc in range(8):
            def mmf(e, fc=fc):
                for k in range(16):
                    i = e.matmul(pA[:],
                                 wo1b[:, k * D + fc * 128: k * D + fc * 128 + 128],
                                 ch(zband, 512, k), start=(k == 0),
                                 stop=(k == 15))
                return i
            c.step("tensor", mmf)
            c.step("scalar", lambda e, fc=fc: e.activation(
                gt[:], ch(y1band, 512, fc), AF.Silu))
            c.step("vector", lambda e: e.tensor_tensor(
                tt_[:], gt[:], pA[:], ALU.mult))
            c.step("vector", lambda e, fc=fc: e.tensor_tensor(
                ch(x1band, 512, fc), ch(x1band, 512, fc), tt_[:], ALU.add))
        c.step("sync", lambda e, tcol=tcol: e.dma_start(
            fm(out_d, 0, tcol, 512, 8),
            x1band[:].rearrange("p (c t) -> p c t", t=512)), dma=True)
    return c


def build_final():
    nc = bass.Bass("TRN2", target_bir_lowering=False)
    x1_d = nc.dram_tensor("x1", [D, T], F32, kind="ExternalInput").ap()
    zb_d = nc.dram_tensor("zb", [DI, T], BF16, kind="ExternalInput").ap()
    y1_d = nc.dram_tensor("y1", [D, T], F32, kind="ExternalInput").ap()
    wo1T_d = nc.dram_tensor("wo1T", [DI, D], F32, kind="ExternalInput").ap()
    out_d = nc.dram_tensor("out", [D, T], F32, kind="ExternalOutput").ap()
    ctx = ExitStack()
    with ctx:
        p = Prog(nc, ctx)
        ic = Chain(p, ctx, "sic", init=True)
        c = Chain(p, ctx, "sc")
        cp = ConstPool(p, ctx, [1e-5], ic)
        idf, idb = ini_identities(p, ctx, ic)
        ptmp = ctx.enter_context(nc.psum_tensor("ptmp", [128, 128], F32))
        emit_final(p, ctx, ctx, ic, c, cp, idf, idb, ptmp,
                   x1_d, zb_d, y1_d, wo1T_d, out_d)
        p.do("sync", lambda e: e.wait_ge(c.sem.h, c.sem.n))
        with nc.Block() as blk0:
            p.play_init(blk0)
        with nc.Block() as blk:
            p.play(blk)
    return nc



def build_ln():
    nc = bass.Bass("TRN2", target_bir_lowering=False)
    x_d = nc.dram_tensor("x", [D, T], F32, kind="ExternalInput").ap()
    xn_d = nc.dram_tensor("xn", [D, T], BF16, kind="ExternalOutput").ap()
    xnf_d = nc.dram_tensor("xnf", [D, T], F32, kind="ExternalOutput").ap()
    ctx = ExitStack()
    with ctx:
        p = Prog(nc, ctx)
        ic = Chain(p, ctx, "sic", init=True)
        cp = ConstPool(p, ctx, [1e-5], ic)
        s_out = emit_ln(p, ctx, ctx, ic, x_d, xn_d, D, "ln_", cp, xnf_d=xnf_d)
        p.do("sync", lambda e: e.wait_ge(s_out.h, s_out.n))
        with nc.Block() as blk0:
            p.play_init(blk0)
        with nc.Block() as blk:
            p.play(blk)
    return nc



def _f32c(a):
    return np.ascontiguousarray(np.asarray(a, dtype=np.float32))


def _prep_weights(inputs):
    w = {}
    w["wqkv"] = _f32c(inputs["L0_attn_Wqkv"])
    w["bqkvm"] = _f32c(np.asarray(inputs["L0_attn_bqkv"]).reshape(24, 128))
    w["wout0"] = _f32c(inputs["L0_attn_Wout"])
    w["boutm"] = _f32c(np.asarray(inputs["L0_attn_bout"]).reshape(8, 128))
    w["rwT"] = _f32c(np.asarray(inputs["L0_ffn_router_w"]).T)
    w["rb"] = _f32c(np.asarray(inputs["L0_ffn_router_b"]).reshape(1, NE))
    w["ewT"] = _f32c(np.transpose(np.asarray(inputs["L0_ffn_expert_w"]), (0, 2, 1)))
    w["ebm"] = _f32c(np.asarray(inputs["L0_ffn_expert_b"]).reshape(64, 128))
    w["winT"] = _f32c(np.asarray(inputs["L1_attn_W_in"]).T)
    w["wconv"] = _f32c(np.asarray(inputs["L1_attn_W_conv"]).reshape(DI, 4))
    w["bconvm"] = _f32c(np.asarray(inputs["L1_attn_b_conv"]).reshape(16, 128))
    w["wxT"] = _f32c(np.asarray(inputs["L1_attn_W_x"]).T)
    w["wdtT"] = _f32c(np.asarray(inputs["L1_attn_W_dt"]).T)
    w["bdtm"] = _f32c(np.asarray(inputs["L1_attn_b_dt"]).reshape(16, 128))
    w["wo1T"] = _f32c(np.asarray(inputs["L1_attn_W_out"]).T)
    w["alog"] = _f32c(inputs["L1_attn_A_log"])
    w["dm"] = _f32c(np.asarray(inputs["L1_attn_D"]).reshape(16, 128))
    w["gw"] = _f32c(inputs["L1_ffn_gate_w"])
    w["w1"] = _f32c(inputs["L1_ffn_w1"])
    w["b1m"] = _f32c(np.asarray(inputs["L1_ffn_b1"]).reshape(256, 128))
    w["w2"] = _f32c(inputs["L1_ffn_w2"])
    w["b2m"] = _f32c(np.asarray(inputs["L1_ffn_b2"]).reshape(64, 128))
    return w


def _run(nc, maps):
    res = run_bass_kernel_spmd(nc, maps, core_ids=list(range(NCORES)))
    return [{k: np.asarray(v) for k, v in r.items()} for r in res.results]





def build_fused(debug=False):
    nc = bass.Bass("TRN2", target_bir_lowering=False)
    E_IN = "ExternalInput"
    x_d = nc.dram_tensor("x", [D, T], F32, kind=E_IN).ap()
    wqkv_d = nc.dram_tensor("wqkv", [D, 3 * D], F32, kind=E_IN).ap()
    bqkvm_d = nc.dram_tensor("bqkvm", [24, 128], F32, kind=E_IN).ap()
    wout_d = nc.dram_tensor("wout0", [D, D], F32, kind=E_IN).ap()
    boutm_d = nc.dram_tensor("boutm", [8, 128], F32, kind=E_IN).ap()
    rwT_d = nc.dram_tensor("rwT", [D, NE], F32, kind=E_IN).ap()
    rb_d = nc.dram_tensor("rb", [1, NE], F32, kind=E_IN).ap()
    ewT_d = nc.dram_tensor("ewT", [NE, D, D], F32, kind=E_IN).ap()
    ebm_d = nc.dram_tensor("ebm", [64, 128], F32, kind=E_IN).ap()
    winT_d = nc.dram_tensor("winT", [D, 2 * DI], F32, kind=E_IN).ap()
    wconv_d = nc.dram_tensor("wconv", [DI, 4], F32, kind=E_IN).ap()
    bconvm_d = nc.dram_tensor("bconvm", [16, 128], F32, kind=E_IN).ap()
    wxT_d = nc.dram_tensor("wxT", [DI, 32], F32, kind=E_IN).ap()
    wdtT_d = nc.dram_tensor("wdtT", [DI, DI], F32, kind=E_IN).ap()
    bdtm_d = nc.dram_tensor("bdtm", [16, 128], F32, kind=E_IN).ap()
    wo1T_d = nc.dram_tensor("wo1T", [DI, D], F32, kind=E_IN).ap()
    alog_d = nc.dram_tensor("alog", [DI, 16], F32, kind=E_IN).ap()
    dm_d = nc.dram_tensor("dm", [16, 128], F32, kind=E_IN).ap()
    gw_d = nc.dram_tensor("gw", [D, NE], F32, kind=E_IN).ap()
    w1_d = nc.dram_tensor("w1", [NE, D, FF], F32, kind=E_IN).ap()
    b1m_d = nc.dram_tensor("b1m", [256, 128], F32, kind=E_IN).ap()
    w2_d = nc.dram_tensor("w2", [NE, FF, D], F32, kind=E_IN).ap()
    b2m_d = nc.dram_tensor("b2m", [64, 128], F32, kind=E_IN).ap()
    out_d = nc.dram_tensor("out", [D, T], F32, kind="ExternalOutput").ap()

    I = "ExternalOutput" if debug else "Internal"
    xn_d = nc.dram_tensor("i_xn", [D, T], BF16, kind=I).ap()
    xnf_d = nc.dram_tensor("i_xnf", [D, T], F32, kind=I).ap()
    s0_d = nc.dram_tensor("i_s0", [D, T], F32, kind=I).ap()
    x1_d = nc.dram_tensor("i_x1", [D, T], F32, kind=I).ap()
    xn1_d = nc.dram_tensor("i_xn1", [D, T], BF16, kind=I).ap()
    xn1f_d = nc.dram_tensor("i_xn1f", [D, T], F32, kind=I).ap()
    res_d = nc.dram_tensor("i_res", [DI, T], F32, kind=I).ap()
    xsp_d = nc.dram_tensor("i_xsp", [DI, T], F32, kind=I).ap()
    xspb_d = nc.dram_tensor("i_xspb", [DI, T], BF16, kind=I).ap()
    bfull_d = nc.dram_tensor("i_bfull", [32, T], F32, kind=I).ap()
    dt_d = nc.dram_tensor("i_dt", [DI, T], F32, kind=I).ap()
    zb_d = nc.dram_tensor("i_zb", [DI, T], BF16, kind=I).ap()
    y1_d = nc.dram_tensor("i_y1", [D, T], F32, kind=I).ap()

    cemo_d = None
    if debug:
        cemo_d = nc.dram_tensor("f_cemo", [NE, T], F32,
                                kind="ExternalOutput").ap()
    ctx = ExitStack()
    with ctx:
        p = Prog(nc, ctx)
        ic = Chain(p, ctx, "ic", init=True)
        c = Chain(p, ctx, "c")
        cp = ConstPool(p, ctx, [1e-5, 0.125, -1.0, 1.0], ic)
        idf, idb = ini_identities(p, ctx, ic)
        ptmp = ctx.enter_context(nc.psum_tensor("ptmp", [128, 128], F32))

        with ExitStack() as tA:
            sA = emit_ln(p, ctx, tA, ic, x_d, xn_d, D, "a_", cp, xnf_d=xnf_d)
            c.step("sync", lambda e: e.nop(), wait=[W_(sA, None)])
        with ExitStack() as tB:
            emit_l0attn(p, ctx, tB, ic, c, cp, idf, idb, ptmp,
                        xn_d, wqkv_d, bqkvm_d, wout_d, boutm_d, s0_d, tag="b")
        with ExitStack() as tC:
            emit_l0moe(p, ctx, tC, ic, c, cp, idf, idb, ptmp,
                       x_d, xn_d, xnf_d, s0_d, rwT_d, rb_d, ewT_d, ebm_d,
                       x1_d, xn1_d, xn1f_d, tag="m")
        with ExitStack() as tD:
            emit_l1pre(p, ctx, tD, ic, c, cp, idf, idb, ptmp,
                       xn1_d, winT_d, wconv_d, bconvm_d, wxT_d, wdtT_d,
                       bdtm_d, res_d, xsp_d, xspb_d, bfull_d, dt_d, tag="d")
        with ExitStack() as tE:
            emit_scan(p, ctx, tE, ic, c, cp, idf, idb, ptmp,
                      xsp_d, res_d, bfull_d, dt_d, alog_d, dm_d, zb_d, tag="s")
        with ExitStack() as tF:
            emit_l1moe(p, ctx, tF, ic, c, cp, idf, idb, ptmp,
                       xn1_d, xn1f_d, gw_d, w1_d, b1m_d, w2_d, b2m_d, y1_d,
                       tag="f", cem_out=cemo_d)
        with ExitStack() as tG:
            emit_final(p, ctx, tG, ic, c, cp, idf, idb, ptmp,
                       x1_d, zb_d, y1_d, wo1T_d, out_d, tag="g")

        p.do("sync", lambda e: e.wait_ge(c.sem.h, c.sem.n))
        with nc.Block() as blk0:
            p.play_init(blk0)
        with nc.Block() as blk:
            p.play(blk)
    return nc


def kernel(**inputs):
    w = _prep_weights(inputs)
    x = np.asarray(inputs["L0_attn_hidden_states"], dtype=np.float32)
    xT = [np.ascontiguousarray(x[BPC * c: BPC * (c + 1)].reshape(T, D).T)
          for c in range(NCORES)]
    nc = build_fused()
    maps = [dict(w, x=xT[c]) for c in range(NCORES)]
    res = run_bass_kernel_spmd(nc, maps, core_ids=list(range(NCORES)))
    out = np.empty((NCORES * BPC, SEQ, D), np.float32)
    for c in range(NCORES):
        out[BPC * c: BPC * (c + 1)] = \
            np.asarray(res.results[c]["out"]).T.reshape(BPC, SEQ, D)
    return out


def kernel_v0(**inputs):
    w = _prep_weights(inputs)
    x = np.asarray(inputs["L0_attn_hidden_states"], dtype=np.float32)
    xT = [np.ascontiguousarray(x[BPC * c: BPC * (c + 1)].reshape(T, D).T)
          for c in range(NCORES)]

    rA = _run(build_ln(), [{"x": xT[c]} for c in range(NCORES)])

    rB = _run(build_l0attn(), [
        {"xn": rA[c]["xn"], "wqkv": w["wqkv"], "bqkvm": w["bqkvm"],
         "wout0": w["wout0"], "boutm": w["boutm"]} for c in range(NCORES)])

    rC = _run(build_l0moe(), [
        {"x": xT[c], "xn": rA[c]["xn"], "xnf": rA[c]["xnf"],
         "s0": rB[c]["s0"], "rwT": w["rwT"], "rb": w["rb"],
         "ewT": w["ewT"], "ebm": w["ebm"]} for c in range(NCORES)])

    rD = _run(build_l1pre(), [
        {"xn1": rC[c]["xn1"], "winT": w["winT"], "wconv": w["wconv"],
         "bconvm": w["bconvm"], "wxT": w["wxT"], "wdtT": w["wdtT"],
         "bdtm": w["bdtm"]} for c in range(NCORES)])

    rE = _run(build_scan(), [
        {"xsp": rD[c]["xsp"], "res": rD[c]["res"], "bfull": rD[c]["bfull"],
         "dt": rD[c]["dt"], "alog": w["alog"], "dm": w["dm"]}
        for c in range(NCORES)])

    rF = _run(build_l1moe(), [
        {"xn1": rC[c]["xn1"], "xn1f": rC[c]["xn1f"], "gw": w["gw"],
         "w1": w["w1"], "b1m": w["b1m"], "w2": w["w2"], "b2m": w["b2m"]}
        for c in range(NCORES)])

    rG = _run(build_final(), [
        {"x1": rC[c]["x1"], "zb": rE[c]["zb"], "y1": rF[c]["y1"],
         "wo1T": w["wo1T"]} for c in range(NCORES)])

    out = np.empty((NCORES * BPC, SEQ, D), np.float32)
    for c in range(NCORES):
        out[BPC * c: BPC * (c + 1)] = \
            rG[c]["out"].T.reshape(BPC, SEQ, D)
    return out


# version 04
# speedup vs baseline: 1.2237x; 1.2237x over previous
import numpy as np
import ml_dtypes
from contextlib import ExitStack

import concourse.bass as bass
import concourse.mybir as mybir
from concourse.bass_utils import run_bass_kernel_spmd

F32 = mybir.dt.float32
BF16 = mybir.dt.bfloat16
AF = mybir.ActivationFunctionType
ALU = mybir.AluOpType
AX = mybir.AxisListType.X

NCORES = 8
D = 1024
SEQ = 2048
BPC = 2
T = BPC * SEQ
DI = 2048
DS = 16
NH = 16
DH = 64
NL = 64
LW = 32
NE = 8
FF = 4096


class Sem:
    def __init__(self, ctx, nc, name):
        self.h = ctx.enter_context(nc.semaphore(name))
        self.n = 0


class Prog:
    ENGS = ("sync", "tensor", "vector", "scalar", "gpsimd")

    def __init__(self, nc, ctx):
        self.nc = nc
        self.ctx = ctx
        self.ops = {e: [] for e in self.ENGS}
        self.init_ops = {e: [] for e in self.ENGS}

    def ini(self, eng, fn):
        self.init_ops[eng].append(lambda e, fn=fn: fn(e))

    def doi(self, eng, fn, inc=None, wait=()):
        waits = [(s.h, v if v is not None else s.n) for (s, v) in wait]
        if inc is not None:
            sem, k = inc
            sem.n += k
            tgt = (sem.h, k)
        else:
            tgt = None

        def op(e, fn=fn, waits=waits, tgt=tgt):
            for h, v in waits:
                if v > 0:
                    e.wait_ge(h, v)
            instr = fn(e)
            if tgt is not None:
                instr.then_inc(tgt[0], tgt[1])

        self.init_ops[eng].append(op)

    def play_init(self, block):
        for eng in self.ENGS:
            ops = self.init_ops[eng]
            if not ops:
                continue

            def body(e, ops=ops):
                for fn in ops:
                    fn(e)

            getattr(block, eng)(body)

    def do(self, eng, fn, inc=None, wait=()):
        waits = [(s.h, v if v is not None else s.n) for (s, v) in wait]
        if inc is not None:
            sem, k = inc
            sem.n += k
            tgt = (sem.h, k)
        else:
            tgt = None

        def op(e, fn=fn, waits=waits, tgt=tgt):
            for h, v in waits:
                if v > 0:
                    e.wait_ge(h, v)
            instr = fn(e)
            if tgt is not None:
                assert instr is not None, "fn must return instruction for inc"
                instr.then_inc(tgt[0], tgt[1])

        self.ops[eng].append(op)
        return None

    def play(self, block):
        for eng in self.ENGS:
            ops = self.ops[eng]
            if not ops:
                continue

            def body(e, ops=ops):
                for fn in ops:
                    fn(e)

            getattr(block, eng)(body)


def W_(s, v=None):
    return (s, v)


class Chain:

    SEM_LIMIT = 24000

    def __init__(self, p, ctx, name, init=False):
        self.p = p
        self.ctx = ctx
        self.name = name
        self.init = init
        self.n_sems = 0
        self.sem = self._new_sem()
        self.carry = None

    def _new_sem(self):
        self.n_sems += 1
        return Sem(self.ctx, self.p.nc, f"{self.name}_{self.n_sems}")

    def step(self, eng, fn, dma=False, wait=()):
        k = 16 if dma else 1
        if self.sem.n + k > self.SEM_LIMIT:
            self.carry = (self.sem, self.sem.n)
            self.sem = self._new_sem()
        w = list(wait) + [W_(self.sem, self.sem.n)]
        if self.carry is not None:
            w.append(W_(self.carry[0], self.carry[1]))
            self.carry = None
        if self.init:
            self.p.doi(eng, fn, inc=(self.sem, k), wait=w)
        else:
            self.p.do(eng, fn, inc=(self.sem, k), wait=w)


class ConstPool:
    def __init__(self, p, ctx, values, ic):
        self.nc = p.nc
        vals = sorted(set(float(v) for v in values))
        self.tile = ctx.enter_context(
            p.nc.sbuf_tensor("constpool", [128, max(1, len(vals))], F32,
                             side="right"))
        self.idx = {}
        for i, v in enumerate(vals):
            self.idx[v] = i
            ic.step("gpsimd", lambda e, i=i, v=v: e.memset(
                self.tile[:, i:i + 1], v))

    def col(self, v, parts=128):
        i = self.idx[float(v)]
        return self.tile[0:parts, i:i + 1]


def fm(dram_ap, chunk, col0, ncols, nchunks=1):
    r = dram_ap.rearrange("(c p) t -> p c t", p=128)
    return r[:, chunk : chunk + nchunks, col0 : col0 + ncols]


def bd(sb, width, chunk, col0, ncols, nchunks=1):
    r = sb[:].rearrange("p (c t) -> p c t", t=width)
    return r[:, chunk : chunk + nchunks, col0 : col0 + ncols]


def ch(sb, width, chunk, col0=0, ncols=None):
    if ncols is None:
        ncols = width
    base = chunk * width + col0
    return sb[:, base : base + ncols]



def emit_ln(p, ctx, tctx, ic, x_d, xn_d, rows, tag, cp, xnf_d=None):
    nc = p.nc
    NC = rows // 128
    W = 512
    NT = T // W

    xt = [tctx.enter_context(nc.sbuf_tensor(f"{tag}x{i}", [128, NC * W], F32)) for i in range(2)]
    xb = [tctx.enter_context(nc.sbuf_tensor(f"{tag}xb{i}", [128, NC * W], BF16)) for i in range(2)]
    sq = [tctx.enter_context(nc.sbuf_tensor(f"{tag}sq{i}", [128, W], BF16)) for i in range(2)]
    xo = [tctx.enter_context(nc.sbuf_tensor(f"{tag}xo{i}", [128, NC * W], BF16)) for i in range(2)]
    onesb = ctx.enter_context(nc.sbuf_tensor(f"{tag}ones", [128, 1], BF16,
                                             side="right"))
    stat = tctx.enter_context(nc.sbuf_tensor(f"{tag}st", [1, 7 * W], F32))
    rsb = [tctx.enter_context(nc.sbuf_tensor(f"{tag}rsb{i}", [128, W], F32)) for i in range(2)]
    mbb = [tctx.enter_context(nc.sbuf_tensor(f"{tag}mbb{i}", [128, W], F32)) for i in range(2)]
    ps = tctx.enter_context(nc.psum_tensor(f"{tag}ps", [1, W], F32))
    pq = tctx.enter_context(nc.psum_tensor(f"{tag}pq", [1, W], F32))
    rows_d = nc.dram_tensor(f"{tag}rows", [2, T], F32, kind="Internal").ap()

    s_ld = Sem(ctx, nc, f"{tag}ld")
    s_cast = Sem(ctx, nc, f"{tag}cast")
    s_mm = Sem(ctx, nc, f"{tag}mm")
    s_st = Sem(ctx, nc, f"{tag}stt")
    s_rw = Sem(ctx, nc, f"{tag}rw")
    s_bc = Sem(ctx, nc, f"{tag}bc")
    s_out = Sem(ctx, nc, f"{tag}out")
    s_fin = Sem(ctx, nc, f"{tag}fin")

    ic.step("gpsimd", lambda e: e.memset(onesb[:], 1.0))

    out_hist = []
    for it in range(NT):
        b = it % 2
        c0 = it * W
        tile_cast0 = s_cast.n

        w = []
        if it >= 2:
            w = [W_(s_fin, it - 1), W_(s_out, out_hist[it - 1])]
        p.do("sync",
             lambda e, b=b, c0=c0: e.dma_start(
                 xt[b][:].rearrange("p (c t) -> p c t", t=W),
                 fm(x_d, 0, c0, W, NC)),
             inc=(s_ld, 16), wait=w)

        p.do("vector",
             lambda e, b=b: e.tensor_copy(xb[b][:], xt[b][:]),
             inc=(s_cast, 1), wait=[W_(s_ld, None)])

        tw = [W_(s_cast, None)]
        if it >= 1:
            tw.append(W_(s_st, (it - 1) * 3 + 1))
        for c in range(NC):
            sb2 = c % 2
            sw = [W_(s_ld, None)]
            if c >= 2:
                sw.append(W_(s_mm, s_mm.n - 1))
            p.do("scalar",
                 lambda e, b=b, c=c, sb2=sb2: e.activation(
                     sq[sb2][:], ch(xt[b], W, c), AF.Square),
                 inc=(s_cast, 1), wait=sw)

            def mk_mm(e, b=b, c=c, sb2=sb2):
                e.matmul(ps[:], onesb[:], ch(xb[b], W, c),
                         start=(c == 0), stop=(c == NC - 1))
                return e.matmul(pq[:], onesb[:], sq[sb2][:],
                                start=(c == 0), stop=(c == NC - 1))
            p.do("tensor", mk_mm, inc=(s_mm, 1),
                 wait=(tw + [W_(s_cast, tile_cast0 + c + 2)]) if c == 0
                 else [W_(s_cast, tile_cast0 + c + 2)])

        mean = stat[0:1, 0:W]
        ex2 = stat[0:1, W:2 * W]
        var = stat[0:1, 2 * W:3 * W]
        std = stat[0:1, 3 * W:4 * W]
        rs = stat[0:1, 4 * W:5 * W]
        mb = stat[0:1, 5 * W:6 * W]
        tmp = stat[0:1, 6 * W:7 * W]

        def mk_stats(e):
            e.tensor_scalar(mean, ps[:], 1.0 / rows, None, ALU.mult)
            e.tensor_scalar(ex2, pq[:], 1.0 / rows, None, ALU.mult)
            e.tensor_tensor(tmp, mean, mean, ALU.mult)
            return e.tensor_tensor(var, ex2, tmp, ALU.subtract)
        vw = [W_(s_mm, None)]
        if it >= 1:
            vw.append(W_(s_rw, it * 32))
        p.do("vector", mk_stats, inc=(s_st, 1), wait=vw)

        p.do("scalar", lambda e: e.activation(std, var, AF.Sqrt,
                                              bias=cp.col(1e-5, 1)),
             inc=(s_st, 1), wait=[W_(s_st, None)])

        def mk_rs(e):
            e.reciprocal(rs, std)
            return e.scalar_tensor_tensor(mb, mean, -1.0, rs, ALU.mult, ALU.mult)
        p.do("vector", mk_rs, inc=(s_st, 1), wait=[W_(s_st, None)])

        p.do("sync", lambda e, c0=c0: e.dma_start(rows_d[0:1, c0:c0 + W], rs),
             inc=(s_rw, 16), wait=[W_(s_st, None)])
        p.do("sync", lambda e, c0=c0: e.dma_start(rows_d[1:2, c0:c0 + W], mb),
             inc=(s_rw, 16))

        p.do("sync",
             lambda e, b=b, c0=c0: e.dma_start(
                 rsb[b][:],
                 rows_d[0:1, c0:c0 + W].partition_broadcast(128)[:, 0, :]),
             inc=(s_bc, 16), wait=[W_(s_rw, None)])
        p.do("sync",
             lambda e, b=b, c0=c0: e.dma_start(
                 mbb[b][:],
                 rows_d[1:2, c0:c0 + W].partition_broadcast(128)[:, 0, :]),
             inc=(s_bc, 16))

        for c in range(NC):
            p.do("vector",
                 lambda e, b=b, c=c: e.tensor_tensor(
                     ch(xt[b], W, c), ch(xt[b], W, c), rsb[b][:], ALU.mult),
                 wait=[W_(s_bc, None)] if c == 0 else ())
        for c in range(NC):
            p.do("vector",
                 lambda e, b=b, c=c: e.tensor_tensor(
                     ch(xt[b], W, c), ch(xt[b], W, c), mbb[b][:], ALU.add))
        p.do("vector", lambda e, b=b: e.tensor_copy(xo[b][:], xt[b][:]),
             inc=(s_fin, 1))

        if xnf_d is not None:
            p.do("sync",
                 lambda e, b=b, c0=c0: e.dma_start(
                     fm(xnf_d, 0, c0, W, NC),
                     xt[b][:].rearrange("p (c t) -> p c t", t=W)),
                 inc=(s_out, 16), wait=[W_(s_fin, None)])
        p.do("sync",
             lambda e, b=b, c0=c0: e.dma_start(
                 fm(xn_d, 0, c0, W, NC),
                 xo[b][:].rearrange("p (c t) -> p c t", t=W)),
             inc=(s_out, 16), wait=[W_(s_fin, None)])
        out_hist.append(s_out.n)

    return s_out



def ini_identities(p, ctx, ic, tag=""):
    nc = p.nc
    idf = ctx.enter_context(nc.sbuf_tensor(f"idf{tag}", [128, 128], F32,
                                            side="right"))
    idb = ctx.enter_context(nc.sbuf_tensor(f"idb{tag}", [128, 128], BF16,
                                           side="right"))

    def mk(e):
        e.memset(idf[:], 1.0)
        e.affine_select(idf[:], idf[:], pattern=[[1, 128]],
                        compare_op=ALU.is_equal, fill=0.0, base=0,
                        channel_multiplier=-1)
        e.memset(idb[:], 1.0)
        return e.affine_select(idb[:], idb[:], pattern=[[1, 128]],
                               compare_op=ALU.is_equal, fill=0.0, base=0,
                               channel_multiplier=-1)
    ic.step("gpsimd", mk)
    return idf, idb


def ini_weight_bf16(p, ctx, ic, w_d, R, C, tag, sctx=None, side="left",
                    stage_side="right"):
    nc = p.nc
    nk = R // 128
    wb = ctx.enter_context(nc.sbuf_tensor(f"wb{tag}", [128, nk * C], BF16,
                                          side=side))
    with nc.sbuf_tensor(f"wstg{tag}", [128, C], F32, side=stage_side) as stg:
        for c in range(nk):
            ic.step("sync",
                    lambda e, c=c: e.dma_start(
                        stg[:], w_d[c * 128:(c + 1) * 128, :]), dma=True)
            ic.step("vector",
                    lambda e, c=c: e.tensor_copy(ch(wb, C, c), stg[:]))
    return wb


def ini_biasT(p, ctx, ic, b_d, m, ptmp, idf, tag, sctx=None, side="right"):
    nc = p.nc
    dst = ctx.enter_context(nc.sbuf_tensor(f"bT{tag}", [128, m], F32,
                                           side=side))
    with nc.sbuf_tensor(f"bstg{tag}", [m, 128], F32, side="right") as stg:
        ic.step("sync", lambda e: e.dma_start(stg[:], b_d), dma=True)
        ic.step("tensor",
                lambda e: e.transpose(ptmp[:, 0:m], stg[:], idf[0:m, 0:m]))
        ic.step("vector", lambda e: e.tensor_copy(dst[:], ptmp[:, 0:m]))
    return dst


def ini_row_bf16(p, ctx, ic, b_d, n, tag, sctx=None, side="right",
                 stage_side="right"):
    nc = p.nc
    dst = ctx.enter_context(nc.sbuf_tensor(f"row{tag}", [1, n], BF16,
                                           side=side))
    with nc.sbuf_tensor(f"rstg{tag}", [1, n], F32, side=stage_side) as stg:
        flat = b_d.rearrange("m p -> (m p)").unsqueeze(0)
        ic.step("sync", lambda e: e.dma_start(stg[:], flat), dma=True)
        ic.step("vector", lambda e: e.tensor_copy(dst[:], stg[:]))
    return dst



def emit_l0attn(p, ctx, tctx, ic, c, cp, idf, idb, ptmp,
                xn_d, wqkv_d, bqkvm_d, wout_d, boutm_d, s0_d, tag="b",
                dbg=None):
    nc = p.nc

    bqkvT = ini_biasT(p, ctx, ic, bqkvm_d, 24, ptmp, idf, f"{tag}qkv")
    boutT = ini_biasT(p, ctx, ic, boutm_d, 8, ptmp, idf, f"{tag}out")

    bqkvT_qs = ctx.enter_context(nc.sbuf_tensor(f"{tag}bqs", [128, 8], F32,
                                                side="right"))
    ic.step("vector", lambda e: e.tensor_scalar(
        bqkvT_qs[:], bqkvT[:, 0:8], 0.125, None, ALU.mult))

    i7 = ctx.enter_context(nc.sbuf_tensor(f"{tag}i7", [64, 64], F32,
                                           side="right"))
    i15 = ctx.enter_context(nc.sbuf_tensor(f"{tag}i15", [64, 64], F32,
                                           side="right"))
    i13 = ctx.enter_context(nc.sbuf_tensor(f"{tag}i13", [64, 64], F32,
                                           side="right"))
    ic.step("vector", lambda e: e.tensor_scalar(
        i7[:], idf[0:64, 0:64], 7.0, None, ALU.mult))
    ic.step("vector", lambda e: e.tensor_scalar(
        i15[:], idf[0:64, 0:64], 15.0, None, ALU.mult))
    ic.step("vector", lambda e: e.tensor_scalar(
        i13[:], idf[0:64, 0:64], 13.0, None, ALU.mult))
    onesr = ctx.enter_context(nc.sbuf_tensor(f"{tag}onesr", [1, 128], BF16,
                                              side="right"))
    ic.step("gpsimd", lambda e: e.memset(onesr[:], 1.0))

    sb = lambda n, s, dt: tctx.enter_context(nc.sbuf_tensor(n, s, dt))
    wqkvb = ini_weight_bf16(p, tctx, c, wqkv_d, D, 3 * D, f"{tag}qkv",
                            stage_side="left")
    woutb = ini_weight_bf16(p, tctx, c, wout_d, D, D, f"{tag}out",
                            stage_side="left")
    bqkv_row = ini_row_bf16(p, tctx, c, bqkvm_d, 3 * D, f"{tag}qkv",
                            side="left", stage_side="left")
    xnb = [sb(f"{tag}xnb0", [128, 8 * 512], BF16)]
    qg = sb(f"{tag}qg", [128, 4 * SEQ], BF16)
    kg = sb(f"{tag}kg", [128, 4 * SEQ], BF16)
    vg = sb(f"{tag}vg", [128, 16 * 512], BF16)
    ohd = sb(f"{tag}ohd", [128, 8 * SEQ], BF16)
    t1 = sb(f"{tag}t1", [128, 1024], F32)
    t2 = sb(f"{tag}t2", [128, 512], F32)
    t3 = sb(f"{tag}t3", [128, 256], F32)
    t4 = sb(f"{tag}t4", [128, 128], F32)
    t5 = sb(f"{tag}t5", [128, 64], F32)
    qlb = sb(f"{tag}qlb", [128, 4 * 64], BF16)
    klb = sb(f"{tag}klb", [128, 4 * 64], BF16)
    a1m = sb(f"{tag}a1m", [64, SEQ], BF16)
    a1t = sb(f"{tag}a1t", [128, 64], BF16)
    e1f = sb(f"{tag}e1f", [128, 64], F32)
    s3f = sb(f"{tag}s3f", [64, SEQ], F32)
    a3f = sb(f"{tag}a3f", [64, SEQ], F32)
    a3b = sb(f"{tag}a3b", [64, SEQ], BF16)
    a3T = sb(f"{tag}a3T", [128, 16 * 64], BF16)
    e2f = sb(f"{tag}e2f", [64, 64], F32)
    cols = sb(f"{tag}cols", [128, 4], F32)
    a2b = sb(f"{tag}a2b", [64, 64], F32)
    a2T = sb(f"{tag}a2T", [64, 64], F32)
    Vb = sb(f"{tag}Vb", [64, 64], F32)
    VTb = sb(f"{tag}VTb", [64, 64], F32)
    KVb = sb(f"{tag}KVb", [64, 64], F32)
    KVT = sb(f"{tag}KVT", [64, 64], F32)
    C1b = sb(f"{tag}C1b", [64, 64], F32)
    C2b = sb(f"{tag}C2b", [64, 64], F32)
    C3b = sb(f"{tag}C3b", [64, 64], F32)
    avb = sb(f"{tag}avb", [64, 64], F32)
    zb = sb(f"{tag}zb", [64, 64], BF16)
    s0t = sb(f"{tag}s0t", [128, 512], F32)

    pA = tctx.enter_context(nc.psum_tensor(f"{tag}pA", [128, 512], F32))
    pE = tctx.enter_context(nc.psum_tensor(f"{tag}pE", [128, 64], F32))
    pT = tctx.enter_context(nc.psum_tensor(f"{tag}pT", [128, 128], BF16))
    pE64 = pE[0:64, 0:64]
    pT64 = pT[0:64, 0:64]
    idb64 = idb[0:64, 0:64]

    neg = cols[:, 0:1]
    ssum = cols[:, 1:2]
    rec = cols[:, 2:3]

    def softmax_free(parts, src, tmp, dst_b):
        c.step("vector", lambda e: e.reduce_max(
            neg[0:parts, :], src, axis=AX, negate=True))
        c.step("scalar", lambda e: e.activation(
            tmp, src, AF.Exp, bias=neg[0:parts, :],
            accum_out=ssum[0:parts, :]))
        c.step("vector", lambda e: e.reciprocal(rec[0:parts, :],
                                                ssum[0:parts, :]))
        c.step("vector", lambda e: e.tensor_scalar(
            dst_b, tmp, rec[0:parts, :], None, ALU.mult))

    for b in range(BPC):
        tb = b * SEQ
        for hg in range(2):
            F0 = hg * 512
            for tt in range(4):
                xv = xnb[0]
                c.step("sync", lambda e, xv=xv, tb=tb, tt=tt: e.dma_start(
                    xv[:].rearrange("p (c t) -> p c t", t=512),
                    fm(xn_d, 0, tb + tt * 512, 512, 8)), dma=True)
                for qc in range(4):
                    gfc = F0 // 128 + qc

                    def mmq(e, xv=xv, gfc=gfc):
                        for k in range(8):
                            i = e.matmul(
                                pA[:],
                                wqkvb[:, k * 3 * D + gfc * 128:
                                      k * 3 * D + gfc * 128 + 128],
                                ch(xv, 512, k),
                                start=(k == 0), stop=(k == 7))
                        return i
                    c.step("tensor", mmq)
                    c.step("scalar", lambda e, qc=qc, tt=tt, gfc=gfc: e.activation(
                        qg[:, qc * SEQ + tt * 512: qc * SEQ + tt * 512 + 512],
                        pA[:], AF.Identity, bias=bqkvT_qs[:, gfc:gfc + 1],
                        scale=cp.col(0.125)))
                for kc in range(4):
                    gfc = 8 + F0 // 128 + kc
                    col0 = D + F0 + kc * 128

                    def mmk(e, xv=xv, col0=col0):
                        for k in range(8):
                            i = e.matmul(
                                pA[:],
                                wqkvb[:, k * 3 * D + col0:
                                      k * 3 * D + col0 + 128],
                                ch(xv, 512, k),
                                start=(k == 0), stop=(k == 7))
                        return i
                    c.step("tensor", mmk)
                    c.step("scalar", lambda e, kc=kc, tt=tt, gfc=gfc: e.activation(
                        kg[:, kc * SEQ + tt * 512: kc * SEQ + tt * 512 + 512],
                        pA[:], AF.Identity, bias=bqkvT[:, gfc:gfc + 1]))
                for ts4 in range(4):
                    i128 = tt * 4 + ts4
                    vcol = 2 * D + F0

                    def mmv(e, xv=xv, ts4=ts4, vcol=vcol):
                        for k in range(8):
                            e.matmul(
                                pA[:],
                                xv[:, k * 512 + ts4 * 128:
                                   k * 512 + ts4 * 128 + 128],
                                wqkvb[:, k * 3 * D + vcol:
                                      k * 3 * D + vcol + 512],
                                start=(k == 0), stop=False)
                        return e.matmul(
                            pA[:], onesr[0:1, :],
                            bqkv_row[0:1, vcol:vcol + 512],
                            start=False, stop=True)
                    c.step("tensor", mmv)
                    c.step("vector", lambda e, i128=i128: e.tensor_copy(
                        ch(vg, 512, i128), pA[:]))

            if dbg is not None and b == 0 and hg == 0:
                c.step("sync", lambda e: e.dma_start(dbg["qg"], qg[:]), dma=True)
                c.step("sync", lambda e: e.dma_start(dbg["kg"], kg[:]), dma=True)
                c.step("sync", lambda e: e.dma_start(dbg["vg"], vg[:]), dma=True)

            for qc in range(4):
                for (srcb, dstb) in ((qg, qlb), (kg, klb)):
                    s0c = qc * SEQ

                    def tree(e, srcb=srcb, dstb=dstb, s0c=s0c, qc=qc):
                        e.tensor_tensor(t1[:], srcb[:, s0c:s0c + SEQ:2],
                                        srcb[:, s0c + 1:s0c + SEQ:2], ALU.add)
                        e.tensor_tensor(t2[:], t1[:, 0:1024:2],
                                        t1[:, 1:1024:2], ALU.add)
                        e.tensor_tensor(t3[:], t2[:, 0:512:2],
                                        t2[:, 1:512:2], ALU.add)
                        e.tensor_tensor(t4[:], t3[:, 0:256:2],
                                        t3[:, 1:256:2], ALU.add)
                        e.tensor_tensor(t5[:], t4[:, 0:128:2],
                                        t4[:, 1:128:2], ALU.add)
                        return e.tensor_scalar(
                            dstb[:, qc * 64:(qc + 1) * 64],
                            t5[:], 1.0 / 32.0, None, ALU.mult)
                    c.step("vector", tree)

            for hh in range(8):
                qc = hh // 2
                half = (hh % 2) * 64
                ql_h = qlb[half:half + 64, qc * 64:(qc + 1) * 64]
                kl_h = klb[half:half + 64, qc * 64:(qc + 1) * 64]

                if dbg is not None and b == 0 and hg == 0 and hh == 0:
                    c.step("sync", lambda e: e.dma_start(dbg["qlb"], qlb[:]), dma=True)
                    c.step("sync", lambda e: e.dma_start(dbg["klb"], klb[:]), dma=True)

                c.step("tensor", lambda e, ql_h=ql_h, kl_h=kl_h: e.matmul(
                    pE64, ql_h, kl_h, start=True, stop=True))
                c.step("vector", lambda e: e.tensor_copy(e2f[:], pE64))
                softmax_free(64, e2f[:], a3f[:, 0:64], a2b[:])
                c.step("tensor", lambda e: e.transpose(
                    ptmp[0:64, 0:64], a2b[:], idf[0:64, 0:64]))
                c.step("vector", lambda e: e.tensor_copy(a2T[:], ptmp[0:64, 0:64]))
                c.step("vector", lambda e: e.tensor_copy(Vb[:], a2b[:]))

                for _ in range(6):
                    c.step("tensor", lambda e: e.matmul(
                        pE64, a2T[:], Vb[:], start=True, stop=True))
                    c.step("vector", lambda e: e.tensor_copy(KVb[:], pE64))
                    c.step("vector", lambda e: e.scalar_tensor_tensor(
                        C1b[:], pE64, -1.0, i7[:], ALU.mult, ALU.add))
                    c.step("tensor", lambda e: e.transpose(
                        ptmp[0:64, 0:64], KVb[:], idf[0:64, 0:64]))
                    c.step("vector", lambda e: e.tensor_copy(KVT[:], ptmp[0:64, 0:64]))
                    c.step("tensor", lambda e: e.matmul(
                        pE64, KVT[:], C1b[:], start=True, stop=True))
                    c.step("vector", lambda e: e.scalar_tensor_tensor(
                        C2b[:], pE64, -1.0, i15[:], ALU.mult, ALU.add))
                    c.step("tensor", lambda e: e.matmul(
                        pE64, KVT[:], C2b[:], start=True, stop=True))
                    c.step("vector", lambda e: e.scalar_tensor_tensor(
                        C3b[:], pE64, -1.0, i13[:], ALU.mult, ALU.add))
                    c.step("tensor", lambda e: e.transpose(
                        ptmp[0:64, 0:64], Vb[:], idf[0:64, 0:64]))
                    c.step("vector", lambda e: e.tensor_copy(VTb[:], ptmp[0:64, 0:64]))
                    c.step("tensor", lambda e: e.matmul(
                        pE64, VTb[:], C3b[:], start=True, stop=True))
                    c.step("vector", lambda e: e.tensor_scalar(
                        Vb[:], pE64, 0.25, None, ALU.mult))
                c.step("tensor", lambda e: e.transpose(
                    ptmp[0:64, 0:64], Vb[:], idf[0:64, 0:64]))
                c.step("vector", lambda e: e.tensor_copy(VTb[:], ptmp[0:64, 0:64]))
                if dbg is not None and b == 0 and hg == 0 and hh == 0:
                    c.step("sync", lambda e: e.dma_start(dbg["a2b"], a2b[:]), dma=True)
                    c.step("sync", lambda e: e.dma_start(dbg["Vb"], Vb[:]), dma=True)

                for i in range(16):
                    qsl = qg[half:half + 64,
                             qc * SEQ + i * 128: qc * SEQ + i * 128 + 128]
                    c.step("tensor", lambda e, qsl=qsl, kl_h=kl_h: e.matmul(
                        pA[:, 0:64], qsl, kl_h, start=True, stop=True))
                    c.step("vector", lambda e: e.reduce_max(
                        neg, pA[:, 0:64], axis=AX, negate=True))
                    c.step("scalar", lambda e: e.activation(
                        e1f[:], pA[:, 0:64], AF.Exp, bias=neg,
                        accum_out=ssum))
                    c.step("vector", lambda e: e.reciprocal(rec, ssum))
                    c.step("vector", lambda e: e.tensor_scalar(
                        a1t[:], e1f[:], rec, None, ALU.mult))
                    c.step("tensor", lambda e: e.transpose(
                        pT[0:64, :], a1t[:], idb[:]))
                    c.step("vector", lambda e, i=i: e.tensor_copy(
                        a1m[:, i * 128:(i + 1) * 128], pT[0:64, :]))

                for j in range(4):
                    ksl = kg[half:half + 64,
                             qc * SEQ + j * 512: qc * SEQ + j * 512 + 512]
                    c.step("tensor", lambda e, ksl=ksl, ql_h=ql_h: e.matmul(
                        pA[0:64, :], ql_h, ksl, start=True, stop=True))
                    c.step("vector", lambda e, j=j: e.tensor_copy(
                        s3f[:, j * 512:(j + 1) * 512], pA[0:64, :]))
                c.step("vector", lambda e: e.reduce_max(
                    neg[0:64, :], s3f[:], axis=AX, negate=True))
                c.step("scalar", lambda e: e.activation(
                    a3f[:], s3f[:], AF.Exp, bias=neg[0:64, :],
                    accum_out=ssum[0:64, :]))
                c.step("vector", lambda e: e.reciprocal(rec[0:64, :],
                                                        ssum[0:64, :]))
                c.step("vector", lambda e: e.tensor_scalar(
                    a3b[:], a3f[:], rec[0:64, :], None, ALU.mult))
                for i in range(16):
                    c.step("tensor", lambda e, i=i: e.transpose(
                        pT[:, 0:64], a3b[:, i * 128:(i + 1) * 128], idb64))
                    c.step("vector", lambda e, i=i: e.tensor_copy(
                        a3T[:, i * 64:(i + 1) * 64], pT[:, 0:64]))

                def mmav(e, hh=hh):
                    for i in range(16):
                        ins = e.matmul(
                            pE64, a3T[:, i * 64:(i + 1) * 64],
                            vg[:, i * 512 + hh * 64: i * 512 + hh * 64 + 64],
                            start=(i == 0), stop=(i == 15))
                    return ins
                c.step("tensor", mmav)
                c.step("vector", lambda e: e.tensor_copy(avb[:], pE64))

                c.step("tensor", lambda e: e.matmul(
                    pE64, VTb[:], avb[:], start=True, stop=True))
                c.step("vector", lambda e: e.tensor_copy(zb[:], pE64))
                if dbg is not None and b == 0 and hg == 0 and hh == 0:
                    c.step("sync", lambda e: e.dma_start(dbg["avb"], avb[:]), dma=True)
                    c.step("sync", lambda e: e.dma_start(dbg["zb"], zb[:]), dma=True)
                    c.step("sync", lambda e: e.dma_start(dbg["a1m"], a1m[:]), dma=True)
                    c.step("sync", lambda e: e.dma_start(dbg["a3b"], a3b[:]), dma=True)
                gc = hg * 4 + qc
                for j in range(4):
                    c.step("tensor", lambda e, j=j: e.matmul(
                        pA[0:64, :], zb[:],
                        a1m[:, j * 512:(j + 1) * 512], start=True, stop=True))
                    c.step("vector", lambda e, j=j, gc=gc, half=half:
                           e.tensor_copy(
                               ohd[half:half + 64,
                                   gc * SEQ + j * 512: gc * SEQ + j * 512 + 512],
                               pA[0:64, :]))

        for fc in range(8):
            for ttj in range(4):
                def mmo(e, fc=fc, ttj=ttj):
                    for k in range(8):
                        i = e.matmul(
                            pA[:],
                            woutb[:, k * D + fc * 128: k * D + fc * 128 + 128],
                            ohd[:, k * SEQ + ttj * 512: k * SEQ + ttj * 512 + 512],
                            start=(k == 0), stop=(k == 7))
                    return i
                c.step("tensor", mmo)
                c.step("scalar", lambda e, fc=fc: e.activation(
                    s0t[:], pA[:], AF.Identity, bias=boutT[:, fc:fc + 1]))
                c.step("sync", lambda e, fc=fc, ttj=ttj, tb=tb: e.dma_start(
                    s0_d[fc * 128:(fc + 1) * 128,
                         tb + ttj * 512: tb + ttj * 512 + 512], s0t[:]),
                    dma=True)
    return c


def build_l0attn(debug=False):
    nc = bass.Bass("TRN2", target_bir_lowering=False)
    xn_d = nc.dram_tensor("xn", [D, T], BF16, kind="ExternalInput").ap()
    wqkv_d = nc.dram_tensor("wqkv", [D, 3 * D], F32, kind="ExternalInput").ap()
    bqkvm_d = nc.dram_tensor("bqkvm", [24, 128], F32, kind="ExternalInput").ap()
    wout_d = nc.dram_tensor("wout0", [D, D], F32, kind="ExternalInput").ap()
    boutm_d = nc.dram_tensor("boutm", [8, 128], F32, kind="ExternalInput").ap()
    s0_d = nc.dram_tensor("s0", [D, T], F32, kind="ExternalOutput").ap()
    dbg = None
    if debug:
        dbg = {}
        for nm, shp, dt in [("qg", [128, 4 * SEQ], BF16), ("kg", [128, 4 * SEQ], BF16),
                            ("vg", [128, 16 * 512], BF16), ("qlb", [128, 256], BF16),
                            ("klb", [128, 256], BF16), ("a2b", [64, 64], F32),
                            ("Vb", [64, 64], F32), ("avb", [64, 64], F32),
                            ("zb", [64, 64], BF16), ("a1m", [64, SEQ], BF16),
                            ("a3b", [64, SEQ], BF16)]:
            dbg[nm] = nc.dram_tensor("dbg_" + nm, shp, dt, kind="ExternalOutput").ap()
    ctx = ExitStack()
    with ctx:
        p = Prog(nc, ctx)
        ic = Chain(p, ctx, "sic", init=True)
        c = Chain(p, ctx, "sc")
        cp = ConstPool(p, ctx, [1e-5, 0.125], ic)
        idf, idb = ini_identities(p, ctx, ic)
        ptmp = ctx.enter_context(nc.psum_tensor("ptmp", [128, 128], F32))
        emit_l0attn(p, ctx, ctx, ic, c, cp, idf, idb, ptmp,
                    xn_d, wqkv_d, bqkvm_d, wout_d, boutm_d, s0_d, dbg=dbg)
        p.do("sync", lambda e: e.wait_ge(c.sem.h, c.sem.n))
        with nc.Block() as blk0:
            p.play_init(blk0)
        with nc.Block() as blk:
            p.play(blk)
    return nc





def emit_l0moe(p, ctx, tctx, ic, c, cp, idf, idb, ptmp,
               x_d, xn_d, xnf_d, s0_d, rwT_d, rb_d, ewT_d, ebm_d,
               x1_d, xn1_d, xn1f_d, tag="m"):
    nc = p.nc

    rwf = ctx.enter_context(nc.sbuf_tensor(f"{tag}rwf", [128, 8 * NE], F32,
                                           side="right"))
    for k in range(8):
        ic.step("sync", lambda e, k=k: e.dma_start(
            rwf[:, k * NE:(k + 1) * NE], rwT_d[k * 128:(k + 1) * 128, :]),
            dma=True)

    sb = lambda n, s, dt: tctx.enter_context(nc.sbuf_tensor(n, s, dt))
    sbr = lambda n, s, dt: ctx.enter_context(
        nc.sbuf_tensor(n, s, dt, side="right"))
    rbb = sbr(f"{tag}rbb", [128, NE], F32)
    ic.step("sync", lambda e: e.dma_start(
        rbb[:], rb_d[0:1, :].partition_broadcast(128)[:, 0, :]), dma=True)
    ones512 = sbr(f"{tag}ones512", [1, 512], BF16)
    onesc = sbr(f"{tag}onesc", [128, 1], BF16)
    ic.step("gpsimd", lambda e: e.memset(ones512[:], 1.0))
    ic.step("gpsimd", lambda e: e.memset(onesc[:], 1.0))

    xnb = sb(f"{tag}xnb", [128, 8 * 512], BF16)
    xnfb = sb(f"{tag}xnfb", [128, 8 * 512], F32)
    ebstg = sb(f"{tag}ebstg", [1, D], F32)
    ebe = sb(f"{tag}ebe", [1, D], BF16)
    yacc = sb(f"{tag}yacc", [128, 8 * SEQ], F32)
    ewb = sb(f"{tag}ewb", [128, 8 * D], BF16)
    wstg = sb(f"{tag}wstg", [128, D], F32)
    lgf = sb(f"{tag}lgf", [128, NE], F32)
    exf = sb(f"{tag}exf", [128, NE], F32)
    iseq = sb(f"{tag}iseq", [128, NE], F32)
    maskt = sb(f"{tag}maskt", [128, NE], F32)
    cols = sb(f"{tag}cols", [128, 4], F32)
    maskb = sb(f"{tag}maskb", [128, 512], F32)
    tmpf = sb(f"{tag}tmpf", [128, 512], F32)
    xband = sb(f"{tag}xband", [128, 8 * 512], F32)
    sband = sb(f"{tag}sband", [128, 8 * 512], F32)
    gtile = sb(f"{tag}gtile", [128, 512], F32)
    x1bb = sb(f"{tag}x1bb", [128, 8 * 512], BF16)
    xo = sb(f"{tag}xo", [128, 8 * 512], BF16)
    sqb = sb(f"{tag}sqb", [128, 512], BF16)
    stat = sb(f"{tag}stat", [1, 7 * 512], F32)
    rsb = sb(f"{tag}rsb", [128, 512], F32)
    mbb = sb(f"{tag}mbb", [128, 512], F32)

    pA = tctx.enter_context(nc.psum_tensor(f"{tag}pA", [128, 512], F32))
    pS = tctx.enter_context(nc.psum_tensor(f"{tag}pS", [1, 512], F32))
    pQ = tctx.enter_context(nc.psum_tensor(f"{tag}pQ", [1, 512], F32))
    cem_d = nc.dram_tensor(f"{tag}cem", [NE, T], F32, kind="Internal").ap()
    rows_d = nc.dram_tensor(f"{tag}rows", [2, T], F32, kind="Internal").ap()

    neg = cols[:, 0:1]
    mx = cols[:, 1:2]
    ssum = cols[:, 2:3]
    rec = cols[:, 3:4]

    for tt in range(8):
        tcol = tt * 512
        c.step("sync", lambda e, tcol=tcol: e.dma_start(
            xnfb[:].rearrange("p (c t) -> p c t", t=512),
            fm(xnf_d, 0, tcol, 512, 8)), dma=True)
        for ts in range(4):
            def mmr(e, ts=ts):
                for k in range(8):
                    i = e.matmul(pA[:, 0:NE],
                                 xnfb[:, k * 512 + ts * 128: k * 512 + ts * 128 + 128],
                                 rwf[:, k * NE:(k + 1) * NE],
                                 start=(k == 0), stop=(k == 7))
                return i
            c.step("tensor", mmr)
            c.step("vector", lambda e: e.tensor_tensor(
                lgf[:], pA[:, 0:NE], rbb[:], ALU.add))
            c.step("vector", lambda e: e.reduce_max(mx, lgf[:], axis=AX))
            c.step("vector", lambda e: e.reduce_max(neg, lgf[:], axis=AX,
                                                    negate=True))
            c.step("scalar", lambda e: e.activation(
                exf[:], lgf[:], AF.Exp, bias=neg, accum_out=ssum))
            c.step("vector", lambda e: e.reciprocal(rec, ssum))
            c.step("vector", lambda e: e.tensor_scalar(
                iseq[:], lgf[:], mx, 0.0, ALU.subtract, ALU.is_ge))
            c.step("vector", lambda e: e.tensor_scalar(
                maskt[:], iseq[:], rec, None, ALU.mult))
            c.step("tensor", lambda e: e.transpose(
                ptmp[0:NE, :], maskt[:], idf[:]))
            c.step("vector", lambda e: e.tensor_copy(tmpf[0:NE, 0:128],
                                                     ptmp[0:NE, :]))
            c.step("sync", lambda e, tcol=tcol, ts=ts: e.dma_start(
                cem_d[:, tcol + ts * 128: tcol + ts * 128 + 128],
                tmpf[0:NE, 0:128]), dma=True)

    for half in range(2):
        Tb = half * SEQ
        for e_i in range(NE):
            c.step("sync", lambda e, e_i=e_i: e.dma_start(
                ebstg[:], ebm_d[e_i * 8:(e_i + 1) * 8, :]
                .rearrange("m p -> (m p)").unsqueeze(0)), dma=True)
            c.step("vector", lambda e: e.tensor_copy(ebe[:], ebstg[:]))
            for k in range(8):
                c.step("sync", lambda e, e_i=e_i, k=k: e.dma_start(
                    wstg[:], ewT_d[e_i, k * 128:(k + 1) * 128, :]), dma=True)
                c.step("vector", lambda e, k=k: e.tensor_copy(
                    ch(ewb, D, k), wstg[:]))
            for T_t in range(4):
                tcol = Tb + T_t * 512
                c.step("sync", lambda e, tcol=tcol: e.dma_start(
                    xnb[:].rearrange("p (c t) -> p c t", t=512),
                    fm(xn_d, 0, tcol, 512, 8)), dma=True)
                c.step("sync", lambda e, e_i=e_i, tcol=tcol: e.dma_start(
                    maskb[:],
                    cem_d[e_i:e_i + 1, tcol:tcol + 512]
                    .partition_broadcast(128)[:, 0, :]), dma=True)
                for fc in range(8):
                    def mme(e, e_i=e_i, fc=fc):
                        for k in range(8):
                            e.matmul(pA[:],
                                     ewb[:, k * D + fc * 128: k * D + fc * 128 + 128],
                                     ch(xnb, 512, k), start=(k == 0), stop=False)
                        return e.matmul(
                            pA[:], ebe[0:1, fc * 128: fc * 128 + 128],
                            ones512[0:1, :], start=False, stop=True)
                    c.step("tensor", mme)
                    ydst = lambda fc=fc, T_t=T_t: yacc[:, fc * SEQ + T_t * 512:
                                                       fc * SEQ + T_t * 512 + 512]
                    if e_i == 0:
                        c.step("vector", lambda e, ydst=ydst: e.tensor_tensor(
                            ydst(), pA[:], maskb[:], ALU.mult))
                    else:
                        c.step("vector", lambda e: e.tensor_tensor(
                            tmpf[:], pA[:], maskb[:], ALU.mult))
                        c.step("vector", lambda e, ydst=ydst: e.tensor_tensor(
                            ydst(), ydst(), tmpf[:], ALU.add))

        for T_t in range(4):
            tcol = Tb + T_t * 512
            c.step("sync", lambda e, tcol=tcol: e.dma_start(
                xband[:].rearrange("p (c t) -> p c t", t=512),
                fm(x_d, 0, tcol, 512, 8)), dma=True)
            c.step("sync", lambda e, tcol=tcol: e.dma_start(
                sband[:].rearrange("p (c t) -> p c t", t=512),
                fm(s0_d, 0, tcol, 512, 8)), dma=True)
            for fc in range(8):
                ysl = yacc[:, fc * SEQ + T_t * 512: fc * SEQ + T_t * 512 + 512]
                c.step("scalar", lambda e, ysl=ysl: e.activation(
                    gtile[:], ysl, AF.Silu))
                c.step("vector", lambda e, fc=fc: e.tensor_tensor(
                    gtile[:], gtile[:], ch(sband, 512, fc), ALU.mult))
                c.step("vector", lambda e, fc=fc: e.tensor_tensor(
                    ch(xband, 512, fc), ch(xband, 512, fc), gtile[:], ALU.add))
            c.step("sync", lambda e, tcol=tcol: e.dma_start(
                fm(x1_d, 0, tcol, 512, 8),
                xband[:].rearrange("p (c t) -> p c t", t=512)), dma=True)
            c.step("vector", lambda e: e.tensor_copy(x1bb[:], xband[:]))
            for fc in range(8):
                c.step("scalar", lambda e, fc=fc: e.activation(
                    sqb[:], ch(xband, 512, fc), AF.Square))
                def mmln(e, fc=fc):
                    e.matmul(pS[:], onesc[:], ch(x1bb, 512, fc),
                             start=(fc == 0), stop=(fc == 7))
                    return e.matmul(pQ[:], onesc[:], sqb[:],
                                    start=(fc == 0), stop=(fc == 7))
                c.step("tensor", mmln)
            mean = stat[0:1, 0:512]
            ex2 = stat[0:1, 512:1024]
            var = stat[0:1, 1024:1536]
            stdv = stat[0:1, 1536:2048]
            rs = stat[0:1, 2048:2560]
            mb = stat[0:1, 2560:3072]
            tmp1 = stat[0:1, 3072:3584]

            def mkst(e):
                e.tensor_scalar(mean, pS[:], 1.0 / D, None, ALU.mult)
                e.tensor_scalar(ex2, pQ[:], 1.0 / D, None, ALU.mult)
                e.tensor_tensor(tmp1, mean, mean, ALU.mult)
                return e.tensor_tensor(var, ex2, tmp1, ALU.subtract)
            c.step("vector", mkst)
            c.step("scalar", lambda e: e.activation(
                stdv, var, AF.Sqrt, bias=cp.col(1e-5, 1)))

            def mkrs(e):
                e.reciprocal(rs, stdv)
                return e.scalar_tensor_tensor(mb, mean, -1.0, rs,
                                              ALU.mult, ALU.mult)
            c.step("vector", mkrs)
            c.step("sync", lambda e, tcol=tcol: e.dma_start(
                rows_d[0:1, tcol:tcol + 512], rs), dma=True)
            c.step("sync", lambda e, tcol=tcol: e.dma_start(
                rows_d[1:2, tcol:tcol + 512], mb), dma=True)
            c.step("sync", lambda e, tcol=tcol: e.dma_start(
                rsb[:], rows_d[0:1, tcol:tcol + 512]
                .partition_broadcast(128)[:, 0, :]), dma=True)
            c.step("sync", lambda e, tcol=tcol: e.dma_start(
                mbb[:], rows_d[1:2, tcol:tcol + 512]
                .partition_broadcast(128)[:, 0, :]), dma=True)
            for fc in range(8):
                c.step("vector", lambda e, fc=fc: e.tensor_tensor(
                    ch(xband, 512, fc), ch(xband, 512, fc), rsb[:], ALU.mult))
            for fc in range(8):
                c.step("vector", lambda e, fc=fc: e.tensor_tensor(
                    ch(xband, 512, fc), ch(xband, 512, fc), mbb[:], ALU.add))
            c.step("vector", lambda e: e.tensor_copy(xo[:], xband[:]))
            c.step("sync", lambda e, tcol=tcol: e.dma_start(
                fm(xn1f_d, 0, tcol, 512, 8),
                xband[:].rearrange("p (c t) -> p c t", t=512)), dma=True)
            c.step("sync", lambda e, tcol=tcol: e.dma_start(
                fm(xn1_d, 0, tcol, 512, 8),
                xo[:].rearrange("p (c t) -> p c t", t=512)), dma=True)
    return c


def build_l0moe():
    nc = bass.Bass("TRN2", target_bir_lowering=False)
    x_d = nc.dram_tensor("x", [D, T], F32, kind="ExternalInput").ap()
    xn_d = nc.dram_tensor("xn", [D, T], BF16, kind="ExternalInput").ap()
    xnf_d = nc.dram_tensor("xnf", [D, T], F32, kind="ExternalInput").ap()
    s0_d = nc.dram_tensor("s0", [D, T], F32, kind="ExternalInput").ap()
    rwT_d = nc.dram_tensor("rwT", [D, NE], F32, kind="ExternalInput").ap()
    rb_d = nc.dram_tensor("rb", [1, NE], F32, kind="ExternalInput").ap()
    ewT_d = nc.dram_tensor("ewT", [NE, D, D], F32, kind="ExternalInput").ap()
    ebm_d = nc.dram_tensor("ebm", [64, 128], F32, kind="ExternalInput").ap()
    x1_d = nc.dram_tensor("x1", [D, T], F32, kind="ExternalOutput").ap()
    xn1_d = nc.dram_tensor("xn1", [D, T], BF16, kind="ExternalOutput").ap()
    xn1f_d = nc.dram_tensor("xn1f", [D, T], F32, kind="ExternalOutput").ap()
    ctx = ExitStack()
    with ctx:
        p = Prog(nc, ctx)
        ic = Chain(p, ctx, "sic", init=True)
        c = Chain(p, ctx, "sc")
        cp = ConstPool(p, ctx, [1e-5, 0.125], ic)
        idf, idb = ini_identities(p, ctx, ic)
        ptmp = ctx.enter_context(nc.psum_tensor("ptmp", [128, 128], F32))
        emit_l0moe(p, ctx, ctx, ic, c, cp, idf, idb, ptmp,
                   x_d, xn_d, xnf_d, s0_d, rwT_d, rb_d, ewT_d, ebm_d,
                   x1_d, xn1_d, xn1f_d)
        p.do("sync", lambda e: e.wait_ge(c.sem.h, c.sem.n))
        with nc.Block() as blk0:
            p.play_init(blk0)
        with nc.Block() as blk:
            p.play(blk)
    return nc





def emit_l1pre(p, ctx, tctx, ic, c, cp, idf, idb, ptmp,
               xn1_d, winT_d, wconv_d, bconvm_d, wxT_d, wdtT_d, bdtm_d,
               res_d, xsp_d, xspb_d, bfull_d, dt_d, tag="d"):
    nc = p.nc
    bconvT = ini_biasT(p, ctx, ic, bconvm_d, 16, ptmp, idf, f"{tag}bc")
    bdtT = ini_biasT(p, ctx, ic, bdtm_d, 16, ptmp, idf, f"{tag}bd")
    wxb = ini_weight_bf16(p, ctx, ic, wxT_d, DI, 32, f"{tag}wx",
                          side="right")

    sb = lambda n, s, dt: tctx.enter_context(nc.sbuf_tensor(n, s, dt))
    wcv = ctx.enter_context(nc.sbuf_tensor(f"{tag}wcv", [128, 16 * 4], F32,
                                           side="right"))
    for ci in range(16):
        ic.step("sync", lambda e, ci=ci: e.dma_start(
            wcv[:, ci * 4:(ci + 1) * 4],
            wconv_d[ci * 128:(ci + 1) * 128, :]), dma=True)

    wband = sb(f"{tag}wband", [128, 32768], BF16)
    wstg = sb(f"{tag}wstg", [128, 4096], F32)
    xn1b = sb(f"{tag}xn1b", [128, 8 * 512], BF16)
    ot = sb(f"{tag}ot", [128, 512], F32)
    xpad = sb(f"{tag}xpad", [128, SEQ + 3], F32)
    cvt = sb(f"{tag}cvt", [128, SEQ], F32)
    xst = sb(f"{tag}xst", [128, SEQ], F32)
    xsb = sb(f"{tag}xsb", [128, SEQ], BF16)
    spband = sb(f"{tag}spband", [128, 16 * 512], BF16)
    sp1 = sb(f"{tag}sp1", [128, 512], F32)
    sp2 = sb(f"{tag}sp2", [128, 512], F32)
    bx = sb(f"{tag}bx", [32, 512], F32)

    pA = tctx.enter_context(nc.psum_tensor(f"{tag}pA", [128, 512], F32))
    pX = tctx.enter_context(nc.psum_tensor(f"{tag}pX", [32, 512], F32))
    xsr_d = nc.dram_tensor(f"{tag}xsr", [DI, T], F32, kind="Internal").ap()

    for k in range(8):
        c.step("sync", lambda e, k=k: e.dma_start(
            wstg[:], winT_d[k * 128:(k + 1) * 128, :]), dma=True)
        c.step("vector", lambda e, k=k: e.tensor_copy(
            ch(wband, 4096, k), wstg[:]))
    for tt in range(8):
        tcol = tt * 512
        c.step("sync", lambda e, tcol=tcol: e.dma_start(
            xn1b[:].rearrange("p (c t) -> p c t", t=512),
            fm(xn1_d, 0, tcol, 512, 8)), dma=True)
        for fc in range(32):
            def mmw(e, fc=fc):
                for k in range(8):
                    i = e.matmul(pA[:],
                                 wband[:, k * 4096 + fc * 128:
                                       k * 4096 + fc * 128 + 128],
                                 ch(xn1b, 512, k), start=(k == 0), stop=(k == 7))
                return i
            c.step("tensor", mmw)
            c.step("vector", lambda e: e.tensor_copy(ot[:], pA[:]))
            if fc < 16:
                c.step("sync", lambda e, fc=fc, tcol=tcol: e.dma_start(
                    xsr_d[fc * 128:(fc + 1) * 128, tcol:tcol + 512], ot[:]),
                    dma=True)
            else:
                c.step("sync", lambda e, fc=fc, tcol=tcol: e.dma_start(
                    res_d[(fc - 16) * 128:(fc - 15) * 128, tcol:tcol + 512],
                    ot[:]), dma=True)

    for ci in range(16):
        for b in range(BPC):
            tb = b * SEQ
            c.step("gpsimd", lambda e: e.memset(xpad[:, 0:3], 0.0))
            c.step("sync", lambda e, ci=ci, tb=tb: e.dma_start(
                xpad[:, 3:SEQ + 3],
                xsr_d[ci * 128:(ci + 1) * 128, tb:tb + SEQ]), dma=True)

            def cv(e, ci=ci):
                e.tensor_scalar(cvt[:], xpad[:, 0:SEQ],
                                wcv[:, ci * 4:ci * 4 + 1], None, ALU.mult)
                e.scalar_tensor_tensor(cvt[:], xpad[:, 1:SEQ + 1],
                                       wcv[:, ci * 4 + 1:ci * 4 + 2], cvt[:],
                                       ALU.mult, ALU.add)
                e.scalar_tensor_tensor(cvt[:], xpad[:, 2:SEQ + 2],
                                       wcv[:, ci * 4 + 2:ci * 4 + 3], cvt[:],
                                       ALU.mult, ALU.add)
                return e.scalar_tensor_tensor(
                    cvt[:], xpad[:, 3:SEQ + 3],
                    wcv[:, ci * 4 + 3:ci * 4 + 4], cvt[:], ALU.mult, ALU.add)
            c.step("vector", cv)
            c.step("scalar", lambda e, ci=ci: e.activation(
                xst[:], cvt[:], AF.Silu, bias=bconvT[:, ci:ci + 1]))
            c.step("vector", lambda e: e.tensor_copy(xsb[:], xst[:]))
            c.step("sync", lambda e, ci=ci, tb=tb: e.dma_start(
                xsp_d[ci * 128:(ci + 1) * 128, tb:tb + SEQ], xst[:]), dma=True)
            c.step("sync", lambda e, ci=ci, tb=tb: e.dma_start(
                xspb_d[ci * 128:(ci + 1) * 128, tb:tb + SEQ], xsb[:]), dma=True)

    for k in range(16):
        c.step("sync", lambda e, k=k: e.dma_start(
            wstg[:, 0:2048], wdtT_d[k * 128:(k + 1) * 128, :]), dma=True)
        c.step("vector", lambda e, k=k: e.tensor_copy(
            ch(wband, 2048, k), wstg[:, 0:2048]))

    for tt in range(8):
        tcol = tt * 512
        c.step("sync", lambda e, tcol=tcol: e.dma_start(
            spband[:].rearrange("p (c t) -> p c t", t=512),
            fm(xspb_d, 0, tcol, 512, 16)), dma=True)

        def mmx(e):
            for k in range(16):
                i = e.matmul(pX[:], ch(wxb, 32, k), ch(spband, 512, k),
                             start=(k == 0), stop=(k == 15))
            return i
        c.step("tensor", mmx)
        c.step("vector", lambda e: e.tensor_copy(bx[:], pX[:]))
        c.step("sync", lambda e, tcol=tcol: e.dma_start(
            bfull_d[:, tcol:tcol + 512], bx[:]), dma=True)

        for fc in range(16):
            def mmd(e, fc=fc):
                for k in range(16):
                    i = e.matmul(pA[:],
                                 wband[:, k * 2048 + fc * 128:
                                       k * 2048 + fc * 128 + 128],
                                 ch(spband, 512, k), start=(k == 0),
                                 stop=(k == 15))
                return i
            c.step("tensor", mmd)
            c.step("scalar", lambda e, fc=fc: e.activation(
                sp1[:], pA[:], AF.Abs, bias=bdtT[:, fc:fc + 1]))
            c.step("scalar", lambda e: e.activation(
                sp1[:], sp1[:], AF.Exp, scale=cp.col(-1.0)))
            c.step("scalar", lambda e: e.activation(
                sp1[:], sp1[:], AF.Ln, bias=cp.col(1.0)))
            c.step("scalar", lambda e, fc=fc: e.activation(
                sp2[:], pA[:], AF.Relu, bias=bdtT[:, fc:fc + 1]))
            c.step("vector", lambda e: e.tensor_tensor(
                sp2[:], sp2[:], sp1[:], ALU.add))
            c.step("sync", lambda e, fc=fc, tcol=tcol: e.dma_start(
                dt_d[fc * 128:(fc + 1) * 128, tcol:tcol + 512], sp2[:]),
                dma=True)
    return c


def build_l1pre():
    nc = bass.Bass("TRN2", target_bir_lowering=False)
    xn1_d = nc.dram_tensor("xn1", [D, T], BF16, kind="ExternalInput").ap()
    winT_d = nc.dram_tensor("winT", [D, 2 * DI], F32, kind="ExternalInput").ap()
    wconv_d = nc.dram_tensor("wconv", [DI, 4], F32, kind="ExternalInput").ap()
    bconvm_d = nc.dram_tensor("bconvm", [16, 128], F32, kind="ExternalInput").ap()
    wxT_d = nc.dram_tensor("wxT", [DI, 32], F32, kind="ExternalInput").ap()
    wdtT_d = nc.dram_tensor("wdtT", [DI, DI], F32, kind="ExternalInput").ap()
    bdtm_d = nc.dram_tensor("bdtm", [16, 128], F32, kind="ExternalInput").ap()
    res_d = nc.dram_tensor("res", [DI, T], F32, kind="ExternalOutput").ap()
    xsp_d = nc.dram_tensor("xsp", [DI, T], F32, kind="ExternalOutput").ap()
    xspb_d = nc.dram_tensor("xspb", [DI, T], BF16, kind="ExternalOutput").ap()
    bfull_d = nc.dram_tensor("bfull", [32, T], F32, kind="ExternalOutput").ap()
    dt_d = nc.dram_tensor("dt", [DI, T], F32, kind="ExternalOutput").ap()
    ctx = ExitStack()
    with ctx:
        p = Prog(nc, ctx)
        ic = Chain(p, ctx, "sic", init=True)
        c = Chain(p, ctx, "sc")
        cp = ConstPool(p, ctx, [1e-5, 0.125, -1.0, 1.0], ic)
        idf, idb = ini_identities(p, ctx, ic)
        ptmp = ctx.enter_context(nc.psum_tensor("ptmp", [128, 128], F32))
        emit_l1pre(p, ctx, ctx, ic, c, cp, idf, idb, ptmp,
                   xn1_d, winT_d, wconv_d, bconvm_d, wxT_d, wdtT_d, bdtm_d,
                   res_d, xsp_d, xspb_d, bfull_d, dt_d)
        p.do("sync", lambda e: e.wait_ge(c.sem.h, c.sem.n))
        with nc.Block() as blk0:
            p.play_init(blk0)
        with nc.Block() as blk:
            p.play(blk)
    return nc





def emit_scan(p, ctx, tctx, ic, c, cp, idf, idb, ptmp,
              xsp_d, res_d, bfull_d, dt_d, alog_d, dm_d, zb_d,
              y_dbg_d=None, tag="s"):
    nc = p.nc
    Dt = ini_biasT(p, ctx, ic, dm_d, 16, ptmp, idf, f"{tag}D")

    sb = lambda n, s, dt: tctx.enter_context(nc.sbuf_tensor(n, s, dt))
    Af = ctx.enter_context(nc.sbuf_tensor(f"{tag}Af", [128, 16 * 16], F32,
                                          side="right"))
    with nc.sbuf_tensor(f"{tag}astg", [128, 16], F32, side="right") as astg:
        for ci in range(16):
            ic.step("sync", lambda e, ci=ci: e.dma_start(
                astg[:], alog_d[ci * 128:(ci + 1) * 128, :]), dma=True)
            ic.step("scalar", lambda e, ci=ci: e.activation(
                Af[:, ci * 16:(ci + 1) * 16], astg[:], AF.Exp))
            ic.step("vector", lambda e, ci=ci: e.tensor_scalar(
                Af[:, ci * 16:(ci + 1) * 16], Af[:, ci * 16:(ci + 1) * 16],
                -1.0, None, ALU.mult))

    dtb = sb(f"{tag}dtb", [128, 4 * SEQ], F32)
    xpb = sb(f"{tag}xpb", [128, 4 * SEQ], F32)
    ub = sb(f"{tag}ub", [128, 4 * SEQ], F32)
    yb = sb(f"{tag}yb", [128, 4 * SEQ], F32)
    at = sb(f"{tag}at", [128, SEQ], F32)
    bt = sb(f"{tag}bt", [128, SEQ], F32)
    ht = sb(f"{tag}ht", [128, SEQ], F32)
    Bb = sb(f"{tag}Bb", [128, SEQ], F32)
    rt = sb(f"{tag}rt", [128, SEQ], F32)
    zt = sb(f"{tag}zt", [128, SEQ], BF16)

    for b in range(BPC):
        tb = b * SEQ
        for cg in range(4):
            c.step("sync", lambda e, cg=cg, tb=tb: e.dma_start(
                dtb[:].rearrange("p (c t) -> p c t", t=SEQ),
                fm(dt_d, cg * 4, tb, SEQ, 4)), dma=True)
            c.step("sync", lambda e, cg=cg, tb=tb: e.dma_start(
                xpb[:].rearrange("p (c t) -> p c t", t=SEQ),
                fm(xsp_d, cg * 4, tb, SEQ, 4)), dma=True)
            c.step("vector", lambda e: e.tensor_tensor(
                ub[:], dtb[:], xpb[:], ALU.mult))
            for si in range(DS):
                c.step("sync", lambda e, si=si, tb=tb: e.dma_start(
                    Bb[:], bfull_d[16 + si:17 + si, tb:tb + SEQ]
                    .partition_broadcast(128)[:, 0, :]), dma=True)
                for cc in range(4):
                    ci = cg * 4 + cc
                    c.step("scalar", lambda e, cc=cc, ci=ci, si=si:
                           e.activation(at[:], ch(dtb, SEQ, cc), AF.Exp,
                                        scale=Af[:, ci * 16 + si:
                                                 ci * 16 + si + 1]))
                    c.step("vector", lambda e, cc=cc: e.tensor_tensor(
                        bt[:], ch(ub, SEQ, cc), Bb[:], ALU.mult))
                    c.step("vector", lambda e: e.tensor_tensor_scan(
                        ht[:], at[:], bt[:], 0.0, ALU.mult, ALU.add))
                    if si == 0:
                        c.step("vector", lambda e, cc=cc: e.tensor_copy(
                            ch(yb, SEQ, cc), ht[:]))
                    else:
                        c.step("vector", lambda e, cc=cc: e.tensor_tensor(
                            ch(yb, SEQ, cc), ch(yb, SEQ, cc), ht[:], ALU.add))
            for cc in range(4):
                ci = cg * 4 + cc
                c.step("vector", lambda e, cc=cc, ci=ci:
                       e.scalar_tensor_tensor(
                           at[:], ch(xpb, SEQ, cc), Dt[:, ci:ci + 1],
                           ch(yb, SEQ, cc), ALU.mult, ALU.add))
                if y_dbg_d is not None:
                    c.step("sync", lambda e, ci=ci, tb=tb: e.dma_start(
                        y_dbg_d[ci * 128:(ci + 1) * 128, tb:tb + SEQ], at[:]),
                        dma=True)
                c.step("sync", lambda e, ci=ci, tb=tb: e.dma_start(
                    rt[:], res_d[ci * 128:(ci + 1) * 128, tb:tb + SEQ]),
                    dma=True)
                c.step("scalar", lambda e: e.activation(rt[:], rt[:], AF.Silu))
                c.step("vector", lambda e: e.tensor_tensor(
                    zt[:], at[:], rt[:], ALU.mult))
                c.step("sync", lambda e, ci=ci, tb=tb: e.dma_start(
                    zb_d[ci * 128:(ci + 1) * 128, tb:tb + SEQ], zt[:]),
                    dma=True)
    return c


def build_scan(debug=False):
    nc = bass.Bass("TRN2", target_bir_lowering=False)
    xsp_d = nc.dram_tensor("xsp", [DI, T], F32, kind="ExternalInput").ap()
    res_d = nc.dram_tensor("res", [DI, T], F32, kind="ExternalInput").ap()
    bfull_d = nc.dram_tensor("bfull", [32, T], F32, kind="ExternalInput").ap()
    dt_d = nc.dram_tensor("dt", [DI, T], F32, kind="ExternalInput").ap()
    alog_d = nc.dram_tensor("alog", [DI, 16], F32, kind="ExternalInput").ap()
    dm_d = nc.dram_tensor("dm", [16, 128], F32, kind="ExternalInput").ap()
    zb_d = nc.dram_tensor("zb", [DI, T], BF16, kind="ExternalOutput").ap()
    y_dbg_d = None
    if debug:
        y_dbg_d = nc.dram_tensor("ydbg", [DI, T], F32, kind="ExternalOutput").ap()
    ctx = ExitStack()
    with ctx:
        p = Prog(nc, ctx)
        ic = Chain(p, ctx, "sic", init=True)
        c = Chain(p, ctx, "sc")
        cp = ConstPool(p, ctx, [1e-5], ic)
        idf, idb = ini_identities(p, ctx, ic)
        ptmp = ctx.enter_context(nc.psum_tensor("ptmp", [128, 128], F32))
        emit_scan(p, ctx, ctx, ic, c, cp, idf, idb, ptmp,
                  xsp_d, res_d, bfull_d, dt_d, alog_d, dm_d, zb_d,
                  y_dbg_d=y_dbg_d)
        p.do("sync", lambda e: e.wait_ge(c.sem.h, c.sem.n))
        with nc.Block() as blk0:
            p.play_init(blk0)
        with nc.Block() as blk:
            p.play(blk)
    return nc





def emit_l1moe(p, ctx, tctx, ic, c, cp, idf, idb, ptmp,
               xn1_d, xn1f_d, gw_d, w1_d, b1m_d, w2_d, b2m_d, y1_d, tag="f",
               cem_out=None):
    nc = p.nc
    pre_sem, pre_val = c.sem, c.sem.n
    gwf = ctx.enter_context(nc.sbuf_tensor(f"{tag}gwf", [128, 8 * NE], F32,
                                           side="right"))
    for k in range(8):
        ic.step("sync", lambda e, k=k: e.dma_start(
            gwf[:, k * NE:(k + 1) * NE], gw_d[k * 128:(k + 1) * 128, :]),
            dma=True)
    b1T = []
    for e_i in range(NE):
        b1T.append(ini_biasT(p, ctx, ic, b1m_d[e_i * 32:(e_i + 1) * 32, :],
                             32, ptmp, idf, f"{tag}b1_{e_i}"))

    sb = lambda n, s, dt: tctx.enter_context(nc.sbuf_tensor(n, s, dt))
    ones512 = ctx.enter_context(nc.sbuf_tensor(f"{tag}ones512", [1, 512],
                                               BF16, side="right"))
    ic.step("gpsimd", lambda e: e.memset(ones512[:], 1.0))

    xn1b = sb(f"{tag}xn1b", [128, 8 * 512], BF16)
    w1b = sb(f"{tag}w1b", [128, 8 * FF], BF16)
    w2b = sb(f"{tag}w2b", [128, 32 * D], BF16)
    b2e = sb(f"{tag}b2e", [1, D], BF16)
    hbig = sb(f"{tag}hbig", [128, 8192], F32)
    hband = hbig[:].bitcast(BF16)
    yband = sb(f"{tag}yband", [128, 8 * 512], F32)
    maskb = sb(f"{tag}maskb", [128, 512], F32)
    tmpf = sb(f"{tag}tmpf", [128, 512], F32)
    lgf = sb(f"{tag}lgf", [128, NE], F32)
    lg2 = sb(f"{tag}lg2", [128, NE], F32)
    iseq1 = sb(f"{tag}iseq1", [128, NE], F32)
    iseq2 = sb(f"{tag}iseq2", [128, NE], F32)
    comb = sb(f"{tag}comb", [128, NE], F32)
    cols = sb(f"{tag}cols", [128, 8], F32)

    pA = [tctx.enter_context(nc.psum_tensor(f"{tag}pA{i}", [128, 512], F32))
          for i in range(2)]
    pY = [tctx.enter_context(nc.psum_tensor(f"{tag}pY{i}", [128, 512], F32))
          for i in range(2)]
    cem_d = nc.dram_tensor(f"{tag}cem", [NE, T], F32, kind="Internal").ap()

    mx1 = cols[:, 0:1]
    neg1 = cols[:, 1:2]
    mx2 = cols[:, 2:3]
    dd = cols[:, 3:4]
    den = cols[:, 4:5]
    c1 = cols[:, 5:6]
    c2 = cols[:, 6:7]

    xn1fb = hbig[:]
    for tt in range(8):
        tcol = tt * 512
        c.step("sync", lambda e, tcol=tcol: e.dma_start(
            xn1fb[:, 0:4096].rearrange("p (c t) -> p c t", t=512),
            fm(xn1f_d, 0, tcol, 512, 8)), dma=True)
        for ts in range(4):
            def mmg(e, ts=ts):
                for k in range(8):
                    i = e.matmul(pA[0][:, 0:NE],
                                 xn1fb[:, k * 512 + ts * 128:
                                       k * 512 + ts * 128 + 128],
                                 gwf[:, k * NE:(k + 1) * NE],
                                 start=(k == 0), stop=(k == 7))
                return i
            c.step("tensor", mmg)
            c.step("vector", lambda e: e.tensor_copy(lgf[:], pA[0][:, 0:NE]))
            c.step("vector", lambda e: e.reduce_max(mx1, lgf[:], axis=AX))
            c.step("vector", lambda e: e.reduce_max(neg1, lgf[:], axis=AX,
                                                    negate=True))
            c.step("vector", lambda e: e.tensor_scalar(
                iseq1[:], lgf[:], mx1, 0.0, ALU.subtract, ALU.is_ge))
            c.step("vector", lambda e: e.scalar_tensor_tensor(
                lg2[:], iseq1[:], -1e30, lgf[:], ALU.mult, ALU.add))
            c.step("vector", lambda e: e.reduce_max(mx2, lg2[:], axis=AX))
            c.step("scalar", lambda e: e.activation(
                dd, mx2, AF.Exp, bias=neg1))
            c.step("vector", lambda e: e.tensor_scalar(
                den, dd, 1.0, None, ALU.add))
            c.step("vector", lambda e: e.reciprocal(c1, den))
            c.step("vector", lambda e: e.tensor_tensor(c2, dd, c1, ALU.mult))
            c.step("vector", lambda e: e.tensor_scalar(
                iseq2[:], lg2[:], mx2, 0.0, ALU.subtract, ALU.is_ge))
            c.step("vector", lambda e: e.tensor_scalar(
                iseq2[:], iseq2[:], c2, None, ALU.mult))
            c.step("vector", lambda e: e.tensor_scalar(
                comb[:], iseq1[:], c1, None, ALU.mult))
            c.step("vector", lambda e: e.tensor_tensor(
                comb[:], comb[:], iseq2[:], ALU.add))
            c.step("tensor", lambda e: e.transpose(
                ptmp[0:NE, :], comb[:], idf[:]))
            c.step("vector", lambda e: e.tensor_copy(tmpf[0:NE, 0:128],
                                                     ptmp[0:NE, :]))
            c.step("sync", lambda e, tcol=tcol, ts=ts: e.dma_start(
                cem_d[:, tcol + ts * 128: tcol + ts * 128 + 128],
                tmpf[0:NE, 0:128]), dma=True)
    if cem_out is not None:
        for rr in range(NE):
            c.step("sync", lambda e, rr=rr: e.dma_start(
                cem_out[rr:rr + 1, :], cem_d[rr:rr + 1, :]), dma=True)
    gate_end = c.sem.n
    gate_sem = c.sem

    s_x = Sem(ctx, nc, f"{tag}sx")
    s_mk = Sem(ctx, nc, f"{tag}smk")
    s_yl = Sem(ctx, nc, f"{tag}syl")
    s_mh = Sem(ctx, nc, f"{tag}smh")
    s_g = Sem(ctx, nc, f"{tag}sg")
    s_my = Sem(ctx, nc, f"{tag}smy")
    s_tt = Sem(ctx, nc, f"{tag}stt")
    s_st = Sem(ctx, nc, f"{tag}sst")
    s_wc = Sem(ctx, nc, f"{tag}swc")

    for e_i in range(NE):
        w_wait = ([W_(s_mh, None), W_(s_my, None)] if e_i > 0
                  else [W_(pre_sem, pre_val)])

        def wload1(e, e_i=e_i):
            return e.dma_start(
                w1b[:].rearrange("p (c n) -> p c n", n=FF),
                w1_d[e_i].rearrange("(c p) n -> p c n", p=128))
        p.do("gpsimd", wload1, inc=(s_wc, 16), wait=w_wait)
        for hf in range(4):
            def wload2(e, e_i=e_i, hf=hf):
                return e.dma_start(
                    w2b[:, hf * 8 * D:(hf + 1) * 8 * D]
                    .rearrange("p (c n) -> p c n", n=D),
                    w2_d[e_i, hf * 8 * 128:(hf + 1) * 8 * 128, :]
                    .rearrange("(c p) n -> p c n", p=128))
            p.do("gpsimd", wload2, inc=(s_wc, 16),
                 wait=w_wait if hf == 0 else ())
        p.do("gpsimd", lambda e, e_i=e_i: e.dma_start(
            b2e[:], b2m_d[e_i * 8:(e_i + 1) * 8, :]
            .rearrange("m p -> (m p)").unsqueeze(0)), inc=(s_wc, 16),
            wait=w_wait)
        wc_e = s_wc.n

        for T_t in range(8):
            tcol = T_t * 512
            xw = [W_(s_mh, None)]
            if e_i == 0 and T_t == 0:
                xw.append(W_(pre_sem, pre_val))
            p.do("sync", lambda e, tcol=tcol: e.dma_start(
                xn1b[:].rearrange("p (c t) -> p c t", t=512),
                fm(xn1_d, 0, tcol, 512, 8)),
                inc=(s_x, 16), wait=xw)
            p.do("sync", lambda e, e_i=e_i, tcol=tcol: e.dma_start(
                maskb[:],
                cem_d[e_i:e_i + 1, tcol:tcol + 512]
                .partition_broadcast(128)[:, 0, :]),
                inc=(s_mk, 16),
                wait=[W_(gate_sem, gate_end), W_(s_tt, None)])
            if e_i > 0:
                p.do("sync", lambda e, tcol=tcol: e.dma_start(
                    yband[:].rearrange("p (c t) -> p c t", t=512),
                    fm(y1_d, 0, tcol, 512, 8)),
                    inc=(s_yl, 16), wait=[W_(s_st, None), W_(s_tt, None)])
            x_t = s_x.n
            mk_t = s_mk.n
            yl_t = s_yl.n
            my_prev = s_my.n
            g_base = s_g.n
            mh_base = s_mh.n
            first = (e_i == 0 and T_t == 0)
            for hc in range(32):
                tw = [W_(s_wc, wc_e), W_(s_x, x_t),
                      W_(s_g, g_base + hc - 1)]
                if first and hc < 2:
                    tw.append(W_(gate_sem, gate_end))

                def mmh(e, hc=hc):
                    for k in range(8):
                        i = e.matmul(pA[hc % 2][:],
                                     w1b[:, k * FF + hc * 128:
                                         k * FF + hc * 128 + 128],
                                     ch(xn1b, 512, k), start=(k == 0),
                                     stop=(k == 7))
                    return i
                p.do("tensor", mmh, inc=(s_mh, 1), wait=tw)
                gw_ = [W_(s_mh, mh_base + hc + 1)]
                if hc == 0:
                    gw_.append(W_(s_my, my_prev))
                    if first:
                        gw_.append(W_(gate_sem, gate_end))
                p.do("scalar", lambda e, hc=hc, e_i=e_i: e.activation(
                    hband[:, hc * 512:(hc + 1) * 512], pA[hc % 2][:], AF.Gelu,
                    bias=b1T[e_i][:, hc:hc + 1]), inc=(s_g, 1), wait=gw_)
            g_t = s_g.n
            my_base = s_my.n
            tt_base = s_tt.n
            for oc in range(8):
                tw = [W_(s_g, g_t), W_(s_tt, tt_base + oc - 1)]

                def mmy(e, oc=oc):
                    for k in range(32):
                        e.matmul(pY[oc % 2][:],
                                 w2b[:, k * D + oc * 128: k * D + oc * 128 + 128],
                                 hband[:, k * 512:(k + 1) * 512],
                                 start=(k == 0), stop=False)
                    return e.matmul(
                        pY[oc % 2][:], b2e[0:1, oc * 128: oc * 128 + 128],
                        ones512[0:1, :], start=False, stop=True)
                p.do("tensor", mmy, inc=(s_my, 1), wait=tw)
                vw = [W_(s_my, my_base + oc + 1), W_(s_mk, mk_t)]
                if e_i > 0:
                    vw.append(W_(s_yl, yl_t))
                else:
                    vw.append(W_(s_st, None))
                if e_i == 0:
                    p.do("vector", lambda e, oc=oc: e.tensor_tensor(
                        ch(yband, 512, oc), pY[oc % 2][:], maskb[:],
                        ALU.mult), inc=(s_tt, 1), wait=vw)
                else:
                    p.do("vector", lambda e, oc=oc: e.tensor_tensor(
                        tmpf[:], pY[oc % 2][:], maskb[:], ALU.mult), wait=vw)
                    p.do("vector", lambda e, oc=oc: e.tensor_tensor(
                        ch(yband, 512, oc), ch(yband, 512, oc), tmpf[:],
                        ALU.add), inc=(s_tt, 1))
            p.do("sync", lambda e, tcol=tcol: e.dma_start(
                fm(y1_d, 0, tcol, 512, 8),
                yband[:].rearrange("p (c t) -> p c t", t=512)),
                inc=(s_st, 16), wait=[W_(s_tt, None)])

    c.step("sync", lambda e: e.nop(), wait=[W_(s_st, None)])
    return c


def build_l1moe():
    nc = bass.Bass("TRN2", target_bir_lowering=False)
    xn1_d = nc.dram_tensor("xn1", [D, T], BF16, kind="ExternalInput").ap()
    xn1f_d = nc.dram_tensor("xn1f", [D, T], F32, kind="ExternalInput").ap()
    gw_d = nc.dram_tensor("gw", [D, NE], F32, kind="ExternalInput").ap()
    w1_d = nc.dram_tensor("w1", [NE, D, FF], F32, kind="ExternalInput").ap()
    b1m_d = nc.dram_tensor("b1m", [256, 128], F32, kind="ExternalInput").ap()
    w2_d = nc.dram_tensor("w2", [NE, FF, D], F32, kind="ExternalInput").ap()
    b2m_d = nc.dram_tensor("b2m", [64, 128], F32, kind="ExternalInput").ap()
    y1_d = nc.dram_tensor("y1", [D, T], F32, kind="ExternalOutput").ap()
    cemo_d = nc.dram_tensor("cemo", [NE, T], F32, kind="ExternalOutput").ap()
    ctx = ExitStack()
    with ctx:
        p = Prog(nc, ctx)
        ic = Chain(p, ctx, "sic", init=True)
        c = Chain(p, ctx, "sc")
        cp = ConstPool(p, ctx, [1e-5], ic)
        idf, idb = ini_identities(p, ctx, ic)
        ptmp = ctx.enter_context(nc.psum_tensor("ptmp", [128, 128], F32))
        emit_l1moe(p, ctx, ctx, ic, c, cp, idf, idb, ptmp,
                   xn1_d, xn1f_d, gw_d, w1_d, b1m_d, w2_d, b2m_d, y1_d,
                   cem_out=cemo_d)
        p.do("sync", lambda e: e.wait_ge(c.sem.h, c.sem.n))
        with nc.Block() as blk0:
            p.play_init(blk0)
        with nc.Block() as blk:
            p.play(blk)
    return nc





def emit_final(p, ctx, tctx, ic, c, cp, idf, idb, ptmp,
               x1_d, zb_d, y1_d, wo1T_d, out_d, tag="g"):
    nc = p.nc
    sb = lambda n, s, dt: tctx.enter_context(nc.sbuf_tensor(n, s, dt))
    wo1b = ini_weight_bf16(p, tctx, c, wo1T_d, DI, D, f"{tag}wo",
                           stage_side="left")
    zband = sb(f"{tag}zband", [128, 16 * 512], BF16)
    x1band = sb(f"{tag}x1band", [128, 8 * 512], F32)
    y1band = sb(f"{tag}y1band", [128, 8 * 512], F32)
    gt = sb(f"{tag}gt", [128, 512], F32)
    tt_ = sb(f"{tag}tt", [128, 512], F32)
    pA = tctx.enter_context(nc.psum_tensor(f"{tag}pA", [128, 512], F32))

    for tt in range(8):
        tcol = tt * 512
        c.step("sync", lambda e, tcol=tcol: e.dma_start(
            zband[:].rearrange("p (c t) -> p c t", t=512),
            fm(zb_d, 0, tcol, 512, 16)), dma=True)
        c.step("sync", lambda e, tcol=tcol: e.dma_start(
            x1band[:].rearrange("p (c t) -> p c t", t=512),
            fm(x1_d, 0, tcol, 512, 8)), dma=True)
        c.step("sync", lambda e, tcol=tcol: e.dma_start(
            y1band[:].rearrange("p (c t) -> p c t", t=512),
            fm(y1_d, 0, tcol, 512, 8)), dma=True)
        for fc in range(8):
            def mmf(e, fc=fc):
                for k in range(16):
                    i = e.matmul(pA[:],
                                 wo1b[:, k * D + fc * 128: k * D + fc * 128 + 128],
                                 ch(zband, 512, k), start=(k == 0),
                                 stop=(k == 15))
                return i
            c.step("tensor", mmf)
            c.step("scalar", lambda e, fc=fc: e.activation(
                gt[:], ch(y1band, 512, fc), AF.Silu))
            c.step("vector", lambda e: e.tensor_tensor(
                tt_[:], gt[:], pA[:], ALU.mult))
            c.step("vector", lambda e, fc=fc: e.tensor_tensor(
                ch(x1band, 512, fc), ch(x1band, 512, fc), tt_[:], ALU.add))
        c.step("sync", lambda e, tcol=tcol: e.dma_start(
            fm(out_d, 0, tcol, 512, 8),
            x1band[:].rearrange("p (c t) -> p c t", t=512)), dma=True)
    return c


def build_final():
    nc = bass.Bass("TRN2", target_bir_lowering=False)
    x1_d = nc.dram_tensor("x1", [D, T], F32, kind="ExternalInput").ap()
    zb_d = nc.dram_tensor("zb", [DI, T], BF16, kind="ExternalInput").ap()
    y1_d = nc.dram_tensor("y1", [D, T], F32, kind="ExternalInput").ap()
    wo1T_d = nc.dram_tensor("wo1T", [DI, D], F32, kind="ExternalInput").ap()
    out_d = nc.dram_tensor("out", [D, T], F32, kind="ExternalOutput").ap()
    ctx = ExitStack()
    with ctx:
        p = Prog(nc, ctx)
        ic = Chain(p, ctx, "sic", init=True)
        c = Chain(p, ctx, "sc")
        cp = ConstPool(p, ctx, [1e-5], ic)
        idf, idb = ini_identities(p, ctx, ic)
        ptmp = ctx.enter_context(nc.psum_tensor("ptmp", [128, 128], F32))
        emit_final(p, ctx, ctx, ic, c, cp, idf, idb, ptmp,
                   x1_d, zb_d, y1_d, wo1T_d, out_d)
        p.do("sync", lambda e: e.wait_ge(c.sem.h, c.sem.n))
        with nc.Block() as blk0:
            p.play_init(blk0)
        with nc.Block() as blk:
            p.play(blk)
    return nc



def build_ln():
    nc = bass.Bass("TRN2", target_bir_lowering=False)
    x_d = nc.dram_tensor("x", [D, T], F32, kind="ExternalInput").ap()
    xn_d = nc.dram_tensor("xn", [D, T], BF16, kind="ExternalOutput").ap()
    xnf_d = nc.dram_tensor("xnf", [D, T], F32, kind="ExternalOutput").ap()
    ctx = ExitStack()
    with ctx:
        p = Prog(nc, ctx)
        ic = Chain(p, ctx, "sic", init=True)
        cp = ConstPool(p, ctx, [1e-5], ic)
        s_out = emit_ln(p, ctx, ctx, ic, x_d, xn_d, D, "ln_", cp, xnf_d=xnf_d)
        p.do("sync", lambda e: e.wait_ge(s_out.h, s_out.n))
        with nc.Block() as blk0:
            p.play_init(blk0)
        with nc.Block() as blk:
            p.play(blk)
    return nc



def _f32c(a):
    return np.ascontiguousarray(np.asarray(a, dtype=np.float32))


def _prep_weights(inputs):
    w = {}
    w["wqkv"] = _f32c(inputs["L0_attn_Wqkv"])
    w["bqkvm"] = _f32c(np.asarray(inputs["L0_attn_bqkv"]).reshape(24, 128))
    w["wout0"] = _f32c(inputs["L0_attn_Wout"])
    w["boutm"] = _f32c(np.asarray(inputs["L0_attn_bout"]).reshape(8, 128))
    w["rwT"] = _f32c(np.asarray(inputs["L0_ffn_router_w"]).T)
    w["rb"] = _f32c(np.asarray(inputs["L0_ffn_router_b"]).reshape(1, NE))
    w["ewT"] = _f32c(np.transpose(np.asarray(inputs["L0_ffn_expert_w"]), (0, 2, 1)))
    w["ebm"] = _f32c(np.asarray(inputs["L0_ffn_expert_b"]).reshape(64, 128))
    w["winT"] = _f32c(np.asarray(inputs["L1_attn_W_in"]).T)
    w["wconv"] = _f32c(np.asarray(inputs["L1_attn_W_conv"]).reshape(DI, 4))
    w["bconvm"] = _f32c(np.asarray(inputs["L1_attn_b_conv"]).reshape(16, 128))
    w["wxT"] = _f32c(np.asarray(inputs["L1_attn_W_x"]).T)
    w["wdtT"] = _f32c(np.asarray(inputs["L1_attn_W_dt"]).T)
    w["bdtm"] = _f32c(np.asarray(inputs["L1_attn_b_dt"]).reshape(16, 128))
    w["wo1T"] = _f32c(np.asarray(inputs["L1_attn_W_out"]).T)
    w["alog"] = _f32c(inputs["L1_attn_A_log"])
    w["dm"] = _f32c(np.asarray(inputs["L1_attn_D"]).reshape(16, 128))
    w["gw"] = _f32c(inputs["L1_ffn_gate_w"])
    w["w1"] = _f32c(inputs["L1_ffn_w1"])
    w["b1m"] = _f32c(np.asarray(inputs["L1_ffn_b1"]).reshape(256, 128))
    w["w2"] = _f32c(inputs["L1_ffn_w2"])
    w["b2m"] = _f32c(np.asarray(inputs["L1_ffn_b2"]).reshape(64, 128))
    return w


def _run(nc, maps):
    res = run_bass_kernel_spmd(nc, maps, core_ids=list(range(NCORES)))
    return [{k: np.asarray(v) for k, v in r.items()} for r in res.results]





def build_fused(debug=False):
    nc = bass.Bass("TRN2", target_bir_lowering=False)
    E_IN = "ExternalInput"
    x_d = nc.dram_tensor("x", [D, T], F32, kind=E_IN).ap()
    wqkv_d = nc.dram_tensor("wqkv", [D, 3 * D], F32, kind=E_IN).ap()
    bqkvm_d = nc.dram_tensor("bqkvm", [24, 128], F32, kind=E_IN).ap()
    wout_d = nc.dram_tensor("wout0", [D, D], F32, kind=E_IN).ap()
    boutm_d = nc.dram_tensor("boutm", [8, 128], F32, kind=E_IN).ap()
    rwT_d = nc.dram_tensor("rwT", [D, NE], F32, kind=E_IN).ap()
    rb_d = nc.dram_tensor("rb", [1, NE], F32, kind=E_IN).ap()
    ewT_d = nc.dram_tensor("ewT", [NE, D, D], F32, kind=E_IN).ap()
    ebm_d = nc.dram_tensor("ebm", [64, 128], F32, kind=E_IN).ap()
    winT_d = nc.dram_tensor("winT", [D, 2 * DI], F32, kind=E_IN).ap()
    wconv_d = nc.dram_tensor("wconv", [DI, 4], F32, kind=E_IN).ap()
    bconvm_d = nc.dram_tensor("bconvm", [16, 128], F32, kind=E_IN).ap()
    wxT_d = nc.dram_tensor("wxT", [DI, 32], F32, kind=E_IN).ap()
    wdtT_d = nc.dram_tensor("wdtT", [DI, DI], F32, kind=E_IN).ap()
    bdtm_d = nc.dram_tensor("bdtm", [16, 128], F32, kind=E_IN).ap()
    wo1T_d = nc.dram_tensor("wo1T", [DI, D], F32, kind=E_IN).ap()
    alog_d = nc.dram_tensor("alog", [DI, 16], F32, kind=E_IN).ap()
    dm_d = nc.dram_tensor("dm", [16, 128], F32, kind=E_IN).ap()
    gw_d = nc.dram_tensor("gw", [D, NE], F32, kind=E_IN).ap()
    w1_d = nc.dram_tensor("w1", [NE, D, FF], F32, kind=E_IN).ap()
    b1m_d = nc.dram_tensor("b1m", [256, 128], F32, kind=E_IN).ap()
    w2_d = nc.dram_tensor("w2", [NE, FF, D], F32, kind=E_IN).ap()
    b2m_d = nc.dram_tensor("b2m", [64, 128], F32, kind=E_IN).ap()
    out_d = nc.dram_tensor("out", [D, T], F32, kind="ExternalOutput").ap()

    I = "ExternalOutput" if debug else "Internal"
    xn_d = nc.dram_tensor("i_xn", [D, T], BF16, kind=I).ap()
    xnf_d = nc.dram_tensor("i_xnf", [D, T], F32, kind=I).ap()
    s0_d = nc.dram_tensor("i_s0", [D, T], F32, kind=I).ap()
    x1_d = nc.dram_tensor("i_x1", [D, T], F32, kind=I).ap()
    xn1_d = nc.dram_tensor("i_xn1", [D, T], BF16, kind=I).ap()
    xn1f_d = nc.dram_tensor("i_xn1f", [D, T], F32, kind=I).ap()
    res_d = nc.dram_tensor("i_res", [DI, T], F32, kind=I).ap()
    xsp_d = nc.dram_tensor("i_xsp", [DI, T], F32, kind=I).ap()
    xspb_d = nc.dram_tensor("i_xspb", [DI, T], BF16, kind=I).ap()
    bfull_d = nc.dram_tensor("i_bfull", [32, T], F32, kind=I).ap()
    dt_d = nc.dram_tensor("i_dt", [DI, T], F32, kind=I).ap()
    zb_d = nc.dram_tensor("i_zb", [DI, T], BF16, kind=I).ap()
    y1_d = nc.dram_tensor("i_y1", [D, T], F32, kind=I).ap()

    cemo_d = None
    if debug:
        cemo_d = nc.dram_tensor("f_cemo", [NE, T], F32,
                                kind="ExternalOutput").ap()
    ctx = ExitStack()
    with ctx:
        p = Prog(nc, ctx)
        ic = Chain(p, ctx, "ic", init=True)
        c = Chain(p, ctx, "c")
        cp = ConstPool(p, ctx, [1e-5, 0.125, -1.0, 1.0], ic)
        idf, idb = ini_identities(p, ctx, ic)
        ptmp = ctx.enter_context(nc.psum_tensor("ptmp", [128, 128], F32))

        with ExitStack() as tA:
            sA = emit_ln(p, ctx, tA, ic, x_d, xn_d, D, "a_", cp, xnf_d=xnf_d)
            c.step("sync", lambda e: e.nop(), wait=[W_(sA, None)])
        with ExitStack() as tB:
            emit_l0attn(p, ctx, tB, ic, c, cp, idf, idb, ptmp,
                        xn_d, wqkv_d, bqkvm_d, wout_d, boutm_d, s0_d, tag="b")
        with ExitStack() as tC:
            emit_l0moe(p, ctx, tC, ic, c, cp, idf, idb, ptmp,
                       x_d, xn_d, xnf_d, s0_d, rwT_d, rb_d, ewT_d, ebm_d,
                       x1_d, xn1_d, xn1f_d, tag="m")
        with ExitStack() as tD:
            emit_l1pre(p, ctx, tD, ic, c, cp, idf, idb, ptmp,
                       xn1_d, winT_d, wconv_d, bconvm_d, wxT_d, wdtT_d,
                       bdtm_d, res_d, xsp_d, xspb_d, bfull_d, dt_d, tag="d")
        with ExitStack() as tE:
            emit_scan(p, ctx, tE, ic, c, cp, idf, idb, ptmp,
                      xsp_d, res_d, bfull_d, dt_d, alog_d, dm_d, zb_d, tag="s")
        with ExitStack() as tF:
            emit_l1moe(p, ctx, tF, ic, c, cp, idf, idb, ptmp,
                       xn1_d, xn1f_d, gw_d, w1_d, b1m_d, w2_d, b2m_d, y1_d,
                       tag="f", cem_out=cemo_d)
        with ExitStack() as tG:
            emit_final(p, ctx, tG, ic, c, cp, idf, idb, ptmp,
                       x1_d, zb_d, y1_d, wo1T_d, out_d, tag="g")

        p.do("sync", lambda e: e.wait_ge(c.sem.h, c.sem.n))
        with nc.Block() as blk0:
            p.play_init(blk0)
        with nc.Block() as blk:
            p.play(blk)
    return nc


def kernel(**inputs):
    w = _prep_weights(inputs)
    x = np.asarray(inputs["L0_attn_hidden_states"], dtype=np.float32)
    xT = [np.ascontiguousarray(x[BPC * c: BPC * (c + 1)].reshape(T, D).T)
          for c in range(NCORES)]
    nc = build_fused()
    maps = [dict(w, x=xT[c]) for c in range(NCORES)]
    res = run_bass_kernel_spmd(nc, maps, core_ids=list(range(NCORES)))
    out = np.empty((NCORES * BPC, SEQ, D), np.float32)
    for c in range(NCORES):
        out[BPC * c: BPC * (c + 1)] = \
            np.asarray(res.results[c]["out"]).T.reshape(BPC, SEQ, D)
    return out


def kernel_v0(**inputs):
    w = _prep_weights(inputs)
    x = np.asarray(inputs["L0_attn_hidden_states"], dtype=np.float32)
    xT = [np.ascontiguousarray(x[BPC * c: BPC * (c + 1)].reshape(T, D).T)
          for c in range(NCORES)]

    rA = _run(build_ln(), [{"x": xT[c]} for c in range(NCORES)])

    rB = _run(build_l0attn(), [
        {"xn": rA[c]["xn"], "wqkv": w["wqkv"], "bqkvm": w["bqkvm"],
         "wout0": w["wout0"], "boutm": w["boutm"]} for c in range(NCORES)])

    rC = _run(build_l0moe(), [
        {"x": xT[c], "xn": rA[c]["xn"], "xnf": rA[c]["xnf"],
         "s0": rB[c]["s0"], "rwT": w["rwT"], "rb": w["rb"],
         "ewT": w["ewT"], "ebm": w["ebm"]} for c in range(NCORES)])

    rD = _run(build_l1pre(), [
        {"xn1": rC[c]["xn1"], "winT": w["winT"], "wconv": w["wconv"],
         "bconvm": w["bconvm"], "wxT": w["wxT"], "wdtT": w["wdtT"],
         "bdtm": w["bdtm"]} for c in range(NCORES)])

    rE = _run(build_scan(), [
        {"xsp": rD[c]["xsp"], "res": rD[c]["res"], "bfull": rD[c]["bfull"],
         "dt": rD[c]["dt"], "alog": w["alog"], "dm": w["dm"]}
        for c in range(NCORES)])

    rF = _run(build_l1moe(), [
        {"xn1": rC[c]["xn1"], "xn1f": rC[c]["xn1f"], "gw": w["gw"],
         "w1": w["w1"], "b1m": w["b1m"], "w2": w["w2"], "b2m": w["b2m"]}
        for c in range(NCORES)])

    rG = _run(build_final(), [
        {"x1": rC[c]["x1"], "zb": rE[c]["zb"], "y1": rF[c]["y1"],
         "wo1T": w["wo1T"]} for c in range(NCORES)])

    out = np.empty((NCORES * BPC, SEQ, D), np.float32)
    for c in range(NCORES):
        out[BPC * c: BPC * (c + 1)] = \
            rG[c]["out"].T.reshape(BPC, SEQ, D)
    return out


# version 05 (final)
# speedup vs baseline: 1.4012x; 1.1450x over previous
import numpy as np
import ml_dtypes
from contextlib import ExitStack

import concourse.bass as bass
import concourse.mybir as mybir
from concourse.bass_utils import run_bass_kernel_spmd

F32 = mybir.dt.float32
BF16 = mybir.dt.bfloat16
AF = mybir.ActivationFunctionType
ALU = mybir.AluOpType
AX = mybir.AxisListType.X

NCORES = 8
D = 1024
SEQ = 2048
BPC = 2
T = BPC * SEQ
DI = 2048
DS = 16
NH = 16
DH = 64
NL = 64
LW = 32
NE = 8
FF = 4096


class Sem:
    def __init__(self, ctx, nc, name):
        self.h = ctx.enter_context(nc.semaphore(name))
        self.n = 0


class Prog:
    ENGS = ("sync", "tensor", "vector", "scalar", "gpsimd")

    def __init__(self, nc, ctx):
        self.nc = nc
        self.ctx = ctx
        self.ops = {e: [] for e in self.ENGS}
        self.init_ops = {e: [] for e in self.ENGS}

    def ini(self, eng, fn):
        self.init_ops[eng].append(lambda e, fn=fn: fn(e))

    def doi(self, eng, fn, inc=None, wait=()):
        waits = [(s.h, v if v is not None else s.n) for (s, v) in wait]
        if inc is not None:
            sem, k = inc
            sem.n += k
            tgt = (sem.h, k)
        else:
            tgt = None

        def op(e, fn=fn, waits=waits, tgt=tgt):
            for h, v in waits:
                if v > 0:
                    e.wait_ge(h, v)
            instr = fn(e)
            if tgt is not None:
                instr.then_inc(tgt[0], tgt[1])

        self.init_ops[eng].append(op)

    def play_init(self, block):
        for eng in self.ENGS:
            ops = self.init_ops[eng]
            if not ops:
                continue

            def body(e, ops=ops):
                for fn in ops:
                    fn(e)

            getattr(block, eng)(body)

    def do(self, eng, fn, inc=None, wait=()):
        waits = [(s.h, v if v is not None else s.n) for (s, v) in wait]
        if inc is not None:
            sem, k = inc
            sem.n += k
            tgt = (sem.h, k)
        else:
            tgt = None

        def op(e, fn=fn, waits=waits, tgt=tgt):
            for h, v in waits:
                if v > 0:
                    e.wait_ge(h, v)
            instr = fn(e)
            if tgt is not None:
                assert instr is not None, "fn must return instruction for inc"
                instr.then_inc(tgt[0], tgt[1])

        self.ops[eng].append(op)
        return None

    def play(self, block):
        for eng in self.ENGS:
            ops = self.ops[eng]
            if not ops:
                continue

            def body(e, ops=ops):
                for fn in ops:
                    fn(e)

            getattr(block, eng)(body)


def W_(s, v=None):
    return (s, v)


class Chain:

    SEM_LIMIT = 24000

    def __init__(self, p, ctx, name, init=False):
        self.p = p
        self.ctx = ctx
        self.name = name
        self.init = init
        self.n_sems = 0
        self.sem = self._new_sem()
        self.carry = None

    def _new_sem(self):
        self.n_sems += 1
        return Sem(self.ctx, self.p.nc, f"{self.name}_{self.n_sems}")

    def step(self, eng, fn, dma=False, wait=()):
        k = 16 if dma else 1
        if self.sem.n + k > self.SEM_LIMIT:
            self.carry = (self.sem, self.sem.n)
            self.sem = self._new_sem()
        w = list(wait) + [W_(self.sem, self.sem.n)]
        if self.carry is not None:
            w.append(W_(self.carry[0], self.carry[1]))
            self.carry = None
        if self.init:
            self.p.doi(eng, fn, inc=(self.sem, k), wait=w)
        else:
            self.p.do(eng, fn, inc=(self.sem, k), wait=w)


class ConstPool:
    def __init__(self, p, ctx, values, ic):
        self.nc = p.nc
        vals = sorted(set(float(v) for v in values))
        self.tile = ctx.enter_context(
            p.nc.sbuf_tensor("constpool", [128, max(1, len(vals))], F32,
                             side="right"))
        self.idx = {}
        for i, v in enumerate(vals):
            self.idx[v] = i
            ic.step("gpsimd", lambda e, i=i, v=v: e.memset(
                self.tile[:, i:i + 1], v))

    def col(self, v, parts=128):
        i = self.idx[float(v)]
        return self.tile[0:parts, i:i + 1]


def fm(dram_ap, chunk, col0, ncols, nchunks=1):
    r = dram_ap.rearrange("(c p) t -> p c t", p=128)
    return r[:, chunk : chunk + nchunks, col0 : col0 + ncols]


def bd(sb, width, chunk, col0, ncols, nchunks=1):
    r = sb[:].rearrange("p (c t) -> p c t", t=width)
    return r[:, chunk : chunk + nchunks, col0 : col0 + ncols]


def ch(sb, width, chunk, col0=0, ncols=None):
    if ncols is None:
        ncols = width
    base = chunk * width + col0
    return sb[:, base : base + ncols]



def emit_ln(p, ctx, tctx, ic, x_d, xn_d, rows, tag, cp, xnf_d=None):
    nc = p.nc
    NC = rows // 128
    W = 512
    NT = T // W

    xt = [tctx.enter_context(nc.sbuf_tensor(f"{tag}x{i}", [128, NC * W], F32)) for i in range(2)]
    xb = [tctx.enter_context(nc.sbuf_tensor(f"{tag}xb{i}", [128, NC * W], BF16)) for i in range(2)]
    sq = [tctx.enter_context(nc.sbuf_tensor(f"{tag}sq{i}", [128, W], BF16)) for i in range(2)]
    xo = [tctx.enter_context(nc.sbuf_tensor(f"{tag}xo{i}", [128, NC * W], BF16)) for i in range(2)]
    onesb = ctx.enter_context(nc.sbuf_tensor(f"{tag}ones", [128, 1], BF16,
                                             side="right"))
    stat = tctx.enter_context(nc.sbuf_tensor(f"{tag}st", [1, 7 * W], F32))
    rsb = [tctx.enter_context(nc.sbuf_tensor(f"{tag}rsb{i}", [128, W], F32)) for i in range(2)]
    mbb = [tctx.enter_context(nc.sbuf_tensor(f"{tag}mbb{i}", [128, W], F32)) for i in range(2)]
    ps = tctx.enter_context(nc.psum_tensor(f"{tag}ps", [1, W], F32))
    pq = tctx.enter_context(nc.psum_tensor(f"{tag}pq", [1, W], F32))
    rows_d = nc.dram_tensor(f"{tag}rows", [2, T], F32, kind="Internal").ap()

    s_ld = Sem(ctx, nc, f"{tag}ld")
    s_cast = Sem(ctx, nc, f"{tag}cast")
    s_mm = Sem(ctx, nc, f"{tag}mm")
    s_st = Sem(ctx, nc, f"{tag}stt")
    s_rw = Sem(ctx, nc, f"{tag}rw")
    s_bc = Sem(ctx, nc, f"{tag}bc")
    s_out = Sem(ctx, nc, f"{tag}out")
    s_fin = Sem(ctx, nc, f"{tag}fin")

    ic.step("gpsimd", lambda e: e.memset(onesb[:], 1.0))

    out_hist = []
    for it in range(NT):
        b = it % 2
        c0 = it * W
        tile_cast0 = s_cast.n

        w = []
        if it >= 2:
            w = [W_(s_fin, it - 1), W_(s_out, out_hist[it - 1])]
        p.do("sync",
             lambda e, b=b, c0=c0: e.dma_start(
                 xt[b][:].rearrange("p (c t) -> p c t", t=W),
                 fm(x_d, 0, c0, W, NC)),
             inc=(s_ld, 16), wait=w)

        p.do("vector",
             lambda e, b=b: e.tensor_copy(xb[b][:], xt[b][:]),
             inc=(s_cast, 1), wait=[W_(s_ld, None)])

        tw = [W_(s_cast, None)]
        if it >= 1:
            tw.append(W_(s_st, (it - 1) * 3 + 1))
        for c in range(NC):
            sb2 = c % 2
            sw = [W_(s_ld, None)]
            if c >= 2:
                sw.append(W_(s_mm, s_mm.n - 1))
            p.do("scalar",
                 lambda e, b=b, c=c, sb2=sb2: e.activation(
                     sq[sb2][:], ch(xt[b], W, c), AF.Square),
                 inc=(s_cast, 1), wait=sw)

            def mk_mm(e, b=b, c=c, sb2=sb2):
                e.matmul(ps[:], onesb[:], ch(xb[b], W, c),
                         start=(c == 0), stop=(c == NC - 1))
                return e.matmul(pq[:], onesb[:], sq[sb2][:],
                                start=(c == 0), stop=(c == NC - 1))
            p.do("tensor", mk_mm, inc=(s_mm, 1),
                 wait=(tw + [W_(s_cast, tile_cast0 + c + 2)]) if c == 0
                 else [W_(s_cast, tile_cast0 + c + 2)])

        mean = stat[0:1, 0:W]
        ex2 = stat[0:1, W:2 * W]
        var = stat[0:1, 2 * W:3 * W]
        std = stat[0:1, 3 * W:4 * W]
        rs = stat[0:1, 4 * W:5 * W]
        mb = stat[0:1, 5 * W:6 * W]
        tmp = stat[0:1, 6 * W:7 * W]

        def mk_stats(e):
            e.tensor_scalar(mean, ps[:], 1.0 / rows, None, ALU.mult)
            e.tensor_scalar(ex2, pq[:], 1.0 / rows, None, ALU.mult)
            e.tensor_tensor(tmp, mean, mean, ALU.mult)
            return e.tensor_tensor(var, ex2, tmp, ALU.subtract)
        vw = [W_(s_mm, None)]
        if it >= 1:
            vw.append(W_(s_rw, it * 32))
        p.do("vector", mk_stats, inc=(s_st, 1), wait=vw)

        p.do("scalar", lambda e: e.activation(std, var, AF.Sqrt,
                                              bias=cp.col(1e-5, 1)),
             inc=(s_st, 1), wait=[W_(s_st, None)])

        def mk_rs(e):
            e.reciprocal(rs, std)
            return e.scalar_tensor_tensor(mb, mean, -1.0, rs, ALU.mult, ALU.mult)
        p.do("vector", mk_rs, inc=(s_st, 1), wait=[W_(s_st, None)])

        p.do("sync", lambda e, c0=c0: e.dma_start(rows_d[0:1, c0:c0 + W], rs),
             inc=(s_rw, 16), wait=[W_(s_st, None)])
        p.do("sync", lambda e, c0=c0: e.dma_start(rows_d[1:2, c0:c0 + W], mb),
             inc=(s_rw, 16))

        p.do("sync",
             lambda e, b=b, c0=c0: e.dma_start(
                 rsb[b][:],
                 rows_d[0:1, c0:c0 + W].partition_broadcast(128)[:, 0, :]),
             inc=(s_bc, 16), wait=[W_(s_rw, None)])
        p.do("sync",
             lambda e, b=b, c0=c0: e.dma_start(
                 mbb[b][:],
                 rows_d[1:2, c0:c0 + W].partition_broadcast(128)[:, 0, :]),
             inc=(s_bc, 16))

        for c in range(NC):
            p.do("vector",
                 lambda e, b=b, c=c: e.tensor_tensor(
                     ch(xt[b], W, c), ch(xt[b], W, c), rsb[b][:], ALU.mult),
                 wait=[W_(s_bc, None)] if c == 0 else ())
        for c in range(NC):
            p.do("vector",
                 lambda e, b=b, c=c: e.tensor_tensor(
                     ch(xt[b], W, c), ch(xt[b], W, c), mbb[b][:], ALU.add))
        p.do("vector", lambda e, b=b: e.tensor_copy(xo[b][:], xt[b][:]),
             inc=(s_fin, 1))

        if xnf_d is not None:
            p.do("sync",
                 lambda e, b=b, c0=c0: e.dma_start(
                     fm(xnf_d, 0, c0, W, NC),
                     xt[b][:].rearrange("p (c t) -> p c t", t=W)),
                 inc=(s_out, 16), wait=[W_(s_fin, None)])
        p.do("sync",
             lambda e, b=b, c0=c0: e.dma_start(
                 fm(xn_d, 0, c0, W, NC),
                 xo[b][:].rearrange("p (c t) -> p c t", t=W)),
             inc=(s_out, 16), wait=[W_(s_fin, None)])
        out_hist.append(s_out.n)

    return s_out



def ini_identities(p, ctx, ic, tag=""):
    nc = p.nc
    idf = ctx.enter_context(nc.sbuf_tensor(f"idf{tag}", [128, 128], F32,
                                            side="right"))
    idb = ctx.enter_context(nc.sbuf_tensor(f"idb{tag}", [128, 128], BF16,
                                           side="right"))

    def mk(e):
        e.memset(idf[:], 1.0)
        e.affine_select(idf[:], idf[:], pattern=[[1, 128]],
                        compare_op=ALU.is_equal, fill=0.0, base=0,
                        channel_multiplier=-1)
        e.memset(idb[:], 1.0)
        return e.affine_select(idb[:], idb[:], pattern=[[1, 128]],
                               compare_op=ALU.is_equal, fill=0.0, base=0,
                               channel_multiplier=-1)
    ic.step("gpsimd", mk)
    return idf, idb


def ini_weight_bf16(p, ctx, ic, w_d, R, C, tag, sctx=None, side="left",
                    stage_side="right"):
    nc = p.nc
    nk = R // 128
    wb = ctx.enter_context(nc.sbuf_tensor(f"wb{tag}", [128, nk * C], BF16,
                                          side=side))
    with nc.sbuf_tensor(f"wstg{tag}", [128, C], F32, side=stage_side) as stg:
        for c in range(nk):
            ic.step("sync",
                    lambda e, c=c: e.dma_start(
                        stg[:], w_d[c * 128:(c + 1) * 128, :]), dma=True)
            ic.step("vector",
                    lambda e, c=c: e.tensor_copy(ch(wb, C, c), stg[:]))
    return wb


def ini_biasT(p, ctx, ic, b_d, m, ptmp, idf, tag, sctx=None, side="right"):
    nc = p.nc
    dst = ctx.enter_context(nc.sbuf_tensor(f"bT{tag}", [128, m], F32,
                                           side=side))
    with nc.sbuf_tensor(f"bstg{tag}", [m, 128], F32, side="right") as stg:
        ic.step("sync", lambda e: e.dma_start(stg[:], b_d), dma=True)
        ic.step("tensor",
                lambda e: e.transpose(ptmp[:, 0:m], stg[:], idf[0:m, 0:m]))
        ic.step("vector", lambda e: e.tensor_copy(dst[:], ptmp[:, 0:m]))
    return dst


def ini_row_bf16(p, ctx, ic, b_d, n, tag, sctx=None, side="right",
                 stage_side="right"):
    nc = p.nc
    dst = ctx.enter_context(nc.sbuf_tensor(f"row{tag}", [1, n], BF16,
                                           side=side))
    with nc.sbuf_tensor(f"rstg{tag}", [1, n], F32, side=stage_side) as stg:
        flat = b_d.rearrange("m p -> (m p)").unsqueeze(0)
        ic.step("sync", lambda e: e.dma_start(stg[:], flat), dma=True)
        ic.step("vector", lambda e: e.tensor_copy(dst[:], stg[:]))
    return dst



def emit_l0attn(p, ctx, tctx, ic, c, cp, idf, idb, ptmp,
                xn_d, wqkv_d, bqkvm_d, wout_d, boutm_d, s0_d, tag="b",
                dbg=None):
    nc = p.nc

    bqkvT = ini_biasT(p, ctx, ic, bqkvm_d, 24, ptmp, idf, f"{tag}qkv")
    boutT = ini_biasT(p, ctx, ic, boutm_d, 8, ptmp, idf, f"{tag}out")

    bqkvT_qs = ctx.enter_context(nc.sbuf_tensor(f"{tag}bqs", [128, 8], F32,
                                                side="right"))
    ic.step("vector", lambda e: e.tensor_scalar(
        bqkvT_qs[:], bqkvT[:, 0:8], 0.125, None, ALU.mult))

    i7 = ctx.enter_context(nc.sbuf_tensor(f"{tag}i7", [64, 64], F32,
                                           side="right"))
    i15 = ctx.enter_context(nc.sbuf_tensor(f"{tag}i15", [64, 64], F32,
                                           side="right"))
    i13 = ctx.enter_context(nc.sbuf_tensor(f"{tag}i13", [64, 64], F32,
                                           side="right"))
    ic.step("vector", lambda e: e.tensor_scalar(
        i7[:], idf[0:64, 0:64], 7.0, None, ALU.mult))
    ic.step("vector", lambda e: e.tensor_scalar(
        i15[:], idf[0:64, 0:64], 15.0, None, ALU.mult))
    ic.step("vector", lambda e: e.tensor_scalar(
        i13[:], idf[0:64, 0:64], 13.0, None, ALU.mult))
    onesr = ctx.enter_context(nc.sbuf_tensor(f"{tag}onesr", [1, 128], BF16,
                                              side="right"))
    ic.step("gpsimd", lambda e: e.memset(onesr[:], 1.0))

    sb = lambda n, s, dt: tctx.enter_context(nc.sbuf_tensor(n, s, dt))
    wqkvb = ini_weight_bf16(p, tctx, c, wqkv_d, D, 3 * D, f"{tag}qkv",
                            stage_side="left")
    woutb = ini_weight_bf16(p, tctx, c, wout_d, D, D, f"{tag}out",
                            stage_side="left")
    bqkv_row = ini_row_bf16(p, tctx, c, bqkvm_d, 3 * D, f"{tag}qkv",
                            side="left", stage_side="left")
    xnb = [sb(f"{tag}xnb0", [128, 8 * 512], BF16)]
    qg = sb(f"{tag}qg", [128, 4 * SEQ], BF16)
    kg = sb(f"{tag}kg", [128, 4 * SEQ], BF16)
    vg = sb(f"{tag}vg", [128, 16 * 512], BF16)
    ohd = sb(f"{tag}ohd", [128, 8 * SEQ], BF16)
    t1 = sb(f"{tag}t1", [128, 1024], F32)
    t2 = sb(f"{tag}t2", [128, 512], F32)
    t3 = sb(f"{tag}t3", [128, 256], F32)
    t4 = sb(f"{tag}t4", [128, 128], F32)
    t5 = sb(f"{tag}t5", [128, 64], F32)
    qlb = sb(f"{tag}qlb", [128, 4 * 64], BF16)
    klb = sb(f"{tag}klb", [128, 4 * 64], BF16)
    a1m = sb(f"{tag}a1m", [64, SEQ], BF16)
    a1t = sb(f"{tag}a1t", [128, 64], BF16)
    e1f = sb(f"{tag}e1f", [128, 64], F32)
    s3f = sb(f"{tag}s3f", [64, SEQ], F32)
    a3f = sb(f"{tag}a3f", [64, SEQ], F32)
    a3b = sb(f"{tag}a3b", [64, SEQ], BF16)
    a3T = sb(f"{tag}a3T", [128, 16 * 64], BF16)
    e2f = sb(f"{tag}e2f", [64, 64], F32)
    cols = sb(f"{tag}cols", [128, 4], F32)
    a2b = sb(f"{tag}a2b", [64, 64], F32)
    a2T = sb(f"{tag}a2T", [64, 64], F32)
    Vb = sb(f"{tag}Vb", [64, 64], F32)
    VTb = sb(f"{tag}VTb", [64, 64], F32)
    KVb = sb(f"{tag}KVb", [64, 64], F32)
    KVT = sb(f"{tag}KVT", [64, 64], F32)
    C1b = sb(f"{tag}C1b", [64, 64], F32)
    C2b = sb(f"{tag}C2b", [64, 64], F32)
    C3b = sb(f"{tag}C3b", [64, 64], F32)
    avb = sb(f"{tag}avb", [64, 64], F32)
    zb = sb(f"{tag}zb", [64, 64], BF16)
    s0t = sb(f"{tag}s0t", [128, 512], F32)

    pA = tctx.enter_context(nc.psum_tensor(f"{tag}pA", [128, 512], F32))
    pE = tctx.enter_context(nc.psum_tensor(f"{tag}pE", [128, 64], F32))
    pT = tctx.enter_context(nc.psum_tensor(f"{tag}pT", [128, 128], BF16))
    pE64 = pE[0:64, 0:64]
    pT64 = pT[0:64, 0:64]
    idb64 = idb[0:64, 0:64]

    neg = cols[:, 0:1]
    ssum = cols[:, 1:2]
    rec = cols[:, 2:3]

    def softmax_free(parts, src, tmp, dst_b):
        c.step("vector", lambda e: e.reduce_max(
            neg[0:parts, :], src, axis=AX, negate=True))
        c.step("scalar", lambda e: e.activation(
            tmp, src, AF.Exp, bias=neg[0:parts, :],
            accum_out=ssum[0:parts, :]))
        c.step("vector", lambda e: e.reciprocal(rec[0:parts, :],
                                                ssum[0:parts, :]))
        c.step("vector", lambda e: e.tensor_scalar(
            dst_b, tmp, rec[0:parts, :], None, ALU.mult))

    for b in range(BPC):
        tb = b * SEQ
        for hg in range(2):
            F0 = hg * 512
            for tt in range(4):
                xv = xnb[0]
                c.step("sync", lambda e, xv=xv, tb=tb, tt=tt: e.dma_start(
                    xv[:].rearrange("p (c t) -> p c t", t=512),
                    fm(xn_d, 0, tb + tt * 512, 512, 8)), dma=True)
                for qc in range(4):
                    gfc = F0 // 128 + qc

                    def mmq(e, xv=xv, gfc=gfc):
                        for k in range(8):
                            i = e.matmul(
                                pA[:],
                                wqkvb[:, k * 3 * D + gfc * 128:
                                      k * 3 * D + gfc * 128 + 128],
                                ch(xv, 512, k),
                                start=(k == 0), stop=(k == 7))
                        return i
                    c.step("tensor", mmq)
                    c.step("scalar", lambda e, qc=qc, tt=tt, gfc=gfc: e.activation(
                        qg[:, qc * SEQ + tt * 512: qc * SEQ + tt * 512 + 512],
                        pA[:], AF.Identity, bias=bqkvT_qs[:, gfc:gfc + 1],
                        scale=cp.col(0.125)))
                for kc in range(4):
                    gfc = 8 + F0 // 128 + kc
                    col0 = D + F0 + kc * 128

                    def mmk(e, xv=xv, col0=col0):
                        for k in range(8):
                            i = e.matmul(
                                pA[:],
                                wqkvb[:, k * 3 * D + col0:
                                      k * 3 * D + col0 + 128],
                                ch(xv, 512, k),
                                start=(k == 0), stop=(k == 7))
                        return i
                    c.step("tensor", mmk)
                    c.step("scalar", lambda e, kc=kc, tt=tt, gfc=gfc: e.activation(
                        kg[:, kc * SEQ + tt * 512: kc * SEQ + tt * 512 + 512],
                        pA[:], AF.Identity, bias=bqkvT[:, gfc:gfc + 1]))
                for ts4 in range(4):
                    i128 = tt * 4 + ts4
                    vcol = 2 * D + F0

                    def mmv(e, xv=xv, ts4=ts4, vcol=vcol):
                        for k in range(8):
                            e.matmul(
                                pA[:],
                                xv[:, k * 512 + ts4 * 128:
                                   k * 512 + ts4 * 128 + 128],
                                wqkvb[:, k * 3 * D + vcol:
                                      k * 3 * D + vcol + 512],
                                start=(k == 0), stop=False)
                        return e.matmul(
                            pA[:], onesr[0:1, :],
                            bqkv_row[0:1, vcol:vcol + 512],
                            start=False, stop=True)
                    c.step("tensor", mmv)
                    c.step("vector", lambda e, i128=i128: e.tensor_copy(
                        ch(vg, 512, i128), pA[:]))

            if dbg is not None and b == 0 and hg == 0:
                c.step("sync", lambda e: e.dma_start(dbg["qg"], qg[:]), dma=True)
                c.step("sync", lambda e: e.dma_start(dbg["kg"], kg[:]), dma=True)
                c.step("sync", lambda e: e.dma_start(dbg["vg"], vg[:]), dma=True)

            for qc in range(4):
                for (srcb, dstb) in ((qg, qlb), (kg, klb)):
                    s0c = qc * SEQ

                    def tree(e, srcb=srcb, dstb=dstb, s0c=s0c, qc=qc):
                        e.tensor_tensor(t1[:], srcb[:, s0c:s0c + SEQ:2],
                                        srcb[:, s0c + 1:s0c + SEQ:2], ALU.add)
                        e.tensor_tensor(t2[:], t1[:, 0:1024:2],
                                        t1[:, 1:1024:2], ALU.add)
                        e.tensor_tensor(t3[:], t2[:, 0:512:2],
                                        t2[:, 1:512:2], ALU.add)
                        e.tensor_tensor(t4[:], t3[:, 0:256:2],
                                        t3[:, 1:256:2], ALU.add)
                        e.tensor_tensor(t5[:], t4[:, 0:128:2],
                                        t4[:, 1:128:2], ALU.add)
                        return e.tensor_scalar(
                            dstb[:, qc * 64:(qc + 1) * 64],
                            t5[:], 1.0 / 32.0, None, ALU.mult)
                    c.step("vector", tree)

            for hh in range(8):
                qc = hh // 2
                half = (hh % 2) * 64
                ql_h = qlb[half:half + 64, qc * 64:(qc + 1) * 64]
                kl_h = klb[half:half + 64, qc * 64:(qc + 1) * 64]

                if dbg is not None and b == 0 and hg == 0 and hh == 0:
                    c.step("sync", lambda e: e.dma_start(dbg["qlb"], qlb[:]), dma=True)
                    c.step("sync", lambda e: e.dma_start(dbg["klb"], klb[:]), dma=True)

                c.step("tensor", lambda e, ql_h=ql_h, kl_h=kl_h: e.matmul(
                    pE64, ql_h, kl_h, start=True, stop=True))
                c.step("vector", lambda e: e.tensor_copy(e2f[:], pE64))
                softmax_free(64, e2f[:], a3f[:, 0:64], a2b[:])
                c.step("tensor", lambda e: e.transpose(
                    ptmp[0:64, 0:64], a2b[:], idf[0:64, 0:64]))
                c.step("vector", lambda e: e.tensor_copy(a2T[:], ptmp[0:64, 0:64]))
                c.step("vector", lambda e: e.tensor_copy(Vb[:], a2b[:]))

                for _ in range(6):
                    c.step("tensor", lambda e: e.matmul(
                        pE64, a2T[:], Vb[:], start=True, stop=True))
                    c.step("vector", lambda e: e.tensor_copy(KVb[:], pE64))
                    c.step("vector", lambda e: e.scalar_tensor_tensor(
                        C1b[:], pE64, -1.0, i7[:], ALU.mult, ALU.add))
                    c.step("tensor", lambda e: e.transpose(
                        ptmp[0:64, 0:64], KVb[:], idf[0:64, 0:64]))
                    c.step("vector", lambda e: e.tensor_copy(KVT[:], ptmp[0:64, 0:64]))
                    c.step("tensor", lambda e: e.matmul(
                        pE64, KVT[:], C1b[:], start=True, stop=True))
                    c.step("vector", lambda e: e.scalar_tensor_tensor(
                        C2b[:], pE64, -1.0, i15[:], ALU.mult, ALU.add))
                    c.step("tensor", lambda e: e.matmul(
                        pE64, KVT[:], C2b[:], start=True, stop=True))
                    c.step("vector", lambda e: e.scalar_tensor_tensor(
                        C3b[:], pE64, -1.0, i13[:], ALU.mult, ALU.add))
                    c.step("tensor", lambda e: e.transpose(
                        ptmp[0:64, 0:64], Vb[:], idf[0:64, 0:64]))
                    c.step("vector", lambda e: e.tensor_copy(VTb[:], ptmp[0:64, 0:64]))
                    c.step("tensor", lambda e: e.matmul(
                        pE64, VTb[:], C3b[:], start=True, stop=True))
                    c.step("vector", lambda e: e.tensor_scalar(
                        Vb[:], pE64, 0.25, None, ALU.mult))
                c.step("tensor", lambda e: e.transpose(
                    ptmp[0:64, 0:64], Vb[:], idf[0:64, 0:64]))
                c.step("vector", lambda e: e.tensor_copy(VTb[:], ptmp[0:64, 0:64]))
                if dbg is not None and b == 0 and hg == 0 and hh == 0:
                    c.step("sync", lambda e: e.dma_start(dbg["a2b"], a2b[:]), dma=True)
                    c.step("sync", lambda e: e.dma_start(dbg["Vb"], Vb[:]), dma=True)

                for i in range(16):
                    qsl = qg[half:half + 64,
                             qc * SEQ + i * 128: qc * SEQ + i * 128 + 128]
                    c.step("tensor", lambda e, qsl=qsl, kl_h=kl_h: e.matmul(
                        pA[:, 0:64], qsl, kl_h, start=True, stop=True))
                    c.step("vector", lambda e: e.reduce_max(
                        neg, pA[:, 0:64], axis=AX, negate=True))
                    c.step("scalar", lambda e: e.activation(
                        e1f[:], pA[:, 0:64], AF.Exp, bias=neg,
                        accum_out=ssum))
                    c.step("vector", lambda e: e.reciprocal(rec, ssum))
                    c.step("vector", lambda e: e.tensor_scalar(
                        a1t[:], e1f[:], rec, None, ALU.mult))
                    c.step("tensor", lambda e: e.transpose(
                        pT[0:64, :], a1t[:], idb[:]))
                    c.step("vector", lambda e, i=i: e.tensor_copy(
                        a1m[:, i * 128:(i + 1) * 128], pT[0:64, :]))

                for j in range(4):
                    ksl = kg[half:half + 64,
                             qc * SEQ + j * 512: qc * SEQ + j * 512 + 512]
                    c.step("tensor", lambda e, ksl=ksl, ql_h=ql_h: e.matmul(
                        pA[0:64, :], ql_h, ksl, start=True, stop=True))
                    c.step("vector", lambda e, j=j: e.tensor_copy(
                        s3f[:, j * 512:(j + 1) * 512], pA[0:64, :]))
                c.step("vector", lambda e: e.reduce_max(
                    neg[0:64, :], s3f[:], axis=AX, negate=True))
                c.step("scalar", lambda e: e.activation(
                    a3f[:], s3f[:], AF.Exp, bias=neg[0:64, :],
                    accum_out=ssum[0:64, :]))
                c.step("vector", lambda e: e.reciprocal(rec[0:64, :],
                                                        ssum[0:64, :]))
                c.step("vector", lambda e: e.tensor_scalar(
                    a3b[:], a3f[:], rec[0:64, :], None, ALU.mult))
                for i in range(16):
                    c.step("tensor", lambda e, i=i: e.transpose(
                        pT[:, 0:64], a3b[:, i * 128:(i + 1) * 128], idb64))
                    c.step("vector", lambda e, i=i: e.tensor_copy(
                        a3T[:, i * 64:(i + 1) * 64], pT[:, 0:64]))

                def mmav(e, hh=hh):
                    for i in range(16):
                        ins = e.matmul(
                            pE64, a3T[:, i * 64:(i + 1) * 64],
                            vg[:, i * 512 + hh * 64: i * 512 + hh * 64 + 64],
                            start=(i == 0), stop=(i == 15))
                    return ins
                c.step("tensor", mmav)
                c.step("vector", lambda e: e.tensor_copy(avb[:], pE64))

                c.step("tensor", lambda e: e.matmul(
                    pE64, VTb[:], avb[:], start=True, stop=True))
                c.step("vector", lambda e: e.tensor_copy(zb[:], pE64))
                if dbg is not None and b == 0 and hg == 0 and hh == 0:
                    c.step("sync", lambda e: e.dma_start(dbg["avb"], avb[:]), dma=True)
                    c.step("sync", lambda e: e.dma_start(dbg["zb"], zb[:]), dma=True)
                    c.step("sync", lambda e: e.dma_start(dbg["a1m"], a1m[:]), dma=True)
                    c.step("sync", lambda e: e.dma_start(dbg["a3b"], a3b[:]), dma=True)
                gc = hg * 4 + qc
                for j in range(4):
                    c.step("tensor", lambda e, j=j: e.matmul(
                        pA[0:64, :], zb[:],
                        a1m[:, j * 512:(j + 1) * 512], start=True, stop=True))
                    c.step("vector", lambda e, j=j, gc=gc, half=half:
                           e.tensor_copy(
                               ohd[half:half + 64,
                                   gc * SEQ + j * 512: gc * SEQ + j * 512 + 512],
                               pA[0:64, :]))

        for fc in range(8):
            for ttj in range(4):
                def mmo(e, fc=fc, ttj=ttj):
                    for k in range(8):
                        i = e.matmul(
                            pA[:],
                            woutb[:, k * D + fc * 128: k * D + fc * 128 + 128],
                            ohd[:, k * SEQ + ttj * 512: k * SEQ + ttj * 512 + 512],
                            start=(k == 0), stop=(k == 7))
                    return i
                c.step("tensor", mmo)
                c.step("scalar", lambda e, fc=fc: e.activation(
                    s0t[:], pA[:], AF.Identity, bias=boutT[:, fc:fc + 1]))
                c.step("sync", lambda e, fc=fc, ttj=ttj, tb=tb: e.dma_start(
                    s0_d[fc * 128:(fc + 1) * 128,
                         tb + ttj * 512: tb + ttj * 512 + 512], s0t[:]),
                    dma=True)
    return c


def build_l0attn(debug=False):
    nc = bass.Bass("TRN2", target_bir_lowering=False)
    xn_d = nc.dram_tensor("xn", [D, T], BF16, kind="ExternalInput").ap()
    wqkv_d = nc.dram_tensor("wqkv", [D, 3 * D], F32, kind="ExternalInput").ap()
    bqkvm_d = nc.dram_tensor("bqkvm", [24, 128], F32, kind="ExternalInput").ap()
    wout_d = nc.dram_tensor("wout0", [D, D], F32, kind="ExternalInput").ap()
    boutm_d = nc.dram_tensor("boutm", [8, 128], F32, kind="ExternalInput").ap()
    s0_d = nc.dram_tensor("s0", [D, T], F32, kind="ExternalOutput").ap()
    dbg = None
    if debug:
        dbg = {}
        for nm, shp, dt in [("qg", [128, 4 * SEQ], BF16), ("kg", [128, 4 * SEQ], BF16),
                            ("vg", [128, 16 * 512], BF16), ("qlb", [128, 256], BF16),
                            ("klb", [128, 256], BF16), ("a2b", [64, 64], F32),
                            ("Vb", [64, 64], F32), ("avb", [64, 64], F32),
                            ("zb", [64, 64], BF16), ("a1m", [64, SEQ], BF16),
                            ("a3b", [64, SEQ], BF16)]:
            dbg[nm] = nc.dram_tensor("dbg_" + nm, shp, dt, kind="ExternalOutput").ap()
    ctx = ExitStack()
    with ctx:
        p = Prog(nc, ctx)
        ic = Chain(p, ctx, "sic", init=True)
        c = Chain(p, ctx, "sc")
        cp = ConstPool(p, ctx, [1e-5, 0.125], ic)
        idf, idb = ini_identities(p, ctx, ic)
        ptmp = ctx.enter_context(nc.psum_tensor("ptmp", [128, 128], F32))
        emit_l0attn(p, ctx, ctx, ic, c, cp, idf, idb, ptmp,
                    xn_d, wqkv_d, bqkvm_d, wout_d, boutm_d, s0_d, dbg=dbg)
        p.do("sync", lambda e: e.wait_ge(c.sem.h, c.sem.n))
        with nc.Block() as blk0:
            p.play_init(blk0)
        with nc.Block() as blk:
            p.play(blk)
    return nc





def emit_l0moe(p, ctx, tctx, ic, c, cp, idf, idb, ptmp,
               x_d, xn_d, xnf_d, s0_d, rwT_d, rb_d, ewT_d, ebm_d,
               x1_d, xn1_d, xn1f_d, tag="m"):
    nc = p.nc
    pre_sem, pre_val = c.sem, c.sem.n

    rwf = ctx.enter_context(nc.sbuf_tensor(f"{tag}rwf", [128, 8 * NE], F32,
                                           side="right"))
    for k in range(8):
        ic.step("sync", lambda e, k=k: e.dma_start(
            rwf[:, k * NE:(k + 1) * NE], rwT_d[k * 128:(k + 1) * 128, :]),
            dma=True)

    sb = lambda n, s, dt: tctx.enter_context(nc.sbuf_tensor(n, s, dt))
    sbr = lambda n, s, dt: ctx.enter_context(
        nc.sbuf_tensor(n, s, dt, side="right"))
    rbb = sbr(f"{tag}rbb", [128, NE], F32)
    ic.step("sync", lambda e: e.dma_start(
        rbb[:], rb_d[0:1, :].partition_broadcast(128)[:, 0, :]), dma=True)
    ones512 = sbr(f"{tag}ones512", [1, 512], BF16)
    onesc = sbr(f"{tag}onesc", [128, 1], BF16)
    ic.step("gpsimd", lambda e: e.memset(ones512[:], 1.0))
    ic.step("gpsimd", lambda e: e.memset(onesc[:], 1.0))

    xnb = sb(f"{tag}xnb", [128, 8 * 512], BF16)
    xnfb = sb(f"{tag}xnfb", [128, 8 * 512], F32)
    ebe = sb(f"{tag}ebe", [1, D], BF16)
    yacc = sb(f"{tag}yacc", [128, 8 * SEQ], F32)
    ewb = sb(f"{tag}ewb", [128, 8 * D], BF16)
    lgf = sb(f"{tag}lgf", [128, NE], F32)
    exf = sb(f"{tag}exf", [128, NE], F32)
    iseq = sb(f"{tag}iseq", [128, NE], F32)
    maskt = sb(f"{tag}maskt", [128, NE], F32)
    cols = sb(f"{tag}cols", [128, 4], F32)
    maskb = sb(f"{tag}maskb", [128, 512], F32)
    tmpf = sb(f"{tag}tmpf", [128, 512], F32)
    xband = sb(f"{tag}xband", [128, 8 * 512], F32)
    sband = sb(f"{tag}sband", [128, 8 * 512], F32)
    gtile = sb(f"{tag}gtile", [128, 512], F32)
    x1bb = sb(f"{tag}x1bb", [128, 8 * 512], BF16)
    xo = sb(f"{tag}xo", [128, 8 * 512], BF16)
    sqb = sb(f"{tag}sqb", [128, 512], BF16)
    stat = sb(f"{tag}stat", [1, 7 * 512], F32)
    rsb = sb(f"{tag}rsb", [128, 512], F32)
    mbb = sb(f"{tag}mbb", [128, 512], F32)

    pC = [tctx.enter_context(nc.psum_tensor(f"{tag}pC{i}", [128, 512], F32))
          for i in range(2)]
    pA = pC[0]
    pS = tctx.enter_context(nc.psum_tensor(f"{tag}pS", [1, 512], F32))
    pQ = tctx.enter_context(nc.psum_tensor(f"{tag}pQ", [1, 512], F32))
    cem_d = nc.dram_tensor(f"{tag}cem", [NE, T], F32, kind="Internal").ap()
    rows_d = nc.dram_tensor(f"{tag}rows", [2, T], F32, kind="Internal").ap()

    neg = cols[:, 0:1]
    mx = cols[:, 1:2]
    ssum = cols[:, 2:3]
    rec = cols[:, 3:4]

    for tt in range(8):
        tcol = tt * 512
        c.step("sync", lambda e, tcol=tcol: e.dma_start(
            xnfb[:].rearrange("p (c t) -> p c t", t=512),
            fm(xnf_d, 0, tcol, 512, 8)), dma=True)
        for ts in range(4):
            def mmr(e, ts=ts):
                for k in range(8):
                    i = e.matmul(pA[:, 0:NE],
                                 xnfb[:, k * 512 + ts * 128: k * 512 + ts * 128 + 128],
                                 rwf[:, k * NE:(k + 1) * NE],
                                 start=(k == 0), stop=(k == 7))
                return i
            c.step("tensor", mmr)
            c.step("vector", lambda e: e.tensor_tensor(
                lgf[:], pA[:, 0:NE], rbb[:], ALU.add))
            c.step("vector", lambda e: e.reduce_max(mx, lgf[:], axis=AX))
            c.step("vector", lambda e: e.reduce_max(neg, lgf[:], axis=AX,
                                                    negate=True))
            c.step("scalar", lambda e: e.activation(
                exf[:], lgf[:], AF.Exp, bias=neg, accum_out=ssum))
            c.step("vector", lambda e: e.reciprocal(rec, ssum))
            c.step("vector", lambda e: e.tensor_scalar(
                iseq[:], lgf[:], mx, 0.0, ALU.subtract, ALU.is_ge))
            c.step("vector", lambda e: e.tensor_scalar(
                maskt[:], iseq[:], rec, None, ALU.mult))
            c.step("tensor", lambda e: e.transpose(
                ptmp[0:NE, :], maskt[:], idf[:]))
            c.step("vector", lambda e: e.tensor_copy(tmpf[0:NE, 0:128],
                                                     ptmp[0:NE, :]))
            c.step("sync", lambda e, tcol=tcol, ts=ts: e.dma_start(
                cem_d[:, tcol + ts * 128: tcol + ts * 128 + 128],
                tmpf[0:NE, 0:128]), dma=True)

    router_end = c.sem.n
    router_sem = c.sem
    s_wc = Sem(ctx, nc, f"{tag}swc")
    s_x = Sem(ctx, nc, f"{tag}sx")
    s_mk2 = Sem(ctx, nc, f"{tag}smk2")
    s_mm = Sem(ctx, nc, f"{tag}smm")
    s_tt2 = Sem(ctx, nc, f"{tag}stt2")

    for half in range(2):
        Tb = half * SEQ
        comb_gate = (c.sem, c.sem.n)
        for e_i in range(NE):
            w_wait = [W_(s_mm, None)]
            if half == 0 and e_i == 0:
                w_wait = [W_(pre_sem, pre_val)]

            def wload(e, e_i=e_i):
                return e.dma_start(
                    ewb[:].rearrange("p (c n) -> p c n", n=D),
                    ewT_d[e_i].rearrange("(c p) n -> p c n", p=128))
            p.do("gpsimd", wload, inc=(s_wc, 16), wait=w_wait)
            p.do("gpsimd", lambda e, e_i=e_i: e.dma_start(
                ebe[:], ebm_d[e_i * 8:(e_i + 1) * 8, :]
                .rearrange("m p -> (m p)").unsqueeze(0)), inc=(s_wc, 16))
            wc_e = s_wc.n
            for T_t in range(4):
                tcol = Tb + T_t * 512
                xw = [W_(s_mm, None)]
                if half == 0 and e_i == 0 and T_t == 0:
                    xw.append(W_(pre_sem, pre_val))
                p.do("sync", lambda e, tcol=tcol: e.dma_start(
                    xnb[:].rearrange("p (c t) -> p c t", t=512),
                    fm(xn_d, 0, tcol, 512, 8)), inc=(s_x, 16), wait=xw)
                p.do("sync", lambda e, e_i=e_i, tcol=tcol: e.dma_start(
                    maskb[:],
                    cem_d[e_i:e_i + 1, tcol:tcol + 512]
                    .partition_broadcast(128)[:, 0, :]),
                    inc=(s_mk2, 16),
                    wait=[W_(router_sem, router_end), W_(s_tt2, None)])
                x_t = s_x.n
                mk_t = s_mk2.n
                mm_base = s_mm.n
                tt_base = s_tt2.n
                for fc in range(8):
                    tw = [W_(s_wc, wc_e), W_(s_x, x_t),
                          W_(s_tt2, tt_base + (fc - 2) * 2 + 2)]
                    if half == 0 and e_i == 0 and T_t == 0 and fc < 2:
                        tw.append(W_(router_sem, router_end))

                    def mme(e, fc=fc, e_i=e_i):
                        for k in range(8):
                            e.matmul(pC[fc % 2][:],
                                     ewb[:, k * D + fc * 128: k * D + fc * 128 + 128],
                                     ch(xnb, 512, k), start=(k == 0), stop=False)
                        return e.matmul(
                            pC[fc % 2][:], ebe[0:1, fc * 128: fc * 128 + 128],
                            ones512[0:1, :], start=False, stop=True)
                    p.do("tensor", mme, inc=(s_mm, 1), wait=tw)
                    ydst = lambda fc=fc, T_t=T_t: yacc[:, fc * SEQ + T_t * 512:
                                                       fc * SEQ + T_t * 512 + 512]
                    vw = [W_(s_mm, mm_base + fc + 1), W_(s_mk2, mk_t)]
                    if e_i == 0:
                        vw.append(W_(comb_gate[0], comb_gate[1]))
                        p.do("vector", lambda e, fc=fc: e.tensor_tensor(
                            tmpf[:], pC[fc % 2][:], maskb[:], ALU.mult),
                            inc=(s_tt2, 1), wait=vw)
                        p.do("vector", lambda e, ydst=ydst: e.tensor_copy(
                            ydst(), tmpf[:]), inc=(s_tt2, 1))
                    else:
                        p.do("vector", lambda e, fc=fc: e.tensor_tensor(
                            tmpf[:], pC[fc % 2][:], maskb[:], ALU.mult),
                            inc=(s_tt2, 1), wait=vw)
                        p.do("vector", lambda e, ydst=ydst: e.tensor_tensor(
                            ydst(), ydst(), tmpf[:], ALU.add), inc=(s_tt2, 1))

        c.step("sync", lambda e: e.nop(), wait=[W_(s_tt2, None)])
        for T_t in range(4):
            tcol = Tb + T_t * 512
            c.step("sync", lambda e, tcol=tcol: e.dma_start(
                xband[:].rearrange("p (c t) -> p c t", t=512),
                fm(x_d, 0, tcol, 512, 8)), dma=True)
            c.step("sync", lambda e, tcol=tcol: e.dma_start(
                sband[:].rearrange("p (c t) -> p c t", t=512),
                fm(s0_d, 0, tcol, 512, 8)), dma=True)
            for fc in range(8):
                ysl = yacc[:, fc * SEQ + T_t * 512: fc * SEQ + T_t * 512 + 512]
                c.step("scalar", lambda e, ysl=ysl: e.activation(
                    gtile[:], ysl, AF.Silu))
                c.step("vector", lambda e, fc=fc: e.tensor_tensor(
                    gtile[:], gtile[:], ch(sband, 512, fc), ALU.mult))
                c.step("vector", lambda e, fc=fc: e.tensor_tensor(
                    ch(xband, 512, fc), ch(xband, 512, fc), gtile[:], ALU.add))
            c.step("sync", lambda e, tcol=tcol: e.dma_start(
                fm(x1_d, 0, tcol, 512, 8),
                xband[:].rearrange("p (c t) -> p c t", t=512)), dma=True)
            c.step("vector", lambda e: e.tensor_copy(x1bb[:], xband[:]))
            for fc in range(8):
                c.step("scalar", lambda e, fc=fc: e.activation(
                    sqb[:], ch(xband, 512, fc), AF.Square))
                def mmln(e, fc=fc):
                    e.matmul(pS[:], onesc[:], ch(x1bb, 512, fc),
                             start=(fc == 0), stop=(fc == 7))
                    return e.matmul(pQ[:], onesc[:], sqb[:],
                                    start=(fc == 0), stop=(fc == 7))
                c.step("tensor", mmln)
            mean = stat[0:1, 0:512]
            ex2 = stat[0:1, 512:1024]
            var = stat[0:1, 1024:1536]
            stdv = stat[0:1, 1536:2048]
            rs = stat[0:1, 2048:2560]
            mb = stat[0:1, 2560:3072]
            tmp1 = stat[0:1, 3072:3584]

            def mkst(e):
                e.tensor_scalar(mean, pS[:], 1.0 / D, None, ALU.mult)
                e.tensor_scalar(ex2, pQ[:], 1.0 / D, None, ALU.mult)
                e.tensor_tensor(tmp1, mean, mean, ALU.mult)
                return e.tensor_tensor(var, ex2, tmp1, ALU.subtract)
            c.step("vector", mkst)
            c.step("scalar", lambda e: e.activation(
                stdv, var, AF.Sqrt, bias=cp.col(1e-5, 1)))

            def mkrs(e):
                e.reciprocal(rs, stdv)
                return e.scalar_tensor_tensor(mb, mean, -1.0, rs,
                                              ALU.mult, ALU.mult)
            c.step("vector", mkrs)
            c.step("sync", lambda e, tcol=tcol: e.dma_start(
                rows_d[0:1, tcol:tcol + 512], rs), dma=True)
            c.step("sync", lambda e, tcol=tcol: e.dma_start(
                rows_d[1:2, tcol:tcol + 512], mb), dma=True)
            c.step("sync", lambda e, tcol=tcol: e.dma_start(
                rsb[:], rows_d[0:1, tcol:tcol + 512]
                .partition_broadcast(128)[:, 0, :]), dma=True)
            c.step("sync", lambda e, tcol=tcol: e.dma_start(
                mbb[:], rows_d[1:2, tcol:tcol + 512]
                .partition_broadcast(128)[:, 0, :]), dma=True)
            for fc in range(8):
                c.step("vector", lambda e, fc=fc: e.tensor_tensor(
                    ch(xband, 512, fc), ch(xband, 512, fc), rsb[:], ALU.mult))
            for fc in range(8):
                c.step("vector", lambda e, fc=fc: e.tensor_tensor(
                    ch(xband, 512, fc), ch(xband, 512, fc), mbb[:], ALU.add))
            c.step("vector", lambda e: e.tensor_copy(xo[:], xband[:]))
            c.step("sync", lambda e, tcol=tcol: e.dma_start(
                fm(xn1f_d, 0, tcol, 512, 8),
                xband[:].rearrange("p (c t) -> p c t", t=512)), dma=True)
            c.step("sync", lambda e, tcol=tcol: e.dma_start(
                fm(xn1_d, 0, tcol, 512, 8),
                xo[:].rearrange("p (c t) -> p c t", t=512)), dma=True)
    return c


def build_l0moe():
    nc = bass.Bass("TRN2", target_bir_lowering=False)
    x_d = nc.dram_tensor("x", [D, T], F32, kind="ExternalInput").ap()
    xn_d = nc.dram_tensor("xn", [D, T], BF16, kind="ExternalInput").ap()
    xnf_d = nc.dram_tensor("xnf", [D, T], F32, kind="ExternalInput").ap()
    s0_d = nc.dram_tensor("s0", [D, T], F32, kind="ExternalInput").ap()
    rwT_d = nc.dram_tensor("rwT", [D, NE], F32, kind="ExternalInput").ap()
    rb_d = nc.dram_tensor("rb", [1, NE], F32, kind="ExternalInput").ap()
    ewT_d = nc.dram_tensor("ewT", [NE, D, D], F32, kind="ExternalInput").ap()
    ebm_d = nc.dram_tensor("ebm", [64, 128], F32, kind="ExternalInput").ap()
    x1_d = nc.dram_tensor("x1", [D, T], F32, kind="ExternalOutput").ap()
    xn1_d = nc.dram_tensor("xn1", [D, T], BF16, kind="ExternalOutput").ap()
    xn1f_d = nc.dram_tensor("xn1f", [D, T], F32, kind="ExternalOutput").ap()
    ctx = ExitStack()
    with ctx:
        p = Prog(nc, ctx)
        ic = Chain(p, ctx, "sic", init=True)
        c = Chain(p, ctx, "sc")
        cp = ConstPool(p, ctx, [1e-5, 0.125], ic)
        idf, idb = ini_identities(p, ctx, ic)
        ptmp = ctx.enter_context(nc.psum_tensor("ptmp", [128, 128], F32))
        emit_l0moe(p, ctx, ctx, ic, c, cp, idf, idb, ptmp,
                   x_d, xn_d, xnf_d, s0_d, rwT_d, rb_d, ewT_d, ebm_d,
                   x1_d, xn1_d, xn1f_d)
        p.do("sync", lambda e: e.wait_ge(c.sem.h, c.sem.n))
        with nc.Block() as blk0:
            p.play_init(blk0)
        with nc.Block() as blk:
            p.play(blk)
    return nc





def emit_l1pre(p, ctx, tctx, ic, c, cp, idf, idb, ptmp,
               xn1_d, winT_d, wconv_d, bconvm_d, wxT_d, wdtT_d, bdtm_d,
               res_d, xsp_d, xspb_d, bfull_d, dt_d, tag="d"):
    nc = p.nc
    pre_sem, pre_val = c.sem, c.sem.n
    bconvT = ini_biasT(p, ctx, ic, bconvm_d, 16, ptmp, idf, f"{tag}bc")
    bdtT = ini_biasT(p, ctx, ic, bdtm_d, 16, ptmp, idf, f"{tag}bd")
    wxb = ini_weight_bf16(p, ctx, ic, wxT_d, DI, 32, f"{tag}wx",
                          side="right")

    sb = lambda n, s, dt: tctx.enter_context(nc.sbuf_tensor(n, s, dt))
    wcv = ctx.enter_context(nc.sbuf_tensor(f"{tag}wcv", [128, 16 * 4], F32,
                                           side="right"))
    for ci in range(16):
        ic.step("sync", lambda e, ci=ci: e.dma_start(
            wcv[:, ci * 4:(ci + 1) * 4],
            wconv_d[ci * 128:(ci + 1) * 128, :]), dma=True)

    wband = sb(f"{tag}wband", [128, 32768], BF16)
    xn1b = sb(f"{tag}xn1b", [128, 8 * 512], BF16)
    ot = [sb(f"{tag}ot{i}", [128, 512], F32) for i in range(2)]
    xpad = sb(f"{tag}xpad", [128, SEQ + 3], F32)
    cvt = sb(f"{tag}cvt", [128, SEQ], F32)
    xst = sb(f"{tag}xst", [128, SEQ], F32)
    xsb = sb(f"{tag}xsb", [128, SEQ], BF16)
    spband = sb(f"{tag}spband", [128, 16 * 512], BF16)
    sp1 = [sb(f"{tag}sp1{i}", [128, 512], F32) for i in range(2)]
    sp2 = [sb(f"{tag}sp2{i}", [128, 512], F32) for i in range(2)]
    bx = sb(f"{tag}bx", [32, 512], F32)

    pD = [tctx.enter_context(nc.psum_tensor(f"{tag}pD{i}", [128, 512], F32))
          for i in range(2)]
    pX = tctx.enter_context(nc.psum_tensor(f"{tag}pX", [32, 512], F32))
    xsr_d = nc.dram_tensor(f"{tag}xsr", [DI, T], F32, kind="Internal").ap()

    s_wc = Sem(ctx, nc, f"{tag}swc")
    s_x = Sem(ctx, nc, f"{tag}sx")
    s_mm = Sem(ctx, nc, f"{tag}smm")
    s_ev = Sem(ctx, nc, f"{tag}sev")
    s_st = Sem(ctx, nc, f"{tag}sst")
    s_sp = Sem(ctx, nc, f"{tag}ssp")
    s_tv = Sem(ctx, nc, f"{tag}stv")
    s_st2 = Sem(ctx, nc, f"{tag}sst2")
    s_mx = Sem(ctx, nc, f"{tag}smx")
    s_stb = Sem(ctx, nc, f"{tag}sstb")

    p.do("gpsimd", lambda e: e.dma_start(
        wband[:].rearrange("p (c n) -> p c n", n=4096),
        winT_d.rearrange("(c p) n -> p c n", p=128)),
        inc=(s_wc, 16), wait=[W_(pre_sem, pre_val)])
    wc1 = s_wc.n

    for tt in range(8):
        tcol = tt * 512
        xw = [W_(s_mm, None)]
        if tt == 0:
            xw.append(W_(pre_sem, pre_val))
        p.do("sync", lambda e, tcol=tcol: e.dma_start(
            xn1b[:].rearrange("p (c t) -> p c t", t=512),
            fm(xn1_d, 0, tcol, 512, 8)), inc=(s_x, 16), wait=xw)
        x_t = s_x.n
        ev_base = s_ev.n
        mm_base = s_mm.n
        st_base = s_st.n
        for fc in range(32):
            def mmw(e, fc=fc):
                for k in range(8):
                    i = e.matmul(pD[fc % 2][:],
                                 wband[:, k * 4096 + fc * 128:
                                       k * 4096 + fc * 128 + 128],
                                 ch(xn1b, 512, k), start=(k == 0), stop=(k == 7))
                return i
            p.do("tensor", mmw, inc=(s_mm, 1),
                 wait=[W_(s_wc, wc1), W_(s_x, x_t),
                       W_(s_ev, ev_base + fc - 1)])
            p.do("scalar", lambda e, fc=fc: e.activation(
                ot[fc % 2][:], pD[fc % 2][:], AF.Identity),
                inc=(s_ev, 1),
                wait=[W_(s_mm, mm_base + fc + 1),
                      W_(s_st, st_base + 16 * (fc - 1))])

            def stw(e, fc=fc, tcol=tcol):
                if fc < 16:
                    return e.dma_start(
                        xsr_d[fc * 128:(fc + 1) * 128, tcol:tcol + 512],
                        ot[fc % 2][:])
                return e.dma_start(
                    res_d[(fc - 16) * 128:(fc - 15) * 128, tcol:tcol + 512],
                    ot[fc % 2][:])
            p.do("sync", stw, inc=(s_st, 16),
                 wait=[W_(s_ev, ev_base + fc + 1)])

    c.step("sync", lambda e: e.nop(), wait=[W_(s_st, None)])
    for ci in range(16):
        for b in range(BPC):
            tb = b * SEQ
            c.step("gpsimd", lambda e: e.memset(xpad[:, 0:3], 0.0))
            c.step("sync", lambda e, ci=ci, tb=tb: e.dma_start(
                xpad[:, 3:SEQ + 3],
                xsr_d[ci * 128:(ci + 1) * 128, tb:tb + SEQ]), dma=True)

            def cv(e, ci=ci):
                e.tensor_scalar(cvt[:], xpad[:, 0:SEQ],
                                wcv[:, ci * 4:ci * 4 + 1], None, ALU.mult)
                e.scalar_tensor_tensor(cvt[:], xpad[:, 1:SEQ + 1],
                                       wcv[:, ci * 4 + 1:ci * 4 + 2], cvt[:],
                                       ALU.mult, ALU.add)
                e.scalar_tensor_tensor(cvt[:], xpad[:, 2:SEQ + 2],
                                       wcv[:, ci * 4 + 2:ci * 4 + 3], cvt[:],
                                       ALU.mult, ALU.add)
                return e.scalar_tensor_tensor(
                    cvt[:], xpad[:, 3:SEQ + 3],
                    wcv[:, ci * 4 + 3:ci * 4 + 4], cvt[:], ALU.mult, ALU.add)
            c.step("vector", cv)
            c.step("scalar", lambda e, ci=ci: e.activation(
                xst[:], cvt[:], AF.Silu, bias=bconvT[:, ci:ci + 1]))
            c.step("vector", lambda e: e.tensor_copy(xsb[:], xst[:]))
            c.step("sync", lambda e, ci=ci, tb=tb: e.dma_start(
                xsp_d[ci * 128:(ci + 1) * 128, tb:tb + SEQ], xst[:]), dma=True)
            c.step("sync", lambda e, ci=ci, tb=tb: e.dma_start(
                xspb_d[ci * 128:(ci + 1) * 128, tb:tb + SEQ], xsb[:]), dma=True)
    conv_sem, conv_end = c.sem, c.sem.n

    p.do("gpsimd", lambda e: e.dma_start(
        wband[:].rearrange("p (c n) -> p c n", n=2048),
        wdtT_d.rearrange("(c p) n -> p c n", p=128)),
        inc=(s_wc, 16), wait=[W_(s_mm, None)])
    wc2 = s_wc.n

    for tt in range(8):
        tcol = tt * 512
        xw = [W_(s_mm, None), W_(s_mx, None)]
        if tt == 0:
            xw.append(W_(conv_sem, conv_end))
        p.do("sync", lambda e, tcol=tcol: e.dma_start(
            spband[:].rearrange("p (c t) -> p c t", t=512),
            fm(xspb_d, 0, tcol, 512, 16)), inc=(s_x, 16), wait=xw)
        x_t = s_x.n

        def mmx(e):
            for k in range(16):
                i = e.matmul(pX[:], ch(wxb, 32, k), ch(spband, 512, k),
                             start=(k == 0), stop=(k == 15))
            return i
        p.do("tensor", mmx, inc=(s_mx, 1),
             wait=[W_(s_x, x_t), W_(s_stb, None)])
        p.do("vector", lambda e: e.tensor_copy(bx[:], pX[:]),
             inc=(s_mx, 1), wait=[W_(s_mx, None)])
        p.do("sync", lambda e, tcol=tcol: e.dma_start(
            bfull_d[:, tcol:tcol + 512], bx[:]),
            inc=(s_stb, 16), wait=[W_(s_mx, None)])

        sp_base = s_sp.n
        tv_base = s_tv.n
        st2_base = s_st2.n
        mm_base = s_mm.n
        for fc in range(16):
            def mmd(e, fc=fc):
                for k in range(16):
                    i = e.matmul(pD[fc % 2][:],
                                 wband[:, k * 2048 + fc * 128:
                                       k * 2048 + fc * 128 + 128],
                                 ch(spband, 512, k), start=(k == 0),
                                 stop=(k == 15))
                return i
            p.do("tensor", mmd, inc=(s_mm, 1),
                 wait=[W_(s_wc, wc2), W_(s_x, x_t),
                       W_(s_sp, sp_base + fc - 1)])

            def soft(e, fc=fc):
                e.activation(sp1[fc % 2][:], pD[fc % 2][:], AF.Abs,
                             bias=bdtT[:, fc:fc + 1])
                e.activation(sp1[fc % 2][:], sp1[fc % 2][:], AF.Exp,
                             scale=cp.col(-1.0))
                e.activation(sp1[fc % 2][:], sp1[fc % 2][:], AF.Ln,
                             bias=cp.col(1.0))
                return e.activation(sp2[fc % 2][:], pD[fc % 2][:], AF.Relu,
                                    bias=bdtT[:, fc:fc + 1])
            p.do("scalar", soft, inc=(s_sp, 1),
                 wait=[W_(s_mm, mm_base + fc + 1),
                       W_(s_tv, tv_base + fc - 1),
                       W_(s_st2, st2_base + 16 * (fc - 1))])
            p.do("vector", lambda e, fc=fc: e.tensor_tensor(
                sp2[fc % 2][:], sp2[fc % 2][:], sp1[fc % 2][:], ALU.add),
                inc=(s_tv, 1), wait=[W_(s_sp, sp_base + fc + 1)])
            p.do("sync", lambda e, fc=fc, tcol=tcol: e.dma_start(
                dt_d[fc * 128:(fc + 1) * 128, tcol:tcol + 512],
                sp2[fc % 2][:]), inc=(s_st2, 16),
                wait=[W_(s_tv, tv_base + fc + 1)])

    c.step("sync", lambda e: e.nop(),
           wait=[W_(s_st2, None), W_(s_stb, None)])
    return c


def build_l1pre():
    nc = bass.Bass("TRN2", target_bir_lowering=False)
    xn1_d = nc.dram_tensor("xn1", [D, T], BF16, kind="ExternalInput").ap()
    winT_d = nc.dram_tensor("winT", [D, 2 * DI], F32, kind="ExternalInput").ap()
    wconv_d = nc.dram_tensor("wconv", [DI, 4], F32, kind="ExternalInput").ap()
    bconvm_d = nc.dram_tensor("bconvm", [16, 128], F32, kind="ExternalInput").ap()
    wxT_d = nc.dram_tensor("wxT", [DI, 32], F32, kind="ExternalInput").ap()
    wdtT_d = nc.dram_tensor("wdtT", [DI, DI], F32, kind="ExternalInput").ap()
    bdtm_d = nc.dram_tensor("bdtm", [16, 128], F32, kind="ExternalInput").ap()
    res_d = nc.dram_tensor("res", [DI, T], F32, kind="ExternalOutput").ap()
    xsp_d = nc.dram_tensor("xsp", [DI, T], F32, kind="ExternalOutput").ap()
    xspb_d = nc.dram_tensor("xspb", [DI, T], BF16, kind="ExternalOutput").ap()
    bfull_d = nc.dram_tensor("bfull", [32, T], F32, kind="ExternalOutput").ap()
    dt_d = nc.dram_tensor("dt", [DI, T], F32, kind="ExternalOutput").ap()
    ctx = ExitStack()
    with ctx:
        p = Prog(nc, ctx)
        ic = Chain(p, ctx, "sic", init=True)
        c = Chain(p, ctx, "sc")
        cp = ConstPool(p, ctx, [1e-5, 0.125, -1.0, 1.0], ic)
        idf, idb = ini_identities(p, ctx, ic)
        ptmp = ctx.enter_context(nc.psum_tensor("ptmp", [128, 128], F32))
        emit_l1pre(p, ctx, ctx, ic, c, cp, idf, idb, ptmp,
                   xn1_d, winT_d, wconv_d, bconvm_d, wxT_d, wdtT_d, bdtm_d,
                   res_d, xsp_d, xspb_d, bfull_d, dt_d)
        p.do("sync", lambda e: e.wait_ge(c.sem.h, c.sem.n))
        with nc.Block() as blk0:
            p.play_init(blk0)
        with nc.Block() as blk:
            p.play(blk)
    return nc





def emit_scan(p, ctx, tctx, ic, c, cp, idf, idb, ptmp,
              xsp_d, res_d, bfull_d, dt_d, alog_d, dm_d, zb_d,
              y_dbg_d=None, tag="s"):
    nc = p.nc
    Dt = ini_biasT(p, ctx, ic, dm_d, 16, ptmp, idf, f"{tag}D")

    sb = lambda n, s, dt: tctx.enter_context(nc.sbuf_tensor(n, s, dt))
    Af = ctx.enter_context(nc.sbuf_tensor(f"{tag}Af", [128, 16 * 16], F32,
                                          side="right"))
    with nc.sbuf_tensor(f"{tag}astg", [128, 16], F32, side="right") as astg:
        for ci in range(16):
            ic.step("sync", lambda e, ci=ci: e.dma_start(
                astg[:], alog_d[ci * 128:(ci + 1) * 128, :]), dma=True)
            ic.step("scalar", lambda e, ci=ci: e.activation(
                Af[:, ci * 16:(ci + 1) * 16], astg[:], AF.Exp))
            ic.step("vector", lambda e, ci=ci: e.tensor_scalar(
                Af[:, ci * 16:(ci + 1) * 16], Af[:, ci * 16:(ci + 1) * 16],
                -1.0, None, ALU.mult))

    dtb = sb(f"{tag}dtb", [128, 4 * SEQ], F32)
    xpb = sb(f"{tag}xpb", [128, 4 * SEQ], F32)
    ub = sb(f"{tag}ub", [128, 4 * SEQ], F32)
    yb = sb(f"{tag}yb", [128, 4 * SEQ], F32)
    at = sb(f"{tag}at", [128, SEQ], F32)
    bt = sb(f"{tag}bt", [128, SEQ], F32)
    ht = sb(f"{tag}ht", [128, SEQ], F32)
    Bb = sb(f"{tag}Bb", [128, SEQ], F32)
    rt = sb(f"{tag}rt", [128, SEQ], F32)
    zt = sb(f"{tag}zt", [128, SEQ], BF16)

    for b in range(BPC):
        tb = b * SEQ
        for cg in range(4):
            c.step("sync", lambda e, cg=cg, tb=tb: e.dma_start(
                dtb[:].rearrange("p (c t) -> p c t", t=SEQ),
                fm(dt_d, cg * 4, tb, SEQ, 4)), dma=True)
            c.step("sync", lambda e, cg=cg, tb=tb: e.dma_start(
                xpb[:].rearrange("p (c t) -> p c t", t=SEQ),
                fm(xsp_d, cg * 4, tb, SEQ, 4)), dma=True)
            c.step("vector", lambda e: e.tensor_tensor(
                ub[:], dtb[:], xpb[:], ALU.mult))
            for si in range(DS):
                c.step("sync", lambda e, si=si, tb=tb: e.dma_start(
                    Bb[:], bfull_d[16 + si:17 + si, tb:tb + SEQ]
                    .partition_broadcast(128)[:, 0, :]), dma=True)
                for cc in range(4):
                    ci = cg * 4 + cc
                    c.step("scalar", lambda e, cc=cc, ci=ci, si=si:
                           e.activation(at[:], ch(dtb, SEQ, cc), AF.Exp,
                                        scale=Af[:, ci * 16 + si:
                                                 ci * 16 + si + 1]))
                    c.step("vector", lambda e, cc=cc: e.tensor_tensor(
                        bt[:], ch(ub, SEQ, cc), Bb[:], ALU.mult))
                    c.step("vector", lambda e: e.tensor_tensor_scan(
                        ht[:], at[:], bt[:], 0.0, ALU.mult, ALU.add))
                    if si == 0:
                        c.step("vector", lambda e, cc=cc: e.tensor_copy(
                            ch(yb, SEQ, cc), ht[:]))
                    else:
                        c.step("vector", lambda e, cc=cc: e.tensor_tensor(
                            ch(yb, SEQ, cc), ch(yb, SEQ, cc), ht[:], ALU.add))
            for cc in range(4):
                ci = cg * 4 + cc
                c.step("vector", lambda e, cc=cc, ci=ci:
                       e.scalar_tensor_tensor(
                           at[:], ch(xpb, SEQ, cc), Dt[:, ci:ci + 1],
                           ch(yb, SEQ, cc), ALU.mult, ALU.add))
                if y_dbg_d is not None:
                    c.step("sync", lambda e, ci=ci, tb=tb: e.dma_start(
                        y_dbg_d[ci * 128:(ci + 1) * 128, tb:tb + SEQ], at[:]),
                        dma=True)
                c.step("sync", lambda e, ci=ci, tb=tb: e.dma_start(
                    rt[:], res_d[ci * 128:(ci + 1) * 128, tb:tb + SEQ]),
                    dma=True)
                c.step("scalar", lambda e: e.activation(rt[:], rt[:], AF.Silu))
                c.step("vector", lambda e: e.tensor_tensor(
                    zt[:], at[:], rt[:], ALU.mult))
                c.step("sync", lambda e, ci=ci, tb=tb: e.dma_start(
                    zb_d[ci * 128:(ci + 1) * 128, tb:tb + SEQ], zt[:]),
                    dma=True)
    return c


def build_scan(debug=False):
    nc = bass.Bass("TRN2", target_bir_lowering=False)
    xsp_d = nc.dram_tensor("xsp", [DI, T], F32, kind="ExternalInput").ap()
    res_d = nc.dram_tensor("res", [DI, T], F32, kind="ExternalInput").ap()
    bfull_d = nc.dram_tensor("bfull", [32, T], F32, kind="ExternalInput").ap()
    dt_d = nc.dram_tensor("dt", [DI, T], F32, kind="ExternalInput").ap()
    alog_d = nc.dram_tensor("alog", [DI, 16], F32, kind="ExternalInput").ap()
    dm_d = nc.dram_tensor("dm", [16, 128], F32, kind="ExternalInput").ap()
    zb_d = nc.dram_tensor("zb", [DI, T], BF16, kind="ExternalOutput").ap()
    y_dbg_d = None
    if debug:
        y_dbg_d = nc.dram_tensor("ydbg", [DI, T], F32, kind="ExternalOutput").ap()
    ctx = ExitStack()
    with ctx:
        p = Prog(nc, ctx)
        ic = Chain(p, ctx, "sic", init=True)
        c = Chain(p, ctx, "sc")
        cp = ConstPool(p, ctx, [1e-5], ic)
        idf, idb = ini_identities(p, ctx, ic)
        ptmp = ctx.enter_context(nc.psum_tensor("ptmp", [128, 128], F32))
        emit_scan(p, ctx, ctx, ic, c, cp, idf, idb, ptmp,
                  xsp_d, res_d, bfull_d, dt_d, alog_d, dm_d, zb_d,
                  y_dbg_d=y_dbg_d)
        p.do("sync", lambda e: e.wait_ge(c.sem.h, c.sem.n))
        with nc.Block() as blk0:
            p.play_init(blk0)
        with nc.Block() as blk:
            p.play(blk)
    return nc





def emit_l1moe(p, ctx, tctx, ic, c, cp, idf, idb, ptmp,
               xn1_d, xn1f_d, gw_d, w1_d, b1m_d, w2_d, b2m_d, y1_d, tag="f",
               cem_out=None):
    nc = p.nc
    pre_sem, pre_val = c.sem, c.sem.n
    gwf = ctx.enter_context(nc.sbuf_tensor(f"{tag}gwf", [128, 8 * NE], F32,
                                           side="right"))
    for k in range(8):
        ic.step("sync", lambda e, k=k: e.dma_start(
            gwf[:, k * NE:(k + 1) * NE], gw_d[k * 128:(k + 1) * 128, :]),
            dma=True)
    b1T = []
    for e_i in range(NE):
        b1T.append(ini_biasT(p, ctx, ic, b1m_d[e_i * 32:(e_i + 1) * 32, :],
                             32, ptmp, idf, f"{tag}b1_{e_i}"))

    sb = lambda n, s, dt: tctx.enter_context(nc.sbuf_tensor(n, s, dt))
    ones512 = ctx.enter_context(nc.sbuf_tensor(f"{tag}ones512", [1, 512],
                                               BF16, side="right"))
    ic.step("gpsimd", lambda e: e.memset(ones512[:], 1.0))

    xn1b = sb(f"{tag}xn1b", [128, 8 * 512], BF16)
    w1b = sb(f"{tag}w1b", [128, 8 * FF], BF16)
    w2b = sb(f"{tag}w2b", [128, 32 * D], BF16)
    b2e = sb(f"{tag}b2e", [1, D], BF16)
    hbig = sb(f"{tag}hbig", [128, 8192], F32)
    hband = hbig[:].bitcast(BF16)
    yband = sb(f"{tag}yband", [128, 8 * 512], F32)
    maskb = sb(f"{tag}maskb", [128, 512], F32)
    tmpf = sb(f"{tag}tmpf", [128, 512], F32)
    lgf = sb(f"{tag}lgf", [128, NE], F32)
    lg2 = sb(f"{tag}lg2", [128, NE], F32)
    iseq1 = sb(f"{tag}iseq1", [128, NE], F32)
    iseq2 = sb(f"{tag}iseq2", [128, NE], F32)
    comb = sb(f"{tag}comb", [128, NE], F32)
    cols = sb(f"{tag}cols", [128, 8], F32)

    pA = [tctx.enter_context(nc.psum_tensor(f"{tag}pA{i}", [128, 512], F32))
          for i in range(2)]
    pY = [tctx.enter_context(nc.psum_tensor(f"{tag}pY{i}", [128, 512], F32))
          for i in range(2)]
    cem_d = nc.dram_tensor(f"{tag}cem", [NE, T], F32, kind="Internal").ap()

    mx1 = cols[:, 0:1]
    neg1 = cols[:, 1:2]
    mx2 = cols[:, 2:3]
    dd = cols[:, 3:4]
    den = cols[:, 4:5]
    c1 = cols[:, 5:6]
    c2 = cols[:, 6:7]

    xn1fb = hbig[:]
    for tt in range(8):
        tcol = tt * 512
        c.step("sync", lambda e, tcol=tcol: e.dma_start(
            xn1fb[:, 0:4096].rearrange("p (c t) -> p c t", t=512),
            fm(xn1f_d, 0, tcol, 512, 8)), dma=True)
        for ts in range(4):
            def mmg(e, ts=ts):
                for k in range(8):
                    i = e.matmul(pA[0][:, 0:NE],
                                 xn1fb[:, k * 512 + ts * 128:
                                       k * 512 + ts * 128 + 128],
                                 gwf[:, k * NE:(k + 1) * NE],
                                 start=(k == 0), stop=(k == 7))
                return i
            c.step("tensor", mmg)
            c.step("vector", lambda e: e.tensor_copy(lgf[:], pA[0][:, 0:NE]))
            c.step("vector", lambda e: e.reduce_max(mx1, lgf[:], axis=AX))
            c.step("vector", lambda e: e.reduce_max(neg1, lgf[:], axis=AX,
                                                    negate=True))
            c.step("vector", lambda e: e.tensor_scalar(
                iseq1[:], lgf[:], mx1, 0.0, ALU.subtract, ALU.is_ge))
            c.step("vector", lambda e: e.scalar_tensor_tensor(
                lg2[:], iseq1[:], -1e30, lgf[:], ALU.mult, ALU.add))
            c.step("vector", lambda e: e.reduce_max(mx2, lg2[:], axis=AX))
            c.step("scalar", lambda e: e.activation(
                dd, mx2, AF.Exp, bias=neg1))
            c.step("vector", lambda e: e.tensor_scalar(
                den, dd, 1.0, None, ALU.add))
            c.step("vector", lambda e: e.reciprocal(c1, den))
            c.step("vector", lambda e: e.tensor_tensor(c2, dd, c1, ALU.mult))
            c.step("vector", lambda e: e.tensor_scalar(
                iseq2[:], lg2[:], mx2, 0.0, ALU.subtract, ALU.is_ge))
            c.step("vector", lambda e: e.tensor_scalar(
                iseq2[:], iseq2[:], c2, None, ALU.mult))
            c.step("vector", lambda e: e.tensor_scalar(
                comb[:], iseq1[:], c1, None, ALU.mult))
            c.step("vector", lambda e: e.tensor_tensor(
                comb[:], comb[:], iseq2[:], ALU.add))
            c.step("tensor", lambda e: e.transpose(
                ptmp[0:NE, :], comb[:], idf[:]))
            c.step("vector", lambda e: e.tensor_copy(tmpf[0:NE, 0:128],
                                                     ptmp[0:NE, :]))
            c.step("sync", lambda e, tcol=tcol, ts=ts: e.dma_start(
                cem_d[:, tcol + ts * 128: tcol + ts * 128 + 128],
                tmpf[0:NE, 0:128]), dma=True)
    if cem_out is not None:
        for rr in range(NE):
            c.step("sync", lambda e, rr=rr: e.dma_start(
                cem_out[rr:rr + 1, :], cem_d[rr:rr + 1, :]), dma=True)
    gate_end = c.sem.n
    gate_sem = c.sem

    s_x = Sem(ctx, nc, f"{tag}sx")
    s_mk = Sem(ctx, nc, f"{tag}smk")
    s_yl = Sem(ctx, nc, f"{tag}syl")
    s_mh = Sem(ctx, nc, f"{tag}smh")
    s_g = Sem(ctx, nc, f"{tag}sg")
    s_my = Sem(ctx, nc, f"{tag}smy")
    s_tt = Sem(ctx, nc, f"{tag}stt")
    s_st = Sem(ctx, nc, f"{tag}sst")
    s_wc = Sem(ctx, nc, f"{tag}swc")

    for e_i in range(NE):
        w_wait = ([W_(s_mh, None), W_(s_my, None)] if e_i > 0
                  else [W_(pre_sem, pre_val)])

        def wload1(e, e_i=e_i):
            return e.dma_start(
                w1b[:].rearrange("p (c n) -> p c n", n=FF),
                w1_d[e_i].rearrange("(c p) n -> p c n", p=128))
        p.do("gpsimd", wload1, inc=(s_wc, 16), wait=w_wait)
        for hf in range(4):
            def wload2(e, e_i=e_i, hf=hf):
                return e.dma_start(
                    w2b[:, hf * 8 * D:(hf + 1) * 8 * D]
                    .rearrange("p (c n) -> p c n", n=D),
                    w2_d[e_i, hf * 8 * 128:(hf + 1) * 8 * 128, :]
                    .rearrange("(c p) n -> p c n", p=128))
            p.do("gpsimd", wload2, inc=(s_wc, 16),
                 wait=w_wait if hf == 0 else ())
        p.do("gpsimd", lambda e, e_i=e_i: e.dma_start(
            b2e[:], b2m_d[e_i * 8:(e_i + 1) * 8, :]
            .rearrange("m p -> (m p)").unsqueeze(0)), inc=(s_wc, 16),
            wait=w_wait)
        wc_e = s_wc.n

        for T_t in range(8):
            tcol = T_t * 512
            xw = [W_(s_mh, None)]
            if e_i == 0 and T_t == 0:
                xw.append(W_(pre_sem, pre_val))
            p.do("sync", lambda e, tcol=tcol: e.dma_start(
                xn1b[:].rearrange("p (c t) -> p c t", t=512),
                fm(xn1_d, 0, tcol, 512, 8)),
                inc=(s_x, 16), wait=xw)
            p.do("sync", lambda e, e_i=e_i, tcol=tcol: e.dma_start(
                maskb[:],
                cem_d[e_i:e_i + 1, tcol:tcol + 512]
                .partition_broadcast(128)[:, 0, :]),
                inc=(s_mk, 16),
                wait=[W_(gate_sem, gate_end), W_(s_tt, None)])
            if e_i > 0:
                p.do("sync", lambda e, tcol=tcol: e.dma_start(
                    yband[:].rearrange("p (c t) -> p c t", t=512),
                    fm(y1_d, 0, tcol, 512, 8)),
                    inc=(s_yl, 16), wait=[W_(s_st, None), W_(s_tt, None)])
            x_t = s_x.n
            mk_t = s_mk.n
            yl_t = s_yl.n
            my_prev = s_my.n
            g_base = s_g.n
            mh_base = s_mh.n
            first = (e_i == 0 and T_t == 0)
            for hc in range(32):
                tw = [W_(s_wc, wc_e), W_(s_x, x_t),
                      W_(s_g, g_base + hc - 1)]
                if first and hc < 2:
                    tw.append(W_(gate_sem, gate_end))

                def mmh(e, hc=hc):
                    for k in range(8):
                        i = e.matmul(pA[hc % 2][:],
                                     w1b[:, k * FF + hc * 128:
                                         k * FF + hc * 128 + 128],
                                     ch(xn1b, 512, k), start=(k == 0),
                                     stop=(k == 7))
                    return i
                p.do("tensor", mmh, inc=(s_mh, 1), wait=tw)
                gw_ = [W_(s_mh, mh_base + hc + 1)]
                if hc == 0:
                    gw_.append(W_(s_my, my_prev))
                    if first:
                        gw_.append(W_(gate_sem, gate_end))
                p.do("scalar", lambda e, hc=hc, e_i=e_i: e.activation(
                    hband[:, hc * 512:(hc + 1) * 512], pA[hc % 2][:], AF.Gelu,
                    bias=b1T[e_i][:, hc:hc + 1]), inc=(s_g, 1), wait=gw_)
            g_t = s_g.n
            my_base = s_my.n
            tt_base = s_tt.n
            for oc in range(8):
                tw = [W_(s_g, g_t), W_(s_tt, tt_base + oc - 1)]

                def mmy(e, oc=oc):
                    for k in range(32):
                        e.matmul(pY[oc % 2][:],
                                 w2b[:, k * D + oc * 128: k * D + oc * 128 + 128],
                                 hband[:, k * 512:(k + 1) * 512],
                                 start=(k == 0), stop=False)
                    return e.matmul(
                        pY[oc % 2][:], b2e[0:1, oc * 128: oc * 128 + 128],
                        ones512[0:1, :], start=False, stop=True)
                p.do("tensor", mmy, inc=(s_my, 1), wait=tw)
                vw = [W_(s_my, my_base + oc + 1), W_(s_mk, mk_t)]
                if e_i > 0:
                    vw.append(W_(s_yl, yl_t))
                else:
                    vw.append(W_(s_st, None))
                if e_i == 0:
                    p.do("vector", lambda e, oc=oc: e.tensor_tensor(
                        ch(yband, 512, oc), pY[oc % 2][:], maskb[:],
                        ALU.mult), inc=(s_tt, 1), wait=vw)
                else:
                    p.do("vector", lambda e, oc=oc: e.tensor_tensor(
                        tmpf[:], pY[oc % 2][:], maskb[:], ALU.mult), wait=vw)
                    p.do("vector", lambda e, oc=oc: e.tensor_tensor(
                        ch(yband, 512, oc), ch(yband, 512, oc), tmpf[:],
                        ALU.add), inc=(s_tt, 1))
            p.do("sync", lambda e, tcol=tcol: e.dma_start(
                fm(y1_d, 0, tcol, 512, 8),
                yband[:].rearrange("p (c t) -> p c t", t=512)),
                inc=(s_st, 16), wait=[W_(s_tt, None)])

    c.step("sync", lambda e: e.nop(), wait=[W_(s_st, None)])
    return c


def build_l1moe():
    nc = bass.Bass("TRN2", target_bir_lowering=False)
    xn1_d = nc.dram_tensor("xn1", [D, T], BF16, kind="ExternalInput").ap()
    xn1f_d = nc.dram_tensor("xn1f", [D, T], F32, kind="ExternalInput").ap()
    gw_d = nc.dram_tensor("gw", [D, NE], F32, kind="ExternalInput").ap()
    w1_d = nc.dram_tensor("w1", [NE, D, FF], F32, kind="ExternalInput").ap()
    b1m_d = nc.dram_tensor("b1m", [256, 128], F32, kind="ExternalInput").ap()
    w2_d = nc.dram_tensor("w2", [NE, FF, D], F32, kind="ExternalInput").ap()
    b2m_d = nc.dram_tensor("b2m", [64, 128], F32, kind="ExternalInput").ap()
    y1_d = nc.dram_tensor("y1", [D, T], F32, kind="ExternalOutput").ap()
    cemo_d = nc.dram_tensor("cemo", [NE, T], F32, kind="ExternalOutput").ap()
    ctx = ExitStack()
    with ctx:
        p = Prog(nc, ctx)
        ic = Chain(p, ctx, "sic", init=True)
        c = Chain(p, ctx, "sc")
        cp = ConstPool(p, ctx, [1e-5], ic)
        idf, idb = ini_identities(p, ctx, ic)
        ptmp = ctx.enter_context(nc.psum_tensor("ptmp", [128, 128], F32))
        emit_l1moe(p, ctx, ctx, ic, c, cp, idf, idb, ptmp,
                   xn1_d, xn1f_d, gw_d, w1_d, b1m_d, w2_d, b2m_d, y1_d,
                   cem_out=cemo_d)
        p.do("sync", lambda e: e.wait_ge(c.sem.h, c.sem.n))
        with nc.Block() as blk0:
            p.play_init(blk0)
        with nc.Block() as blk:
            p.play(blk)
    return nc





def emit_final(p, ctx, tctx, ic, c, cp, idf, idb, ptmp,
               x1_d, zb_d, y1_d, wo1T_d, out_d, tag="g"):
    nc = p.nc
    sb = lambda n, s, dt: tctx.enter_context(nc.sbuf_tensor(n, s, dt))
    wo1b = ini_weight_bf16(p, tctx, c, wo1T_d, DI, D, f"{tag}wo",
                           stage_side="left")
    zband = sb(f"{tag}zband", [128, 16 * 512], BF16)
    x1band = sb(f"{tag}x1band", [128, 8 * 512], F32)
    y1band = sb(f"{tag}y1band", [128, 8 * 512], F32)
    gt = sb(f"{tag}gt", [128, 512], F32)
    tt_ = sb(f"{tag}tt", [128, 512], F32)
    pA = tctx.enter_context(nc.psum_tensor(f"{tag}pA", [128, 512], F32))

    for tt in range(8):
        tcol = tt * 512
        c.step("sync", lambda e, tcol=tcol: e.dma_start(
            zband[:].rearrange("p (c t) -> p c t", t=512),
            fm(zb_d, 0, tcol, 512, 16)), dma=True)
        c.step("sync", lambda e, tcol=tcol: e.dma_start(
            x1band[:].rearrange("p (c t) -> p c t", t=512),
            fm(x1_d, 0, tcol, 512, 8)), dma=True)
        c.step("sync", lambda e, tcol=tcol: e.dma_start(
            y1band[:].rearrange("p (c t) -> p c t", t=512),
            fm(y1_d, 0, tcol, 512, 8)), dma=True)
        for fc in range(8):
            def mmf(e, fc=fc):
                for k in range(16):
                    i = e.matmul(pA[:],
                                 wo1b[:, k * D + fc * 128: k * D + fc * 128 + 128],
                                 ch(zband, 512, k), start=(k == 0),
                                 stop=(k == 15))
                return i
            c.step("tensor", mmf)
            c.step("scalar", lambda e, fc=fc: e.activation(
                gt[:], ch(y1band, 512, fc), AF.Silu))
            c.step("vector", lambda e: e.tensor_tensor(
                tt_[:], gt[:], pA[:], ALU.mult))
            c.step("vector", lambda e, fc=fc: e.tensor_tensor(
                ch(x1band, 512, fc), ch(x1band, 512, fc), tt_[:], ALU.add))
        c.step("sync", lambda e, tcol=tcol: e.dma_start(
            fm(out_d, 0, tcol, 512, 8),
            x1band[:].rearrange("p (c t) -> p c t", t=512)), dma=True)
    return c


def build_final():
    nc = bass.Bass("TRN2", target_bir_lowering=False)
    x1_d = nc.dram_tensor("x1", [D, T], F32, kind="ExternalInput").ap()
    zb_d = nc.dram_tensor("zb", [DI, T], BF16, kind="ExternalInput").ap()
    y1_d = nc.dram_tensor("y1", [D, T], F32, kind="ExternalInput").ap()
    wo1T_d = nc.dram_tensor("wo1T", [DI, D], F32, kind="ExternalInput").ap()
    out_d = nc.dram_tensor("out", [D, T], F32, kind="ExternalOutput").ap()
    ctx = ExitStack()
    with ctx:
        p = Prog(nc, ctx)
        ic = Chain(p, ctx, "sic", init=True)
        c = Chain(p, ctx, "sc")
        cp = ConstPool(p, ctx, [1e-5], ic)
        idf, idb = ini_identities(p, ctx, ic)
        ptmp = ctx.enter_context(nc.psum_tensor("ptmp", [128, 128], F32))
        emit_final(p, ctx, ctx, ic, c, cp, idf, idb, ptmp,
                   x1_d, zb_d, y1_d, wo1T_d, out_d)
        p.do("sync", lambda e: e.wait_ge(c.sem.h, c.sem.n))
        with nc.Block() as blk0:
            p.play_init(blk0)
        with nc.Block() as blk:
            p.play(blk)
    return nc



def build_ln():
    nc = bass.Bass("TRN2", target_bir_lowering=False)
    x_d = nc.dram_tensor("x", [D, T], F32, kind="ExternalInput").ap()
    xn_d = nc.dram_tensor("xn", [D, T], BF16, kind="ExternalOutput").ap()
    xnf_d = nc.dram_tensor("xnf", [D, T], F32, kind="ExternalOutput").ap()
    ctx = ExitStack()
    with ctx:
        p = Prog(nc, ctx)
        ic = Chain(p, ctx, "sic", init=True)
        cp = ConstPool(p, ctx, [1e-5], ic)
        s_out = emit_ln(p, ctx, ctx, ic, x_d, xn_d, D, "ln_", cp, xnf_d=xnf_d)
        p.do("sync", lambda e: e.wait_ge(s_out.h, s_out.n))
        with nc.Block() as blk0:
            p.play_init(blk0)
        with nc.Block() as blk:
            p.play(blk)
    return nc



def _f32c(a):
    return np.ascontiguousarray(np.asarray(a, dtype=np.float32))


def _prep_weights(inputs):
    w = {}
    w["wqkv"] = _f32c(inputs["L0_attn_Wqkv"])
    w["bqkvm"] = _f32c(np.asarray(inputs["L0_attn_bqkv"]).reshape(24, 128))
    w["wout0"] = _f32c(inputs["L0_attn_Wout"])
    w["boutm"] = _f32c(np.asarray(inputs["L0_attn_bout"]).reshape(8, 128))
    w["rwT"] = _f32c(np.asarray(inputs["L0_ffn_router_w"]).T)
    w["rb"] = _f32c(np.asarray(inputs["L0_ffn_router_b"]).reshape(1, NE))
    w["ewT"] = _f32c(np.transpose(np.asarray(inputs["L0_ffn_expert_w"]), (0, 2, 1)))
    w["ebm"] = _f32c(np.asarray(inputs["L0_ffn_expert_b"]).reshape(64, 128))
    w["winT"] = _f32c(np.asarray(inputs["L1_attn_W_in"]).T)
    w["wconv"] = _f32c(np.asarray(inputs["L1_attn_W_conv"]).reshape(DI, 4))
    w["bconvm"] = _f32c(np.asarray(inputs["L1_attn_b_conv"]).reshape(16, 128))
    w["wxT"] = _f32c(np.asarray(inputs["L1_attn_W_x"]).T)
    w["wdtT"] = _f32c(np.asarray(inputs["L1_attn_W_dt"]).T)
    w["bdtm"] = _f32c(np.asarray(inputs["L1_attn_b_dt"]).reshape(16, 128))
    w["wo1T"] = _f32c(np.asarray(inputs["L1_attn_W_out"]).T)
    w["alog"] = _f32c(inputs["L1_attn_A_log"])
    w["dm"] = _f32c(np.asarray(inputs["L1_attn_D"]).reshape(16, 128))
    w["gw"] = _f32c(inputs["L1_ffn_gate_w"])
    w["w1"] = _f32c(inputs["L1_ffn_w1"])
    w["b1m"] = _f32c(np.asarray(inputs["L1_ffn_b1"]).reshape(256, 128))
    w["w2"] = _f32c(inputs["L1_ffn_w2"])
    w["b2m"] = _f32c(np.asarray(inputs["L1_ffn_b2"]).reshape(64, 128))
    return w


def _run(nc, maps):
    res = run_bass_kernel_spmd(nc, maps, core_ids=list(range(NCORES)))
    return [{k: np.asarray(v) for k, v in r.items()} for r in res.results]





def build_fused(debug=False):
    nc = bass.Bass("TRN2", target_bir_lowering=False)
    E_IN = "ExternalInput"
    x_d = nc.dram_tensor("x", [D, T], F32, kind=E_IN).ap()
    wqkv_d = nc.dram_tensor("wqkv", [D, 3 * D], F32, kind=E_IN).ap()
    bqkvm_d = nc.dram_tensor("bqkvm", [24, 128], F32, kind=E_IN).ap()
    wout_d = nc.dram_tensor("wout0", [D, D], F32, kind=E_IN).ap()
    boutm_d = nc.dram_tensor("boutm", [8, 128], F32, kind=E_IN).ap()
    rwT_d = nc.dram_tensor("rwT", [D, NE], F32, kind=E_IN).ap()
    rb_d = nc.dram_tensor("rb", [1, NE], F32, kind=E_IN).ap()
    ewT_d = nc.dram_tensor("ewT", [NE, D, D], F32, kind=E_IN).ap()
    ebm_d = nc.dram_tensor("ebm", [64, 128], F32, kind=E_IN).ap()
    winT_d = nc.dram_tensor("winT", [D, 2 * DI], F32, kind=E_IN).ap()
    wconv_d = nc.dram_tensor("wconv", [DI, 4], F32, kind=E_IN).ap()
    bconvm_d = nc.dram_tensor("bconvm", [16, 128], F32, kind=E_IN).ap()
    wxT_d = nc.dram_tensor("wxT", [DI, 32], F32, kind=E_IN).ap()
    wdtT_d = nc.dram_tensor("wdtT", [DI, DI], F32, kind=E_IN).ap()
    bdtm_d = nc.dram_tensor("bdtm", [16, 128], F32, kind=E_IN).ap()
    wo1T_d = nc.dram_tensor("wo1T", [DI, D], F32, kind=E_IN).ap()
    alog_d = nc.dram_tensor("alog", [DI, 16], F32, kind=E_IN).ap()
    dm_d = nc.dram_tensor("dm", [16, 128], F32, kind=E_IN).ap()
    gw_d = nc.dram_tensor("gw", [D, NE], F32, kind=E_IN).ap()
    w1_d = nc.dram_tensor("w1", [NE, D, FF], F32, kind=E_IN).ap()
    b1m_d = nc.dram_tensor("b1m", [256, 128], F32, kind=E_IN).ap()
    w2_d = nc.dram_tensor("w2", [NE, FF, D], F32, kind=E_IN).ap()
    b2m_d = nc.dram_tensor("b2m", [64, 128], F32, kind=E_IN).ap()
    out_d = nc.dram_tensor("out", [D, T], F32, kind="ExternalOutput").ap()

    I = "ExternalOutput" if debug else "Internal"
    xn_d = nc.dram_tensor("i_xn", [D, T], BF16, kind=I).ap()
    xnf_d = nc.dram_tensor("i_xnf", [D, T], F32, kind=I).ap()
    s0_d = nc.dram_tensor("i_s0", [D, T], F32, kind=I).ap()
    x1_d = nc.dram_tensor("i_x1", [D, T], F32, kind=I).ap()
    xn1_d = nc.dram_tensor("i_xn1", [D, T], BF16, kind=I).ap()
    xn1f_d = nc.dram_tensor("i_xn1f", [D, T], F32, kind=I).ap()
    res_d = nc.dram_tensor("i_res", [DI, T], F32, kind=I).ap()
    xsp_d = nc.dram_tensor("i_xsp", [DI, T], F32, kind=I).ap()
    xspb_d = nc.dram_tensor("i_xspb", [DI, T], BF16, kind=I).ap()
    bfull_d = nc.dram_tensor("i_bfull", [32, T], F32, kind=I).ap()
    dt_d = nc.dram_tensor("i_dt", [DI, T], F32, kind=I).ap()
    zb_d = nc.dram_tensor("i_zb", [DI, T], BF16, kind=I).ap()
    y1_d = nc.dram_tensor("i_y1", [D, T], F32, kind=I).ap()

    cemo_d = None
    if debug:
        cemo_d = nc.dram_tensor("f_cemo", [NE, T], F32,
                                kind="ExternalOutput").ap()
    ctx = ExitStack()
    with ctx:
        p = Prog(nc, ctx)
        ic = Chain(p, ctx, "ic", init=True)
        c = Chain(p, ctx, "c")
        cp = ConstPool(p, ctx, [1e-5, 0.125, -1.0, 1.0], ic)
        idf, idb = ini_identities(p, ctx, ic)
        ptmp = ctx.enter_context(nc.psum_tensor("ptmp", [128, 128], F32))

        with ExitStack() as tA:
            sA = emit_ln(p, ctx, tA, ic, x_d, xn_d, D, "a_", cp, xnf_d=xnf_d)
            c.step("sync", lambda e: e.nop(), wait=[W_(sA, None)])
        with ExitStack() as tB:
            emit_l0attn(p, ctx, tB, ic, c, cp, idf, idb, ptmp,
                        xn_d, wqkv_d, bqkvm_d, wout_d, boutm_d, s0_d, tag="b")
        with ExitStack() as tC:
            emit_l0moe(p, ctx, tC, ic, c, cp, idf, idb, ptmp,
                       x_d, xn_d, xnf_d, s0_d, rwT_d, rb_d, ewT_d, ebm_d,
                       x1_d, xn1_d, xn1f_d, tag="m")
        with ExitStack() as tD:
            emit_l1pre(p, ctx, tD, ic, c, cp, idf, idb, ptmp,
                       xn1_d, winT_d, wconv_d, bconvm_d, wxT_d, wdtT_d,
                       bdtm_d, res_d, xsp_d, xspb_d, bfull_d, dt_d, tag="d")
        with ExitStack() as tE:
            emit_scan(p, ctx, tE, ic, c, cp, idf, idb, ptmp,
                      xsp_d, res_d, bfull_d, dt_d, alog_d, dm_d, zb_d, tag="s")
        with ExitStack() as tF:
            emit_l1moe(p, ctx, tF, ic, c, cp, idf, idb, ptmp,
                       xn1_d, xn1f_d, gw_d, w1_d, b1m_d, w2_d, b2m_d, y1_d,
                       tag="f", cem_out=cemo_d)
        with ExitStack() as tG:
            emit_final(p, ctx, tG, ic, c, cp, idf, idb, ptmp,
                       x1_d, zb_d, y1_d, wo1T_d, out_d, tag="g")

        p.do("sync", lambda e: e.wait_ge(c.sem.h, c.sem.n))
        with nc.Block() as blk0:
            p.play_init(blk0)
        with nc.Block() as blk:
            p.play(blk)
    return nc


def kernel(**inputs):
    w = _prep_weights(inputs)
    x = np.asarray(inputs["L0_attn_hidden_states"], dtype=np.float32)
    xT = [np.ascontiguousarray(x[BPC * c: BPC * (c + 1)].reshape(T, D).T)
          for c in range(NCORES)]
    nc = build_fused()
    maps = [dict(w, x=xT[c]) for c in range(NCORES)]
    res = run_bass_kernel_spmd(nc, maps, core_ids=list(range(NCORES)))
    out = np.empty((NCORES * BPC, SEQ, D), np.float32)
    for c in range(NCORES):
        out[BPC * c: BPC * (c + 1)] = \
            np.asarray(res.results[c]["out"]).T.reshape(BPC, SEQ, D)
    return out


def kernel_v0(**inputs):
    w = _prep_weights(inputs)
    x = np.asarray(inputs["L0_attn_hidden_states"], dtype=np.float32)
    xT = [np.ascontiguousarray(x[BPC * c: BPC * (c + 1)].reshape(T, D).T)
          for c in range(NCORES)]

    rA = _run(build_ln(), [{"x": xT[c]} for c in range(NCORES)])

    rB = _run(build_l0attn(), [
        {"xn": rA[c]["xn"], "wqkv": w["wqkv"], "bqkvm": w["bqkvm"],
         "wout0": w["wout0"], "boutm": w["boutm"]} for c in range(NCORES)])

    rC = _run(build_l0moe(), [
        {"x": xT[c], "xn": rA[c]["xn"], "xnf": rA[c]["xnf"],
         "s0": rB[c]["s0"], "rwT": w["rwT"], "rb": w["rb"],
         "ewT": w["ewT"], "ebm": w["ebm"]} for c in range(NCORES)])

    rD = _run(build_l1pre(), [
        {"xn1": rC[c]["xn1"], "winT": w["winT"], "wconv": w["wconv"],
         "bconvm": w["bconvm"], "wxT": w["wxT"], "wdtT": w["wdtT"],
         "bdtm": w["bdtm"]} for c in range(NCORES)])

    rE = _run(build_scan(), [
        {"xsp": rD[c]["xsp"], "res": rD[c]["res"], "bfull": rD[c]["bfull"],
         "dt": rD[c]["dt"], "alog": w["alog"], "dm": w["dm"]}
        for c in range(NCORES)])

    rF = _run(build_l1moe(), [
        {"xn1": rC[c]["xn1"], "xn1f": rC[c]["xn1f"], "gw": w["gw"],
         "w1": w["w1"], "b1m": w["b1m"], "w2": w["w2"], "b2m": w["b2m"]}
        for c in range(NCORES)])

    rG = _run(build_final(), [
        {"x1": rC[c]["x1"], "zb": rE[c]["zb"], "y1": rF[c]["y1"],
         "wo1T": w["wo1T"]} for c in range(NCORES)])

    out = np.empty((NCORES * BPC, SEQ, D), np.float32)
    for c in range(NCORES):
        out[BPC * c: BPC * (c + 1)] = \
            rG[c]["out"].T.reshape(BPC, SEQ, D)
    return out
